# version 06
# baseline (speedup 1.0000x reference)
import math
import numpy as np
import concourse.bass as bass
import concourse.mybir as mybir
from concourse.bass import AP
from concourse.bass_utils import run_bass_kernel_spmd

F32 = mybir.dt.float32
BF16 = mybir.dt.bfloat16
AF = mybir.ActivationFunctionType
ALU = mybir.AluOpType
AX = mybir.AxisListType

ENGINES = ("pe", "act", "dve", "pool", "sp")
N_DMA_SLOTS = 16
SAME_ENGINE_SAFE = {"pe", "act", "pool", "sp"}


def _names_of(aps):
    out = []
    for a in aps:
        if a is None:
            continue
        if isinstance(a, (list, tuple)):
            out.extend(_names_of(a))
        elif isinstance(a, str):
            out.append(a)
        elif hasattr(a, "tensor"):
            out.append(a.tensor.name)
        else:
            out.append(a.name)
    return out


class _Op:
    __slots__ = ("engine", "fn", "reads", "writes", "idx", "deps", "is_dma",
                 "slot", "slot_use", "marked", "inc_val", "vclock", "tag")

    def __init__(self, engine, fn, reads, writes, idx, is_dma, tag):
        self.engine = engine
        self.fn = fn
        self.reads = reads
        self.writes = writes
        self.idx = idx
        self.deps = []
        self.is_dma = is_dma
        self.slot = -1
        self.slot_use = 0
        self.marked = False
        self.inc_val = 0
        self.vclock = None
        self.tag = tag


class Prog:

    def __init__(self, nc):
        self.nc = nc
        self.ops = []
        self.last_writer = {}
        self.readers = {}
        self.birth_deps = {}
        self.dma_count = 0
        self.sems = {}
        for e in ENGINES:
            self.sems[e] = nc.alloc_semaphore(name=f"sem_{e}")
        self.dma_sems = [nc.alloc_semaphore(name=f"dma{i}") for i in range(N_DMA_SLOTS)]
        self.slot_uses = [0] * N_DMA_SLOTS
        self.slot_last_op = [None] * N_DMA_SLOTS
        self._const_cache = {}
        self._sbuf_region_ops = []
        self._psum_region_ops = []
        self._guards = {}
        self._touch_hist = {}
        self._uid = 0

    def add(self, engine, fn, ins=(), outs=(), dma=False, tag=""):
        reads = _names_of(ins)
        writes = _names_of(outs)
        op = _Op(engine, fn, reads, writes, len(self.ops), dma, tag)
        deps = set()
        for b in reads:
            w = self.last_writer.get(b)
            if w is not None:
                deps.add(w)
            bd = self.birth_deps.pop(b, None)
            if bd:
                deps.update(bd)
        for b in writes:
            w = self.last_writer.get(b)
            if w is not None:
                deps.add(w)
            for r in self.readers.get(b, ()):
                deps.add(r)
            bd = self.birth_deps.pop(b, None)
            if bd:
                deps.update(bd)
        if dma:
            if engine == "pool":
                self._dma_cnt_pool = getattr(self, "_dma_cnt_pool", 0) + 1
                slot = (self._dma_cnt_pool - 1) % (N_DMA_SLOTS // 2)
            else:
                self._dma_cnt_sp = getattr(self, "_dma_cnt_sp", 0) + 1
                slot = N_DMA_SLOTS // 2 + (self._dma_cnt_sp - 1) % (N_DMA_SLOTS // 2)
            op.slot = slot
            prev = self.slot_last_op[slot]
            if prev is not None:
                deps.add(prev)
            self.slot_uses[slot] += 1
            op.slot_use = self.slot_uses[slot]
            self.slot_last_op[slot] = op.idx
            self.dma_count += 1
        op.deps = sorted(deps)
        for b in reads:
            self.readers.setdefault(b, []).append(op.idx)
            self._touch_hist.setdefault(b, []).append(op.idx)
        for b in writes:
            self.last_writer[b] = op.idx
            self.readers[b] = []
            self._touch_hist.setdefault(b, []).append(op.idx)
        self.ops.append(op)
        return op

    def mem_init(self, slack=1024):
        nc = self.nc
        per_part = (nc.sbuf_bytes_remaining - slack) & ~63
        arena = nc.alloc_sbuf_tensor("sb_arena", [128, per_part], mybir.dt.uint8)
        base = None
        for alloc in nc.m.functions[0].allocations:
            if (hasattr(alloc, "memorylocations") and alloc.memorylocations
                    and alloc.memorylocations[0].name == "sb_arena"):
                base = alloc.memorylocations[0].addr
        assert base is not None
        self._sb_base = base
        self._sb_size = per_part
        self._sb_free = [(0, per_part)]
        self._sb_live = {}
        self._sb_dead = []
        self._ps_banks = []
        self._ps_free = []
        for i in range(8):
            b = nc.alloc_psum_tensor(f"pbank{i}", [128, 512], F32)
            self._ps_banks.append(b)
            self._ps_free.append(i)
        self._ps_dead = {}

    def salloc(self, name, shape, dtype):
        per_part = int(np.prod(shape[1:])) * np.dtype(mybir.dt.np(dtype)).itemsize
        per_part = (per_part + 63) & ~63
        best = None
        for i, (a, b) in enumerate(self._sb_free):
            if b - a >= per_part and (best is None or
                                      b - a < self._sb_free[best][1] - self._sb_free[best][0]):
                best = i
        if best is None:
            raise MemoryError(f"SBUF arena full allocating {name} ({per_part}B/part); "
                              f"free={self._sb_free}")
        i = best
        a, b = self._sb_free[i]
        self._sb_free[i] = (a + per_part, b)
        if self._sb_free[i][0] == self._sb_free[i][1]:
            self._sb_free.pop(i)
        t = self.nc.alloc_sbuf_tensor_at(name, list(shape), dtype,
                                         offset=self._sb_base + a)
        name = t.name
        end = a + per_part
        deps = []
        keep = []
        for (a0, b0, opl) in self._sb_dead:
            if a0 < end and b0 > a:
                deps.extend(opl)
                if a0 < a:
                    keep.append((a0, a, opl))
                if b0 > end:
                    keep.append((end, b0, opl))
            else:
                keep.append((a0, b0, opl))
        self._sb_dead = keep
        if deps:
            self.birth_deps[name] = sorted(set(deps))
        for nm2, (a2, b2) in self._sb_live.items():
            assert not (a2 < end and b2 > a), \
                f"OVERLAP {name}[{a},{end}) with live {nm2}[{a2},{b2})"
        self._sb_live[name] = (a, end)
        return t

    def sfree(self, t):
        if isinstance(t, (list, tuple)):
            for x in t:
                self.sfree(x)
            return
        name = t.name if hasattr(t, "name") else t
        a, b = self._sb_live.pop(name)
        opl = sorted(set(self._touch_hist.get(name, ())))
        self._sb_dead.append((a, b, opl))
        self._sb_free.append((a, b))
        self._sb_free.sort()
        merged = []
        for s0, s1 in self._sb_free:
            if merged and merged[-1][1] == s0:
                merged[-1] = (merged[-1][0], s1)
            else:
                merged.append((s0, s1))
        self._sb_free = merged

    def palloc(self, name=None):
        i = self._ps_free.pop(0)
        b = self._ps_banks[i]
        deps = self._ps_dead.pop(i, None)
        if deps:
            pass
        return b

    def pfree(self, t):
        name = t.name
        i = int(name[5:])
        self._ps_free.append(i)

    def uid(self):
        self._uid += 1
        return self._uid

    def finalize(self):
        ops = self.ops
        for op in ops:
            for d in op.deps:
                dop = ops[d]
                if dop.is_dma:
                    continue
                if dop.engine == op.engine and dop.engine in SAME_ENGINE_SAFE:
                    continue
                dop.marked = True
        inc_count = {e: 0 for e in ENGINES}
        for op in ops:
            if op.marked and not op.is_dma:
                inc_count[op.engine] += 1
                op.inc_val = inc_count[op.engine]
        dim = len(ENGINES) + N_DMA_SLOTS
        eidx = {e: i for i, e in enumerate(ENGINES)}
        zero = [0] * dim
        for op in ops:
            vc = list(zero)
            for d in op.deps:
                dvc = ops[d].vclock
                for i in range(dim):
                    if dvc[i] > vc[i]:
                        vc[i] = dvc[i]
            if op.is_dma:
                j = len(ENGINES) + op.slot
                if 16 * op.slot_use > vc[j]:
                    vc[j] = 16 * op.slot_use
            elif op.marked:
                j = eidx[op.engine]
                if op.inc_val > vc[j]:
                    vc[j] = op.inc_val
            op.vclock = vc
        per_engine = {e: [] for e in ENGINES}
        eng_clock = {e: list(zero) for e in ENGINES}
        n_waits = 0
        for op in ops:
            waits = []
            clk = eng_clock[op.engine]
            for d in op.deps:
                dop = ops[d]
                if dop.is_dma:
                    j = len(ENGINES) + dop.slot
                    val = 16 * dop.slot_use
                    if clk[j] < val:
                        waits.append((self.dma_sems[dop.slot], val, j))
                elif dop.engine == op.engine and dop.engine in SAME_ENGINE_SAFE:
                    continue
                else:
                    j = eidx[dop.engine]
                    val = dop.inc_val
                    if clk[j] < val:
                        waits.append((self.sems[dop.engine], val, j))
                dvc = dop.vclock
                for i in range(dim):
                    if dvc[i] > clk[i]:
                        clk[i] = dvc[i]
            by_sem = {}
            for s, v, j in waits:
                k = id(s)
                if k not in by_sem or by_sem[k][1] < v:
                    by_sem[k] = (s, v, j)
            waits = list(by_sem.values())
            n_waits += len(waits)
            per_engine[op.engine].append((op, waits))
        self._per_engine = per_engine
        self.n_waits = n_waits

    def emit(self):
        nc = self.nc
        per_engine = self._per_engine
        sems = self.sems
        dma_sems = self.dma_sems

        def run_engine(eng_obj, ename):
            for op, waits in per_engine[ename]:
                for s, v, _ in waits:
                    eng_obj.wait_ge(s, v)
                ins = op.fn(eng_obj)
                if op.is_dma:
                    ins.then_inc(dma_sems[op.slot], 16)
                elif op.marked:
                    ins.then_inc(sems[op.engine], 1)

        with nc.Block() as block:
            @block.tensor
            def _(t):
                run_engine(t, "pe")

            @block.scalar
            def _(s):
                run_engine(s, "act")

            @block.vector
            def _(v):
                run_engine(v, "dve")

            @block.gpsimd
            def _(g):
                run_engine(g, "pool")

            @block.sync
            def _(s):
                run_engine(s, "sp")
                for k in range(N_DMA_SLOTS):
                    if self.slot_uses[k]:
                        s.wait_ge(dma_sems[k], 16 * self.slot_uses[k])

    def dma(self, q, out_ap, in_ap, ins=(), outs=(), tag="", out_key=None,
            in_key=None, **kw):
        ik = in_key if in_key else in_ap
        ins = list(ins) + (ik if isinstance(ik, list) else [ik])
        ok = out_key if out_key else out_ap
        outs = list(outs) + (ok if isinstance(ok, list) else [ok])

        def fn(e, out_ap=out_ap, in_ap=in_ap, kw=kw):
            return e.dma_start(out=out_ap, in_=in_ap, **kw)
        return self.add(q, fn, ins=ins, outs=outs, dma=True, tag=tag)


D = 1024
SEQ = 2048
BPC = 2
T = BPC * SEQ
NT = T // 128
NF = D // 128
NCORES = 8
LN_EPS = 1e-5


class Ctx:

    def __init__(self, nc, P, win):
        self.nc = nc
        self.P = P
        self.win = win
        self.dram = {}

    def scratch_dram(self, name, shape, dtype=F32):
        if name not in self.dram:
            self.dram[name] = self.nc.dram_tensor(name, shape, dtype,
                                                  kind="Internal").ap()
        return self.dram[name]


def emit_dev_consts(C):
    P = C.P
    C.ident_bf = P.salloc("c_ident_bf", [128, 128], BF16)
    C.ident_f32 = P.salloc("c_ident_f32", [128, 128], F32)
    C.ones_row = P.salloc("c_ones_row", [1, 128], F32)
    C.m0 = P.salloc("c_m0", [1, 128], F32)
    C.m1 = P.salloc("c_m1", [1, 128], F32)
    for t, nm in ((C.ident_bf, "ib"), (C.ident_f32, "if")):
        P.add("pool", lambda e, t=t: e.memset(t[:], 1.0), outs=[t])
        P.add("pool", lambda e, t=t: e.affine_select(
            out=t[:], in_=t[:], pattern=[[1, 128]], compare_op=ALU.is_equal,
            fill=0.0, base=0, channel_multiplier=-1), ins=[t], outs=[t])
    P.add("pool", lambda e: e.memset(C.ones_row[:], 1.0), outs=[C.ones_row])
    P.add("pool", lambda e: e.memset(C.m0[:], 1.0), outs=[C.m0])
    P.add("pool", lambda e: e.affine_select(
        out=C.m0[:], in_=C.m0[:], pattern=[[-1, 128]], compare_op=ALU.is_ge,
        fill=0.0, base=63, channel_multiplier=0), ins=[C.m0], outs=[C.m0])
    P.add("pool", lambda e: e.memset(C.m1[:], 1.0), outs=[C.m1])
    P.add("pool", lambda e: e.affine_select(
        out=C.m1[:], in_=C.m1[:], pattern=[[1, 128]], compare_op=ALU.is_ge,
        fill=0.0, base=-64, channel_multiplier=0), ins=[C.m1], outs=[C.m1])


def constf(C, v):
    P = C.P
    key = float(v)
    if key not in P._const_cache:
        t = P.salloc(f"c_f{P.uid()}", [128, 1], F32)
        P.add("pool", lambda g, t=t, v=key: g.memset(t[:], v), outs=[t])
        P._const_cache[key] = t
    return P._const_cache[key]


def load_weight(C, name, n_in, n_out, dram_ap=None, dt=BF16):
    P = C.P
    src = dram_ap if dram_ap is not None else C.win[name]
    nchunk = n_in // 128
    u_ = P.uid()
    Ws = [P.salloc(f"w_{name}_{u_}_{c}", [128, n_out], dt)
          for c in range(nchunk)]
    for c in range(nchunk):
        P.dma("pool", Ws[c][:], src[c * 128:(c + 1) * 128, :])
    return Ws


def wslice(W, n_out, fc, o0, ow):
    if isinstance(W, list):
        return W[fc][:, o0:o0 + ow]
    return W[:, fc * n_out + o0: fc * n_out + o0 + ow]


def load_bias(C, name, n_out, dram_ap=None):
    P = C.P
    src = dram_ap if dram_ap is not None else C.win[name + "_pc"]
    B = P.salloc(f"b_{name}_{P.uid()}", [128, n_out // 128], F32)
    P.dma("sp", B[:], src)
    return B


def fmv(fm, c, t0, tw, tstride=T):
    return fm[:, c * tstride + t0: c * tstride + t0 + tw]


def emit_outer_ln(C, xin, *, want_fm=True, want_tm=False, tm_dtype=BF16,
                  u32_dram=None, want_lo_fm=False, tag=""):
    P = C.P
    u = P.uid()
    xt = [P.salloc(f"lnx{u}_{i}", [128, D], F32) for i in range(4)]
    sums = P.salloc(f"lnsum{u}", [128, NT], F32)
    sqs = P.salloc(f"lnsq{u}", [128, NT], F32)
    dump = P.salloc(f"lndump{u}", [128, D], BF16)
    res = {}
    if want_fm:
        res["u_fm"] = P.salloc(f"ufm{u}{tag}", [128, NF * T], BF16)
    if want_tm:
        res["u_tm"] = P.salloc(f"utm{u}{tag}", [128, NT * D], tm_dtype)
    if want_lo_fm:
        res["ulo_fm"] = P.salloc(f"ulo{u}{tag}", [128, NF * T], BF16)
    P.add("act", lambda e: e.memzero(sqs[:]), outs=[sqs])
    for tt in range(NT):
        x_sb = xt[tt % 4]
        P.dma("sp", x_sb[:], xin[tt * 128:(tt + 1) * 128, :])
        P.add("dve", lambda e, a=x_sb, c=tt: e.reduce_sum(
            sums[:, c:c + 1], a[:], axis=AX.X), ins=[x_sb], outs=[sums])
        P.add("act", lambda e, a=x_sb, c=tt: e.activation(
            out=dump[:], in_=a[:], func=AF.Square,
            accum_out=sqs[:, c:c + 1]), ins=[x_sb], outs=[dump, sqs])
    mean = P.salloc(f"lnmean{u}", [128, NT], F32)
    var = P.salloc(f"lnvar{u}", [128, NT], F32)
    rstd = P.salloc(f"lnrstd{u}", [128, NT], F32)
    nmr = P.salloc(f"lnnmr{u}", [128, NT], F32)
    P.add("dve", lambda e: e.tensor_scalar_mul(mean[:], sums[:], 1.0 / D),
          ins=[sums], outs=[mean])
    P.add("dve", lambda e: e.tensor_mul(var[:], mean[:], mean[:]),
          ins=[mean], outs=[var])
    P.add("dve", lambda e: e.scalar_tensor_tensor(
        var[:], sqs[:], 1.0 / D, var[:], ALU.mult, ALU.subtract),
        ins=[sqs, var], outs=[var])
    epsc = constf(C, LN_EPS)
    nhalf = constf(C, 0.0)
    P.add("act", lambda e: e.activation(out=rstd[:], in_=var[:], func=AF.Ln,
                                        bias=epsc[:]),
          ins=[var, epsc], outs=[rstd])
    P.add("act", lambda e: e.activation(out=rstd[:], in_=rstd[:], func=AF.Exp,
                                        scale=constf(C, -0.5)[:]),
          ins=[rstd, constf(C, -0.5)], outs=[rstd])
    P.add("dve", lambda e: e.scalar_tensor_tensor(
        nmr[:], mean[:], -1.0, rstd[:], ALU.mult, ALU.mult),
        ins=[mean, rstd], outs=[nmr])
    if want_lo_fm and u32_dram is None:
        u32_dram = False
    u32t = [P.salloc(f"lnu32{u}_{i}", [128, D], F32) for i in range(2)] \
        if u32_dram is not None else None
    lot = [P.salloc(f"lnlo{u}_{i}", [128, D], BF16) for i in range(2)] \
        if want_lo_fm else None
    ubf = None
    if want_fm and not want_tm:
        ubf = [P.salloc(f"lnubf{u}_{i}", [128, D], BF16) for i in range(2)]
    pb = [P.palloc() for _ in range(2)] if want_fm else []
    pb_bf = [b.ap().bitcast(BF16) for b in pb]
    for tt in range(NT):
        x_sb = xt[tt % 4]
        P.dma("sp", x_sb[:], xin[tt * 128:(tt + 1) * 128, :])
        if want_tm:
            dst = res["u_tm"][:, tt * D:(tt + 1) * D]
            dstt = res["u_tm"]
        elif want_fm:
            dst = ubf[tt % 2][:]
            dstt = ubf[tt % 2]
        if u32_dram is not None:
            u32 = u32t[tt % 2]
            P.add("dve", lambda e, a=x_sb, c=tt, o=u32: e.tensor_scalar(
                out=o[:], in0=a[:], scalar1=rstd[:, c:c + 1],
                scalar2=nmr[:, c:c + 1], op0=ALU.mult, op1=ALU.add),
                ins=[x_sb, rstd, nmr], outs=[u32])
            if u32_dram is not False:
                P.dma("sp", u32_dram[tt * 128:(tt + 1) * 128, :], u32[:])
            if want_fm or want_tm:
                P.add("dve", lambda e, i=u32, o=dst, ot=dstt: e.tensor_copy(
                    out=o, in_=i[:]), ins=[u32], outs=[dstt])
            if want_lo_fm:
                lo = lot[tt % 2]
                P.add("dve", lambda e, i=u32, o=lo, b=dst: e.tensor_sub(
                    o[:], i[:], b), ins=[u32, dstt], outs=[lo])
        else:
            P.add("dve", lambda e, a=x_sb, c=tt, o=dst, ot=dstt: e.tensor_scalar(
                out=o, in0=a[:], scalar1=rstd[:, c:c + 1],
                scalar2=nmr[:, c:c + 1], op0=ALU.mult, op1=ALU.add),
                ins=[x_sb, rstd, nmr], outs=[dstt])
        if want_fm:
            src_t = res["u_tm"] if want_tm else ubf[tt % 2]
            src_ap = (res["u_tm"][:, tt * D:(tt + 1) * D] if want_tm
                      else ubf[tt % 2][:])
            bank = pb[tt % 2]
            bb = pb_bf[tt % 2]
            for fc in range(NF):
                P.add("pe", lambda e, s=src_ap, f=fc, b=bb: e.transpose(
                    b[:, f * 128:(f + 1) * 128], s[:, f * 128:(f + 1) * 128],
                    C.ident_bf[:]),
                    ins=[src_t, C.ident_bf], outs=[bank])
            ofm = res["u_fm"].ap().rearrange("p (c t) -> p c t", c=NF)[
                :, :, tt * 128:(tt + 1) * 128]
            ibk = bb.rearrange("p (c t) -> p c t", c=NF)
            P.add("dve", lambda e, o=ofm, i=ibk: e.tensor_copy(out=o, in_=i),
                  ins=[bank], outs=[res["u_fm"]])
            if want_lo_fm:
                lo = lot[tt % 2]
                bank2 = pb[(tt + 1) % 2]
                bb2 = pb_bf[(tt + 1) % 2]
                for fc in range(NF):
                    P.add("pe", lambda e, s=lo, f=fc, b=bb2: e.transpose(
                        b[:, f * 128:(f + 1) * 128], s[:, f * 128:(f + 1) * 128],
                        C.ident_bf[:]),
                        ins=[lo, C.ident_bf], outs=[bank2])
                olo = res["ulo_fm"].ap().rearrange("p (c t) -> p c t", c=NF)[
                    :, :, tt * 128:(tt + 1) * 128]
                ib2 = bb2.rearrange("p (c t) -> p c t", c=NF)
                P.add("dve", lambda e, o=olo, i=ib2: e.tensor_copy(out=o, in_=i),
                      ins=[bank2], outs=[res["ulo_fm"]])
    for t_ in xt:
        P.sfree(t_)
    for t_ in (sums, sqs, dump, mean, var, rstd, nmr):
        P.sfree(t_)
    if u32t:
        for t_ in u32t:
            P.sfree(t_)
    if want_lo_fm:
        for t_ in lot:
            P.sfree(t_)
    if ubf:
        for t_ in ubf:
            P.sfree(t_)
    for b in pb:
        P.pfree(b)
    return res


def emit_delta_residual_store(C, xin, xout, delta_sb, tt_range=None,
                              skip_chunks=(), tag="", tstride=T, t_base=0):
    P = C.P
    u = P.uid()
    bf = (delta_sb.dtype == BF16) if hasattr(delta_sb, "dtype") else False
    nbuf = 2
    xt = [P.salloc(f"rsx{u}_{i}", [128, D], F32) for i in range(nbuf)]
    ot = [P.salloc(f"rso{u}_{i}", [128, D], F32) for i in range(nbuf)]
    delta = delta_sb
    ident = C.ident_bf if bf else C.ident_f32
    if bf:
        pbs = [P.palloc() for _ in range(2)]
        views = [b.ap().bitcast(BF16) for b in pbs]
    else:
        pbs = [P.palloc() for _ in range(4)]
        views = [b[:] for b in pbs]
    for tt in (range(NT) if tt_range is None else tt_range):
        x_sb = xt[tt % nbuf]
        o_sb = ot[tt % nbuf]
        P.dma("sp", x_sb[:], xin[tt * 128:(tt + 1) * 128, :])
        if bf:
            banksof = lambda fc: (pbs[tt % 2], views[tt % 2], fc * 128)
        else:
            banksof = lambda fc: (pbs[(tt % 2) * 2 + (0 if fc < 4 else 1)],
                                  views[(tt % 2) * 2 + (0 if fc < 4 else 1)],
                                  (fc % 4) * 128)
        for fc in range(NF):
            if fc in skip_chunks:
                continue
            bank, view, off = banksof(fc)
            P.add("pe", lambda e, f=fc, v=view, o=off, t=tt: e.transpose(
                v[:, o:o + 128], fmv(delta, f, t * 128 - t_base, 128, tstride),
                ident[:]), ins=[delta, ident], outs=[bank])
        for half in (0, 1):
            if bf:
                bank, view = pbs[tt % 2], views[tt % 2]
                voff = half * 512
            else:
                bank, view = pbs[(tt % 2) * 2 + half], views[(tt % 2) * 2 + half]
                voff = 0
            sk = [fc for fc in range(half * 4, half * 4 + 4) if fc in skip_chunks]
            if not sk:
                c = slice(half * 512, (half + 1) * 512)
                P.add("dve", lambda e, o=o_sb, x=x_sb, v=view, c=c, vo=voff:
                      e.tensor_add(o[:, c], x[:, c], v[:, vo:vo + 512]),
                      ins=[x_sb, bank], outs=[o_sb])
            else:
                for fc in range(half * 4, half * 4 + 4):
                    c1 = slice(fc * 128, (fc + 1) * 128)
                    o1 = slice(voff + (fc % 4) * 128, voff + ((fc % 4) + 1) * 128)
                    if fc in skip_chunks:
                        P.add("dve", lambda e, o=o_sb, x=x_sb, c=c1: e.tensor_copy(
                            out=o[:, c], in_=x[:, c]), ins=[x_sb], outs=[o_sb])
                    else:
                        P.add("dve", lambda e, o=o_sb, x=x_sb, v=view, c=c1, oo=o1:
                              e.tensor_add(o[:, c], x[:, c], v[:, oo[0] if False else oo]),
                              ins=[x_sb, bank], outs=[o_sb])
        P.dma("sp", xout[tt * 128:(tt + 1) * 128, :], o_sb[:], tag=tag)
    for t_ in xt + ot:
        P.sfree(t_)
    for b in pbs:
        P.pfree(b)


def build_sel16(C):
    P = C.P
    sels = []
    for e in range(8):
        s = P.salloc(f"c_sel16_{e}", [1, 128], F32)
        P.add("pool", lambda g, s=s: g.memset(s[:], 1.0 / 16.0), outs=[s])
        P.add("pool", lambda g, s=s, e=e: g.affine_select(
            out=s[:], in_=s[:], pattern=[[1, 128]], compare_op=ALU.is_ge,
            fill=0.0, base=-16 * e, channel_multiplier=0), ins=[s], outs=[s])
        P.add("pool", lambda g, s=s, e=e: g.affine_select(
            out=s[:], in_=s[:], pattern=[[-1, 128]], compare_op=ALU.is_ge,
            fill=0.0, base=16 * e + 15, channel_multiplier=0), ins=[s], outs=[s])
        sels.append(s)
    return sels


def emit_l3_ffn(C, xin, xout, tag=""):
    P = C.P
    Wb = load_weight(C, "L3_ffn_weight", D, D)
    Ug = load_weight(C, "L3_ffn_lora_U", D, 128)
    Vs = load_weight(C, "L3_ffn_lora_V", 128, D)
    Gw = load_weight(C, "L3_ffn_gate_w", D, 8)
    u = emit_outer_ln(C, xin, want_fm=True, tag=tag)
    ufm = u["u_fm"]
    uu = P.uid()
    tfm = P.salloc(f"tfm_{uu}", [128, T], BF16)
    pbt = [P.palloc() for _ in range(2)]
    for tc in range(T // 512):
        bank = pbt[tc % 2]
        for fc in range(NF):
            P.add("pe", lambda e, f=fc, t=tc, b=bank, s=(fc == 0), p=(fc == NF - 1):
                  e.matmul(b[:], wslice(Ug, 128, f, 0, 128),
                           fmv(ufm, f, t * 512, 512), start=s, stop=p),
                  ins=[Ug, ufm], outs=[bank])
        P.add("dve", lambda e, t=tc, b=bank: e.tensor_copy(
            out=tfm[:, t * 512:(t + 1) * 512], in_=b[:]), ins=[bank], outs=[tfm])
    for b in pbt:
        P.pfree(b)
    gtm = P.salloc(f"gtm_{uu}", [128, 8], F32)
    gnm = P.salloc(f"gnm_{uu}", [128, 1], F32)
    gs = P.salloc(f"gs_{uu}", [128, 1], F32)
    tg = P.salloc(f"tg_{uu}", [128, T], BF16)
    gexp = P.salloc(f"gexp_{uu}", [128, 128], F32)
    pb = P.palloc()
    pbg = [P.palloc() for _ in range(2)]
    gtm_exp = AP(gtm, 0, [[8, 128], [1, 8], [0, 16]])
    for tc in range(T // 512):
        bank = pbg[tc % 2]
        for i in range(4):
            tt = tc * 4 + i
            for fc in range(NF):
                P.add("pe", lambda e, f=fc, t=tt, s=(fc == 0), p=(fc == NF - 1):
                      e.matmul(pb[:, 0:8], fmv(ufm, f, t * 128, 128),
                               wslice(Gw, 8, f, 0, 8), start=s, stop=p),
                      ins=[ufm, Gw], outs=[pb])
            P.add("dve", lambda e: e.reduce_max(gnm[:], pb[:, 0:8], axis=AX.X,
                                                negate=True), ins=[pb], outs=[gnm])
            P.add("act", lambda e: e.activation(out=gtm[:], in_=pb[:, 0:8],
                                                func=AF.Exp, bias=gnm[:],
                                                accum_out=gs[:]),
                  ins=[pb, gnm], outs=[gtm, gs])
            P.add("dve", lambda e: e.reciprocal(gs[:], gs[:]), ins=[gs], outs=[gs])
            P.add("dve", lambda e: e.tensor_scalar_mul(gtm[:], gtm[:], gs[:]),
                  ins=[gtm, gs], outs=[gtm])
            P.add("dve", lambda e: e.tensor_copy(out=gexp[:], in_=gtm_exp),
                  ins=[gtm], outs=[gexp])
            P.add("pe", lambda e, b=bank, i=i: e.transpose(
                b[:, i * 128:(i + 1) * 128], gexp[:], C.ident_f32[:]),
                ins=[gexp, C.ident_f32], outs=[bank])
        P.add("dve", lambda e, t=tc, b=bank: e.scalar_tensor_tensor(
            tg[:, t * 512:(t + 1) * 512], tfm[:, t * 512:(t + 1) * 512],
            1.0 / 16.0, b[:], ALU.mult, ALU.mult),
            ins=[tfm, bank], outs=[tg])
    P.pfree(pb)
    for b in pbg:
        P.pfree(b)
    dsb = [P.salloc(f"l3d_{uu}_{i}", [128, NF * 512], F32) for i in range(2)]
    pbo = [P.palloc() for _ in range(4)]
    for tc in range(T // 512):
        ds = dsb[tc % 2]
        for oc in range(NF):
            bank = pbo[(tc * NF + oc) % 4]
            for fc in range(NF):
                P.add("pe", lambda e, f=fc, o=oc, t=tc, b=bank, s=(fc == 0):
                      e.matmul(b[:], wslice(Wb, D, f, o * 128, 128),
                               fmv(ufm, f, t * 512, 512), start=s, stop=False),
                      ins=[Wb, ufm], outs=[bank])
            P.add("pe", lambda e, o=oc, t=tc, b=bank:
                  e.matmul(b[:], wslice(Vs, D, 0, o * 128, 128),
                           tg[:, t * 512:(t + 1) * 512], start=False, stop=True),
                  ins=[Vs, tg], outs=[bank])
            P.add("dve", lambda e, o=oc, b=bank, d=ds: e.tensor_copy(
                out=d[:, o * 512:(o + 1) * 512], in_=b[:]), ins=[bank], outs=[ds])
        emit_delta_residual_store(C, xin, xout, ds,
                                  tt_range=range(tc * 4, tc * 4 + 4),
                                  tag=tag, tstride=512, t_base=tc * 512)
    for b in pbo:
        P.pfree(b)
    for t_ in (gtm, gnm, gs, gexp, tfm, tg):
        P.sfree(t_)
    for t_ in dsb:
        P.sfree(t_)
    P.sfree(ufm)
    for w in (Wb, Ug, Vs, Gw):
        P.sfree(w)


def prep_weights(inp):
    w = {}
    f32 = lambda a: np.ascontiguousarray(a, dtype=np.float32)
    for nm in ("Wq", "Wk", "Wv", "Wo", "W1", "W2"):
        w[f"L0_attn_{nm}"] = f32(inp[f"L0_attn_{nm}"])
    w["L0_attn_wkqv"] = f32(np.concatenate(
        [np.asarray(inp["L0_attn_Wk"]), np.asarray(inp["L0_attn_Wq"]),
         np.asarray(inp["L0_attn_Wv"])], axis=1))
    for nm in ("bq", "bk", "bv", "bo", "g1", "b1", "c1", "c2", "g2", "b2"):
        w[f"L0_attn_{nm}"] = f32(inp[f"L0_attn_{nm}"])
    w["L0_ffn_W_fc"] = f32(inp["L0_ffn_W_fc"].T)
    w["L0_ffn_W_proj"] = f32(inp["L0_ffn_W_proj"].T)
    w["L0_ffn_ln_w"] = f32(inp["L0_ffn_ln_w"])
    w["L0_ffn_ln_b"] = f32(inp["L0_ffn_ln_b"])
    wk = np.asarray(inp["L1_attn_w_kqv"]).T.reshape(1024, 16, 64, 3)
    w["L1_attn_wkqv"] = f32(np.transpose(wk, (0, 3, 1, 2)).reshape(1024, 3072))
    w["L1_attn_w_proj"] = f32(np.asarray(inp["L1_attn_w_proj"]).T)
    w["L1_attn_b_proj"] = f32(inp["L1_attn_b_proj"])
    w["L1_ffn_router_w"] = f32(inp["L1_ffn_router_w"])
    w["L1_ffn_router_b"] = f32(inp["L1_ffn_router_b"])
    w["L1_ffn_expert_w"] = f32(np.asarray(inp["L1_ffn_expert_w"]).reshape(8192, 1024))
    w["L1_ffn_expert_b"] = f32(inp["L1_ffn_expert_b"])
    w["L2_attn_w_lhs"] = f32(np.asarray(inp["L2_attn_w_lhs"]).reshape(1024))
    w["L2_attn_w_rhs"] = f32(np.asarray(inp["L2_attn_w_rhs"]).reshape(1024))
    for nm in ("wq", "wk", "wv", "wo", "e0_up", "e0_down", "e1_w1", "e1_w2",
               "e1_down", "e2_l1", "e2_l2", "e2_l3", "e2_l4", "e3_down",
               "e3_up", "e3_out", "gate_w"):
        w[f"L2_ffn_{nm}"] = f32(np.asarray(inp[f"L2_ffn_{nm}"]).T)
    w["L2_ffn_wkqv"] = f32(np.concatenate(
        [w["L2_ffn_wk"], w["L2_ffn_wq"], w["L2_ffn_wv"]], axis=1))
    for nm in ("n1_w", "n2_w", "qn_w", "kn_w"):
        w[f"L2_ffn_{nm}"] = f32(inp[f"L2_ffn_{nm}"])
    for nm in ("Wq", "Wk", "Wv"):
        w[f"L3_attn_{nm}"] = f32(np.asarray(inp[f"L3_attn_{nm}"]).T)
    for nm in ("bq", "bk", "bv"):
        w[f"L3_attn_{nm}"] = f32(inp[f"L3_attn_{nm}"])
    de = np.asarray(inp["L3_attn_dist_emb"])
    pad = np.zeros((129, 64), np.float32)
    w["L3_attn_peT"] = f32(np.concatenate([de, pad], 0).T)
    w["L3_attn_peT_rev"] = f32(np.concatenate([de[::-1], pad], 0).T)
    w["L3_attn_wkqv"] = f32(np.concatenate(
        [np.asarray(inp["L3_attn_Wk"]).T, np.asarray(inp["L3_attn_Wq"]).T,
         np.asarray(inp["L3_attn_Wv"]).T], axis=1))
    w["L3_ffn_weight"] = f32(np.asarray(inp["L3_ffn_weight"]).T)
    w["L3_ffn_gate_w"] = f32(np.asarray(inp["L3_ffn_gate_w"]).T)
    lu = np.asarray(inp["L3_ffn_lora_U"])
    w["L3_ffn_lora_U"] = f32(np.transpose(lu, (2, 0, 1)).reshape(1024, 128))
    lv = np.asarray(inp["L3_ffn_lora_V"])
    w["L3_ffn_lora_V"] = f32(np.transpose(lv, (0, 2, 1)).reshape(128, 1024))
    for k in list(w.keys()):
        v = w[k]
        if v.ndim == 1 and v.size % 128 == 0:
            w[k + "_pc"] = f32(v.reshape(-1, 128).T)
    return w


class LazyWeights:

    def __init__(self, nc, w):
        self.nc = nc
        self.w = w
        self.aps = {}

    def __getitem__(self, k):
        if k not in self.aps:
            v = self.w[k]
            self.aps[k] = self.nc.dram_tensor(
                k, list(v.shape), F32, kind="ExternalInput").ap()
        return self.aps[k]

    def used(self):
        return list(self.aps)


def emit_ones_col_bf(C):
    if not hasattr(C, "ones_col_bf"):
        P = C.P
        C.ones_col_bf = P.salloc("c_ones_col_bf", [128, 1], BF16)
        P.add("pool", lambda g: g.memset(C.ones_col_bf[:], 1.0),
              outs=[C.ones_col_bf])
    return C.ones_col_bf


def emit_l2_attn(C, xin, xout, tag="", cut=0):
    P = C.P
    wl = load_bias(C, "L2_attn_w_lhs", D)
    wr = load_bias(C, "L2_attn_w_rhs", D)
    onesb = emit_ones_col_bf(C)
    uu = P.uid()
    wlr = P.salloc(f"l2wlr_{uu}", [128, NF], F32)
    P.add("dve", lambda e: e.tensor_mul(wlr[:], wl[:], wr[:]),
          ins=[wl, wr], outs=[wlr])
    u = emit_outer_ln(C, xin, want_fm=True, want_tm=True, tag=tag)
    ufm, utm = u["u_fm"], u["u_tm"]
    utm_d = C.scratch_dram(f"l2utm_{uu}", [128, NT * D], BF16)
    P.dma("sp", utm_d[:, :], utm[:])
    P.sfree(utm)
    utmb = P.salloc(f"l2utmb_{uu}", [128, (NT // BPC) * D], BF16)
    sc = P.salloc(f"l2sc_{uu}", [128, NF * SEQ], BF16)
    sqs = P.salloc(f"l2sq_{uu}", [128, 512], BF16)
    nrm = P.salloc(f"l2n_{uu}", [1, SEQ], F32)
    mx = P.salloc(f"l2mx_{uu}", [1, 2], F32)
    c0n = P.salloc(f"l2c0_{uu}", [1, 1], F32)
    c0big = P.salloc(f"l2c0b_{uu}", [128, 1], F32)
    NKT = SEQ // 128
    ptil = P.salloc(f"l2p_{uu}", [128, NKT * 512], BF16)
    rb = P.salloc(f"l2rb_{uu}", [128, 512], F32)
    rsum = P.salloc(f"l2rs_{uu}", [1, 512], F32)
    ds = P.salloc(f"l2d_{uu}", [128, NF * 512], BF16)
    if cut == 1:
        P.add("dve", lambda e: e.memset(ds[:], 0.0), outs=[ds])
        for tt in range(NT):
            emit_delta_residual_store(C, xin, xout, ds,
                                      tt_range=[tt], tag=tag, tstride=512,
                                      t_base=(tt // 4) * 512)
        for t_ in (wlr, sc, sqs, nrm, mx, c0n, c0big, ptil, rb, rsum, ds, ufm, utmb):
            P.sfree(t_)
        P.sfree(wl); P.sfree(wr)
        return
    for b_ in range(BPC):
        P.dma("sp", utmb[:],
              utm_d[:, b_ * (NT // BPC) * D:(b_ + 1) * (NT // BPC) * D])
        for fc in range(NF):
            P.add("dve", lambda e, f=fc, b=b_: e.tensor_scalar_mul(
                fmv(sc, f, 0, SEQ, SEQ), fmv(ufm, f, b * SEQ, SEQ),
                wlr[:, f:f + 1]), ins=[ufm, wlr], outs=[sc])
        bank_n = P.palloc()
        for side, (src_t, colmax) in enumerate((("u", 0), ("s", 1))):
            for tc in range(SEQ // 512):
                for fc in range(NF):
                    if side == 0:
                        sap = fmv(ufm, fc, b_ * SEQ + tc * 512, 512)
                    else:
                        sap = fmv(sc, fc, tc * 512, 512, SEQ)
                    P.add("act", lambda e, s=sap: e.activation(
                        out=sqs[:], in_=s, func=AF.Square),
                        ins=[ufm if side == 0 else sc], outs=[sqs])
                    P.add("pe", lambda e, f=fc, bn=bank_n, s=(fc == 0), p=(fc == NF - 1):
                          e.matmul(bn[0:1, :], onesb[:], sqs[:],
                                   start=s, stop=p), ins=[onesb, sqs], outs=[bank_n])
                P.add("dve", lambda e, t=tc, bn=bank_n: e.tensor_copy(
                    out=nrm[0:1, t * 512:(t + 1) * 512], in_=bn[0:1, :]),
                    ins=[bank_n], outs=[nrm])
            P.add("dve", lambda e, c=side: e.reduce_max(
                mx[0:1, c:c + 1], nrm[0:1, :], axis=AX.X), ins=[nrm], outs=[mx])
        P.add("dve", lambda e: e.tensor_mul(c0n[:], mx[0:1, 0:1], mx[0:1, 1:2]),
              ins=[mx], outs=[c0n])
        P.add("act", lambda e: e.activation(out=c0n[:], in_=c0n[:], func=AF.Ln),
              ins=[c0n], outs=[c0n])
        P.add("act", lambda e: e.activation(out=c0n[:], in_=c0n[:], func=AF.Exp,
                                            scale=constf(C, 0.5)[0:1, :]),
              ins=[c0n, constf(C, 0.5)], outs=[c0n])
        P.add("dve", lambda e: e.tensor_scalar_mul(c0n[:], c0n[:], -1.0),
              ins=[c0n], outs=[c0n])
        P.add("pe", lambda e, bn=bank_n: e.matmul(bn[:, 0:1], C.ones_row[:], c0n[:],
                                       start=True, stop=True),
              ins=[C.ones_row, c0n], outs=[bank_n])
        P.add("dve", lambda e, bn=bank_n: e.tensor_copy(out=c0big[:], in_=bn[:, 0:1]),
              ins=[bank_n], outs=[c0big])
        P.pfree(bank_n)
        if cut == 2:
            continue
        pbS = [P.palloc() for _ in range(2)]
        pbA = [P.palloc() for _ in range(2)]
        pbsum = P.palloc()
        pbrb = P.palloc()
        for qch in range(SEQ // 512):
            q0 = b_ * SEQ + qch * 512
            for kt in range(NKT):
                bank = pbS[kt % 2]
                for fc in range(NF):
                    P.add("pe", lambda e, f=fc, k=kt, q0=q0, b=bank,
                          s=(fc == 0), p=(fc == NF - 1):
                          e.matmul(b[:], fmv(sc, f, k * 128, 128, SEQ),
                                   fmv(ufm, f, q0, 512), start=s, stop=p),
                          ins=[sc, ufm], outs=[bank])
                P.add("act", lambda e, b=bank, k=kt: e.activation(
                    out=ptil[:, k * 512:(k + 1) * 512], in_=b[:], func=AF.Exp,
                    bias=c0big[:]),
                    ins=[bank, c0big], outs=[ptil])
            if cut == 3:
                continue
            for kt in range(NKT):
                P.add("pe", lambda e, k=kt, pb_=pbsum, s=(kt == 0), p=(kt == NKT - 1):
                      e.matmul(pb_[0:1, :], onesb[:],
                               ptil[:, k * 512:(k + 1) * 512], start=s, stop=p),
                      ins=[onesb, ptil], outs=[pbsum])
            P.add("dve", lambda e, pb_=pbsum: e.tensor_copy(out=rsum[:], in_=pb_[0:1, :]),
                  ins=[pbsum], outs=[rsum])
            P.add("dve", lambda e: e.reciprocal(rsum[:], rsum[:]),
                  ins=[rsum], outs=[rsum])
            P.add("pe", lambda e, pb_=pbrb: e.matmul(pb_[:], C.ones_row[:], rsum[:],
                                           start=True, stop=True),
                  ins=[C.ones_row, rsum], outs=[pbrb])
            P.add("dve", lambda e, pb_=pbrb: e.tensor_copy(out=rb[:], in_=pb_[:]),
                  ins=[pbrb], outs=[rb])
            if cut == 4:
                continue
            if cut == 6:
                for dc in range(NF):
                    bank = pbA[dc % 2]
                    for kt in range(NKT):
                        P.add("pe", lambda e, k=kt, d=dc, b=bank,
                              s=(kt == 0), p=(kt == NKT - 1):
                              e.matmul(b[:], utmb[:, k * D + d * 128: k * D + (d + 1) * 128],
                                       ptil[:, k * 512:(k + 1) * 512], start=s, stop=p),
                              ins=[utmb, ptil], outs=[bank])
                    P.add("dve", lambda e, b=bank: e.tensor_copy(
                        out=rb[:], in_=b[:]), ins=[bank], outs=[rb])
                continue
            for dc in range(NF):
                bank = pbA[dc % 2]
                for kt in range(NKT):
                    P.add("pe", lambda e, k=kt, d=dc, b=bank,
                          s=(kt == 0), p=(kt == NKT - 1):
                          e.matmul(b[:], utmb[:, k * D + d * 128: k * D + (d + 1) * 128],
                                   ptil[:, k * 512:(k + 1) * 512], start=s, stop=p),
                          ins=[utmb, ptil], outs=[bank])
                if cut == 5:
                    P.add("dve", lambda e, d=dc, b=bank: e.tensor_copy(
                        out=ds[:, d * 512:(d + 1) * 512], in_=b[:]),
                        ins=[bank], outs=[ds])
                else:
                    P.add("dve", lambda e, d=dc, b=bank: e.tensor_mul(
                        ds[:, d * 512:(d + 1) * 512], b[:], rb[:]),
                        ins=[bank, rb], outs=[ds])
            emit_delta_residual_store(
                C, xin, xout, ds,
                tt_range=range(q0 // 128, q0 // 128 + 4),
                tag=tag, tstride=512, t_base=q0)
        for b in pbS + pbA + [pbsum, pbrb]:
            P.pfree(b)
    if cut in (2, 3, 4, 6):
        P.add("dve", lambda e: e.memset(ds[:], 0.0), outs=[ds])
        for tt in range(NT):
            emit_delta_residual_store(C, xin, xout, ds,
                                      tt_range=[tt], tag=tag, tstride=512,
                                      t_base=(tt // 4) * 512)
    for t_ in (wlr, sc, sqs, nrm, mx, c0n, c0big, ptil, rb, rsum, ds, ufm, utmb):
        P.sfree(t_)
    P.sfree(wl)
    P.sfree(wr)


def emit_qkv_spill(C, ufm, Wkqv, uu, scale_q=0.125, bq=None, bk=None,
                   bvbig=None):
    P = C.P
    qd = C.scratch_dram(f"qd_{uu}", [D, T], BF16)
    kd = C.scratch_dram(f"kd_{uu}", [D, T], BF16)
    vd = C.scratch_dram(f"vd_{uu}", [128, NT * 16 * 65], BF16)
    stage = [P.salloc(f"qkvs_{uu}_{i}", [128, 512], BF16) for i in range(2)]
    vstage = [P.salloc(f"vst_{uu}_{i}", [128, 16 * 65], BF16) for i in range(2)]
    pb = [P.palloc() for _ in range(2)]
    sc = constf(C, scale_q)
    for sblk, dram, scaled in ((0, kd, False), (1, qd, True)):
        for oc in range(NF):
            for tc in range(T // 512):
                bank = pb[(oc * 8 + tc) % 2]
                st = stage[(oc * 8 + tc) % 2]
                for fc in range(NF):
                    P.add("pe", lambda e, f=fc, o=oc, t=tc, b=bank, sb=sblk,
                          s=(fc == 0), p=(fc == NF - 1):
                          e.matmul(b[:], wslice(Wkqv, 3072, f, sb * 1024 + o * 128, 128),
                                   fmv(ufm, f, t * 512, 512), start=s, stop=p),
                          ins=[Wkqv, ufm], outs=[bank])
                bias = bq if scaled else bk
                if bias is not None:
                    P.add("dve", lambda e, b=bank, st=st, o=oc, bi=bias, s2=scaled:
                          e.tensor_scalar(out=st[:], in0=b[:],
                                          scalar1=(scale_q if s2 else 1.0),
                                          scalar2=bi[:, o:o + 1],
                                          op0=ALU.mult, op1=ALU.add),
                          ins=[bank, bias], outs=[st])
                elif scaled:
                    P.add("act", lambda e, b=bank, st=st: e.activation(
                        out=st[:], in_=b[:], func=AF.Copy, scale=sc[:]),
                        ins=[bank, sc], outs=[st])
                else:
                    P.add("act", lambda e, b=bank, st=st: e.activation(
                        out=st[:], in_=b[:], func=AF.Copy),
                        ins=[bank, sc], outs=[st])
                P.dma("sp", dram[oc * 128:(oc + 1) * 128, tc * 512:(tc + 1) * 512],
                      st[:])
    for tt in range(NT):
        vs = vstage[tt % 2]
        ones_ap = AP(vs, 64, [[16 * 65, 128], [65, 16], [0, 1]])
        P.add("pool", lambda g, v=vs, oa=ones_ap: g.memset(oa, 1.0), outs=[vs])
        for half in range(2):
            bank = pb[half]
            for fc in range(NF):
                P.add("pe", lambda e, f=fc, t=tt, b=bank, hf=half,
                      s=(fc == 0), p=(fc == NF - 1):
                      e.matmul(b[:], fmv(ufm, f, t * 128, 128),
                               wslice(Wkqv, 3072, f, 2048 + hf * 512, 512),
                               start=s, stop=p),
                      ins=[ufm, Wkqv], outs=[bank])
            oap = AP(vs, half * 8 * 65, [[16 * 65, 128], [65, 8], [1, 64]])
            ibk = bank.ap().rearrange("p (h j) -> p h j", h=8)
            if bvbig is not None:
                bvv = bvbig.ap().rearrange("p (h j) -> p h j", h=16)[
                    :, half * 8:(half + 1) * 8, :]
                P.add("dve", lambda e, o=oap, i=ibk, b=bvv: e.tensor_tensor(
                    out=o, in0=i, in1=b, op=ALU.add),
                    ins=[bank, bvbig], outs=[vs])
            else:
                P.add("dve", lambda e, o=oap, i=ibk: e.tensor_copy(out=o, in_=i),
                      ins=[bank], outs=[vs])
        P.dma("sp", vd[:, tt * 1040:(tt + 1) * 1040], vs[:])
    for t_ in stage + vstage:
        P.sfree(t_)
    for b in pb:
        P.pfree(b)
    return qd, kd, vd


def emit_l1_attn(C, xin, xout, tag=""):
    P = C.P
    uu = P.uid()
    Wkqv = load_weight(C, "L1_attn_wkqv", D, 3072)
    onesb = emit_ones_col_bf(C)
    u = emit_outer_ln(C, xin, want_fm=True, tag=tag)
    ufm = u["u_fm"]
    qd, kd, vd = emit_qkv_spill(C, ufm, Wkqv, uu)
    P.sfree(ufm)
    P.sfree(Wkqv)
    NH, HD = 16, 64
    NKT = SEQ // 128
    BB = []
    pbS = [P.palloc() for _ in range(2)]
    pbrb = P.palloc()
    for par in range(3):
        BB.append(dict(
            qsb=P.salloc(f"l1q_{uu}_{par}", [64, SEQ], BF16),
            ksb=P.salloc(f"l1k_{uu}_{par}", [64, SEQ], BF16),
            sq=P.salloc(f"l1sq_{uu}_{par}", [64, 512], BF16),
            nrm=P.salloc(f"l1n_{uu}_{par}", [1, 16], F32),
            c0n=P.salloc(f"l1c0_{uu}_{par}", [1, 1], F32),
            c0big=P.salloc(f"l1c0b_{uu}_{par}", [128, 1], F32),
            ptil=P.salloc(f"l1p_{uu}_{par}", [128, NKT * 512], BF16),
            rsum=P.salloc(f"l1rs_{uu}_{par}", [1, 512], F32),
            rbig=P.salloc(f"l1rb_{uu}_{par}", [64, 512], F32),
            pbA=P.palloc(),
        ))
    vtmb = P.salloc(f"l1v_{uu}", [128, NKT * 16 * 65], BF16)
    afm = P.salloc(f"l1a_{uu}", [128, NF * T], BF16)

    def head_body(b_, h):
        B = BB[h % 3]
        qsb, ksb, sq, nrm = B["qsb"], B["ksb"], B["sq"], B["nrm"]
        c0n, c0big, ptil = B["c0n"], B["c0big"], B["ptil"]
        rsum, rbig = B["rsum"], B["rbig"]
        pbA = B["pbA"]
        P.dma("sp", qsb[:], qd[h * 64:(h + 1) * 64, b_ * SEQ:(b_ + 1) * SEQ])
        P.dma("sp", ksb[:], kd[h * 64:(h + 1) * 64, b_ * SEQ:(b_ + 1) * SEQ])
        bank_n = pbA
        for side, src_ in ((0, qsb), (1, ksb)):
            for tc in range(SEQ // 512):
                P.add("act", lambda e, s=src_, t=tc, sq=sq: e.activation(
                    out=sq[:], in_=s[:, t * 512:(t + 1) * 512], func=AF.Square),
                    ins=[src_], outs=[sq])
                P.add("pe", lambda e, bn=bank_n, sq=sq: e.matmul(
                    bn[0:1, :], onesb[0:64, :], sq[:],
                    start=True, stop=True), ins=[onesb, sq], outs=[bank_n])
                P.add("dve", lambda e, sd=side, t=tc, bn=bank_n, nrm=nrm:
                      e.reduce_max(
                          nrm[0:1, sd * 8 + t: sd * 8 + t + 1],
                          bn[0:1, :], axis=AX.X), ins=[bank_n], outs=[nrm])
        P.add("dve", lambda e, nrm=nrm, c0n=c0n: e.reduce_max(
            c0n[:], nrm[0:1, 0:SEQ // 512], axis=AX.X), ins=[nrm], outs=[c0n])
        P.add("dve", lambda e, nrm=nrm: e.reduce_max(
            nrm[0:1, 0:1], nrm[0:1, 8:8 + SEQ // 512], axis=AX.X),
            ins=[nrm], outs=[nrm])
        P.add("dve", lambda e, nrm=nrm, c0n=c0n: e.tensor_mul(
            c0n[:], c0n[:], nrm[0:1, 0:1]), ins=[c0n, nrm], outs=[c0n])
        P.add("act", lambda e, c0n=c0n: e.activation(
            out=c0n[:], in_=c0n[:], func=AF.Ln), ins=[c0n], outs=[c0n])
        P.add("act", lambda e, c0n=c0n: e.activation(
            out=c0n[:], in_=c0n[:], func=AF.Exp,
            scale=constf(C, 0.5)[0:1, :]),
            ins=[c0n, constf(C, 0.5)], outs=[c0n])
        P.add("dve", lambda e, c0n=c0n: e.tensor_scalar_mul(c0n[:], c0n[:], -1.0),
              ins=[c0n], outs=[c0n])
        P.add("pe", lambda e, bn=bank_n, c0n=c0n: e.matmul(
            bn[:, 0:1], C.ones_row[:], c0n[:], start=True, stop=True),
            ins=[C.ones_row, c0n], outs=[bank_n])
        P.add("dve", lambda e, bn=bank_n, c0big=c0big: e.tensor_copy(
            out=c0big[:], in_=bn[:, 0:1]), ins=[bank_n], outs=[c0big])
        for qch in range(SEQ // 512):
            ktmax = qch * 4 + 4
            for kt in range(ktmax):
                bank = pbS[kt % 2]
                P.add("pe", lambda e, k=kt, q=qch, b=bank, qsb=qsb, ksb=ksb:
                      e.matmul(b[:], ksb[:, k * 128:(k + 1) * 128],
                               qsb[:, q * 512:(q + 1) * 512],
                               start=True, stop=True),
                      ins=[ksb, qsb], outs=[bank])
                P.add("act", lambda e, b=bank, k=kt, ptil=ptil, c0big=c0big:
                      e.activation(
                          out=ptil[:, k * 512:(k + 1) * 512], in_=b[:],
                          func=AF.Exp, bias=c0big[:]),
                      ins=[bank, c0big], outs=[ptil])
                if kt >= qch * 4:
                    P.add("pool", lambda g, k=kt, q=qch, ptil=ptil:
                          g.affine_select(
                              out=ptil[:, k * 512:(k + 1) * 512],
                              in_=ptil[:, k * 512:(k + 1) * 512],
                              pattern=[[1, 512]], compare_op=ALU.is_ge, fill=0.0,
                              base=q * 512 - k * 128, channel_multiplier=-1),
                          ins=[ptil], outs=[ptil])
            for kt in range(ktmax):
                P.add("pe", lambda e, k=kt, h=h, b=pbA, ptil=ptil,
                      s=(kt == 0), p=(kt == ktmax - 1):
                      e.matmul(b[0:65, :],
                               vtmb[:, k * 1040 + h * 65: k * 1040 + (h + 1) * 65],
                               ptil[:, k * 512:(k + 1) * 512], start=s, stop=p),
                      ins=[vtmb, ptil], outs=[pbA])
            P.add("dve", lambda e, b=pbA, rsum=rsum: e.tensor_copy(
                out=rsum[:], in_=b[64:65, :]), ins=[pbA], outs=[rsum])
            P.add("dve", lambda e, rsum=rsum: e.reciprocal(rsum[:], rsum[:]),
                  ins=[rsum], outs=[rsum])
            P.add("pe", lambda e, pb_=pbrb, rsum=rsum: e.matmul(
                pb_[0:64, :], C.ones_row[0:1, 0:64], rsum[:],
                start=True, stop=True), ins=[C.ones_row, rsum], outs=[pbrb])
            P.add("dve", lambda e, pb_=pbrb, rbig=rbig: e.tensor_copy(
                out=rbig[:], in_=pb_[0:64, :]), ins=[pbrb], outs=[rbig])
            oc = h // 2
            pb0 = (h % 2) * 64
            tg = b_ * 4 + qch
            P.add("dve", lambda e, b=pbA, o=oc, p0=pb0, q=qch, bb=b_, rbig=rbig:
                  e.tensor_mul(
                      afm[p0:p0 + 64,
                          o * T + bb * SEQ + q * 512: o * T + bb * SEQ + (q + 1) * 512],
                      b[0:64, :], rbig[:]),
                  ins=[pbA, rbig], outs=[f"l1afm{uu}_{h}_{tg}"])

    for b_ in range(BPC):
        P.dma("sp", vtmb[:], vd[:, b_ * NKT * 1040:(b_ + 1) * NKT * 1040])
        for h in range(NH):
            head_body(b_, h)
    for B in BB:
        for k_ in ("qsb", "ksb", "sq", "nrm", "c0n", "c0big", "ptil", "rsum",
                   "rbig"):
            P.sfree(B[k_])
        P.pfree(B["pbA"])
    for b in pbS + [pbrb]:
        P.pfree(b)
    P.sfree(vtmb)
    Wp = load_weight(C, "L1_attn_w_proj", D, D)
    bp = load_bias(C, "L1_attn_b_proj", D)
    ds = P.salloc(f"l1d_{uu}", [128, NF * 512], F32)
    pbo = [P.palloc() for _ in range(2)]
    for tc in range(T // 512):
        for oc in range(NF):
            bank = pbo[oc % 2]
            for fc in range(NF):
                P.add("pe", lambda e, f=fc, o=oc, t=tc, b=bank,
                      s=(fc == 0), p=(fc == NF - 1):
                      e.matmul(b[:], wslice(Wp, D, f, o * 128, 128),
                               fmv(afm, f, t * 512, 512), start=s, stop=p),
                      ins=[Wp, f"l1afm{uu}_{2 * fc}_{tc}",
                           f"l1afm{uu}_{2 * fc + 1}_{tc}"], outs=[bank])
            P.add("dve", lambda e, o=oc, b=bank: e.tensor_scalar(
                out=ds[:, o * 512:(o + 1) * 512], in0=b[:],
                scalar1=bp[:, o:o + 1], scalar2=None, op0=ALU.add),
                ins=[bank, bp], outs=[ds])
        emit_delta_residual_store(C, xin, xout, ds,
                                  tt_range=range(tc * 4, tc * 4 + 4),
                                  tag=tag, tstride=512, t_base=tc * 512)
    for b in pbo:
        P.pfree(b)
    for t_ in (afm, ds, Wp, bp):
        P.sfree(t_)


def emit_quantize_weight(C, name, n_in, n_out, uu):
    P = C.P
    src = C.win[name]
    nch = n_in // 128
    Wq = [P.salloc(f"wq_{name}_{uu}_{c}", [128, n_out], BF16)
          for c in range(nch)]
    wst = [P.salloc(f"wst_{name}_{i}", [128, n_out], F32) for i in range(2)]
    gsum = P.salloc(f"gs_{name}", [128, nch], F32)
    dump = P.salloc(f"gd_{name}", [128, n_out], BF16)
    tots = P.salloc(f"gt_{name}", [128, 1], F32)
    onesf = constf(C, 1.0)
    nthr = P.salloc(f"gn_{name}", [128, 1], F32)
    bank = P.palloc()
    P.add("dve", lambda e: e.memset(gsum[:], 0.0), outs=[gsum])
    for c in range(nch):
        w_sb = wst[c % 2]
        P.dma("sp", w_sb[:], src[c * 128:(c + 1) * 128, :])
        P.add("act", lambda e, w=w_sb, c=c: e.activation(
            out=dump[:], in_=w[:], func=AF.Abs, accum_out=gsum[:, c:c + 1]),
            ins=[w_sb], outs=[dump, gsum])
    P.add("dve", lambda e: e.reduce_sum(tots[:], gsum[:], axis=AX.X),
          ins=[gsum], outs=[tots])
    P.add("pe", lambda e, b=bank: e.matmul(b[0:1, 0:1], tots[:], onesf[:],
                                           start=True, stop=True),
          ins=[tots, onesf], outs=[bank])
    P.add("dve", lambda e, b=bank: e.tensor_scalar(
        out=b[0:1, 0:1], in0=b[0:1, 0:1], scalar1=-0.5 / (n_in * n_out),
        scalar2=-0.5e-6, op0=ALU.mult, op1=ALU.add), ins=[bank], outs=[bank])
    P.add("dve", lambda e, b=bank: e.tensor_copy(out=nthr[0:1, :], in_=b[0:1, 0:1]),
          ins=[bank], outs=[nthr])
    P.add("pe", lambda e, b=bank: e.matmul(b[:, 0:1], C.ones_row[:], nthr[0:1, :],
                                           start=True, stop=True),
          ins=[C.ones_row, nthr], outs=[bank])
    P.add("dve", lambda e, b=bank: e.tensor_copy(out=nthr[:], in_=b[:, 0:1]),
          ins=[bank], outs=[nthr])
    P.pfree(bank)
    sgn = P.salloc(f"gsgn_{name}", [128, n_out], BF16)
    mm = P.salloc(f"gm_{name}", [128, n_out], F32)
    for c in range(nch):
        w_sb = wst[c % 2]
        P.dma("sp", w_sb[:], src[c * 128:(c + 1) * 128, :])
        P.add("act", lambda e, w=w_sb: e.activation(
            out=mm[:], in_=w[:], func=AF.Abs),
            ins=[w_sb], outs=[mm])
        P.add("act", lambda e: e.activation(out=mm[:], in_=mm[:], func=AF.Sign,
                                            bias=nthr[:]),
              ins=[mm, nthr], outs=[mm])
        P.add("dve", lambda e: e.tensor_scalar_max(mm[:], mm[:], 0.0),
              ins=[mm], outs=[mm])
        P.add("act", lambda e, w=w_sb: e.activation(
            out=sgn[:], in_=w[:], func=AF.Sign), ins=[w_sb], outs=[sgn])
        P.add("dve", lambda e, c=c: e.tensor_mul(
            Wq[c][:], sgn[:], mm[:]),
            ins=[sgn, mm], outs=[Wq[c]])
    for t_ in wst + [gsum, dump, tots, nthr, sgn, mm]:
        P.sfree(t_)
    return Wq


def emit_l0_ffn(C, xin, xout, tag=""):
    P = C.P
    uu = P.uid()
    HID = 4096
    NH = HID // 128
    onesb = emit_ones_col_bf(C)
    u = emit_outer_ln(C, xin, want_fm=True, tag=tag)
    ufm = u["u_fm"]
    Wqf = emit_quantize_weight(C, "L0_ffn_W_fc", D, HID, uu)
    h_d = C.scratch_dram(f"l0fh_{uu}", [HID, T], BF16)
    hst = [P.salloc(f"l0hs_{uu}_{i}", [128, 512], BF16) for i in range(2)]
    hsq = P.salloc(f"l0hq_{uu}", [128, 512], BF16)
    mu = P.salloc(f"l0mu_{uu}", [1, T], F32)
    rstd = P.salloc(f"l0rs_{uu}", [1, T], F32)
    pb = [P.palloc() for _ in range(2)]
    pbs1 = P.palloc()
    pbs2 = P.palloc()
    for tc in range(T // 512):
        for oc in range(NH):
            bank = pb[oc % 2]
            st = hst[oc % 2]
            for fc in range(NF):
                P.add("pe", lambda e, f=fc, o=oc, t=tc, b=bank,
                      s=(fc == 0), p=(fc == NF - 1):
                      e.matmul(b[:], wslice(Wqf, HID, f, o * 128, 128),
                               fmv(ufm, f, t * 512, 512), start=s, stop=p),
                      ins=[Wqf, ufm], outs=[bank])
            P.add("act", lambda e, b=bank, st=st: e.activation(
                out=st[:], in_=b[:], func=AF.Gelu), ins=[bank], outs=[st])
            P.add("act", lambda e, st=st: e.activation(
                out=hsq[:], in_=st[:], func=AF.Square), ins=[st], outs=[hsq])
            P.add("pe", lambda e, o=oc, st=st, s=(oc == 0), p=(oc == NH - 1):
                  e.matmul(pbs1[0:1, :], onesb[:], st[:], start=s, stop=p),
                  ins=[onesb, st], outs=[pbs1])
            P.add("pe", lambda e, o=oc, s=(oc == 0), p=(oc == NH - 1):
                  e.matmul(pbs2[0:1, :], onesb[:], hsq[:], start=s, stop=p),
                  ins=[onesb, hsq], outs=[pbs2])
            P.dma("sp", h_d[oc * 128:(oc + 1) * 128, tc * 512:(tc + 1) * 512],
                  st[:])
        P.add("dve", lambda e, t=tc: e.tensor_scalar_mul(
            mu[0:1, t * 512:(t + 1) * 512], pbs1[0:1, :], 1.0 / HID),
            ins=[pbs1], outs=[mu])
        P.add("dve", lambda e, t=tc: e.tensor_mul(
            rstd[0:1, t * 512:(t + 1) * 512], mu[0:1, t * 512:(t + 1) * 512],
            mu[0:1, t * 512:(t + 1) * 512]), ins=[mu], outs=[rstd])
        P.add("dve", lambda e, t=tc: e.scalar_tensor_tensor(
            rstd[0:1, t * 512:(t + 1) * 512], pbs2[0:1, :], 1.0 / HID,
            rstd[0:1, t * 512:(t + 1) * 512], ALU.mult, ALU.subtract),
            ins=[pbs2, rstd], outs=[rstd])
    P.add("act", lambda e: e.activation(out=rstd[:], in_=rstd[:], func=AF.Ln,
                                        bias=constf(C, LN_EPS)[0:1, :]),
          ins=[rstd, constf(C, LN_EPS)], outs=[rstd])
    P.add("act", lambda e: e.activation(out=rstd[:], in_=rstd[:], func=AF.Exp,
                                        scale=constf(C, -0.5)[0:1, :]),
          ins=[rstd, constf(C, -0.5)], outs=[rstd])
    for t_ in hst + [hsq]:
        P.sfree(t_)
    P.sfree(ufm)
    P.sfree(Wqf)
    for b in pb:
        P.pfree(b)
    P.pfree(pbs1)
    P.pfree(pbs2)
    Wqp = emit_quantize_weight(C, "L0_ffn_W_proj", HID, D, uu)
    lnw = load_bias(C, "L0_ffn_ln_w", HID)
    lnb = load_bias(C, "L0_ffn_ln_b", HID)
    bias_fm = P.salloc(f"l0bv_{uu}", [128, NF], F32)
    lnb_bf = P.salloc(f"l0lb_{uu}", [128, NH], BF16)
    P.add("dve", lambda e: e.tensor_copy(out=lnb_bf[:], in_=lnb[:]),
          ins=[lnb], outs=[lnb_bf])
    bank = P.palloc()
    for oc in range(NF):
        for hc in range(NH):
            P.add("pe", lambda e, h=hc, o=oc, b=bank, s=(hc == 0), p=(hc == NH - 1):
                  e.matmul(b[:, o:o + 1], wslice(Wqp, D, h, o * 128, 128),
                           lnb_bf[:, h:h + 1], start=s, stop=p),
                  ins=[Wqp, lnb_bf], outs=[bank])
    P.add("dve", lambda e, b=bank: e.tensor_copy(out=bias_fm[:], in_=b[:, 0:NF]),
          ins=[bank], outs=[bias_fm])
    P.pfree(bank)
    for hc in range(NH):
        P.add("dve", lambda e, h=hc: e.tensor_scalar_mul(
            wslice(Wqp, D, h, 0, D), wslice(Wqp, D, h, 0, D), lnw[:, h:h + 1]),
            ins=[Wqp, lnw], outs=[Wqp])
    hin = [P.salloc(f"l0hi_{uu}_{i}", [128, 512], BF16) for i in range(2)]
    hna = P.salloc(f"l0hn_{uu}", [128, NH * 512], BF16)
    ds = P.salloc(f"l0d_{uu}", [128, NF * 512], F32)
    pbo = [P.palloc() for _ in range(2)]
    pbmu = P.palloc()
    pbrs = P.palloc()
    for tc in range(T // 512):
        P.add("pe", lambda e, t=tc, b=pbmu: e.matmul(
            b[:], C.ones_row[:], mu[0:1, t * 512:(t + 1) * 512],
            start=True, stop=True), ins=[C.ones_row, mu], outs=[pbmu])
        P.add("pe", lambda e, t=tc, b=pbrs: e.matmul(
            b[:], C.ones_row[:], rstd[0:1, t * 512:(t + 1) * 512],
            start=True, stop=True), ins=[C.ones_row, rstd], outs=[pbrs])
        for hc in range(NH):
            h_sb = hin[hc % 2]
            P.dma("sp", h_sb[:], h_d[hc * 128:(hc + 1) * 128,
                                     tc * 512:(tc + 1) * 512])
            P.add("dve", lambda e, h=h_sb, c=hc: e.tensor_sub(
                hna[:, c * 512:(c + 1) * 512], h[:], pbmu[:]),
                ins=[h_sb, pbmu], outs=[hna])
            P.add("dve", lambda e, c=hc: e.tensor_mul(
                hna[:, c * 512:(c + 1) * 512], hna[:, c * 512:(c + 1) * 512],
                pbrs[:]), ins=[hna, pbrs], outs=[hna])
        for oc in range(NF):
            bank = pbo[oc % 2]
            for hc in range(NH):
                P.add("pe", lambda e, h=hc, o=oc, b=bank,
                      s=(hc == 0), p=(hc == NH - 1):
                      e.matmul(b[:], wslice(Wqp, D, h, o * 128, 128),
                               hna[:, h * 512:(h + 1) * 512], start=s, stop=p),
                      ins=[Wqp, hna], outs=[bank])
            P.add("dve", lambda e, o=oc, b=bank: e.tensor_scalar(
                out=ds[:, o * 512:(o + 1) * 512], in0=b[:],
                scalar1=bias_fm[:, o:o + 1], scalar2=None, op0=ALU.add),
                ins=[bank, bias_fm], outs=[ds])
        emit_delta_residual_store(C, xin, xout, ds,
                                  tt_range=range(tc * 4, tc * 4 + 4),
                                  tag=tag, tstride=512, t_base=tc * 512)
    for b in pbo + [pbmu, pbrs]:
        P.pfree(b)
    for t_ in hin + [hna, ds, bias_fm, lnb_bf, Wqp, lnw, lnb, mu, rstd]:
        P.sfree(t_)


def emit_l1_ffn(C, xin, xout, tag=""):
    P = C.P
    uu = P.uid()
    u = emit_outer_ln(C, xin, want_fm=True, want_lo_fm=True, tag=tag)
    ufm, ulofm = u["u_fm"], u["ulo_fm"]
    Rf = P.salloc(f"l1rf_{uu}", [128, NF * 8], F32)
    Rhi = P.salloc(f"l1rh_{uu}", [128, NF * 8], BF16)
    Rlo = P.salloc(f"l1rl_{uu}", [128, NF * 8], BF16)
    P.dma("sp", Rf[:].rearrange("p (c o) -> p c o", c=NF),
          C.win["L1_ffn_router_w"].rearrange("(c p) o -> p c o", p=128))
    P.dma("pool", Rhi[:].rearrange("p (c o) -> p c o", c=NF),
          C.win["L1_ffn_router_w"].rearrange("(c p) o -> p c o", p=128))
    P.add("dve", lambda e: e.tensor_sub(Rlo[:], Rf[:], Rhi[:]),
          ins=[Rf, Rhi], outs=[Rlo])
    rb_row = P.salloc(f"l1rbr_{uu}", [1, 8], F32)
    P.dma("sp", rb_row[:], C.win["L1_ffn_router_b"].unsqueeze(0))
    rb_big = P.salloc(f"l1rbb_{uu}", [128, 8], F32)
    bank = P.palloc()
    P.add("pe", lambda e, b=bank: e.matmul(b[:, 0:8], C.ones_row[:], rb_row[:],
                                           start=True, stop=True),
          ins=[C.ones_row, rb_row], outs=[bank])
    P.add("dve", lambda e, b=bank: e.tensor_copy(out=rb_big[:], in_=b[:, 0:8]),
          ins=[bank], outs=[rb_big])
    P.pfree(bank)
    wta = P.salloc(f"l1wa_{uu}", [128, NT * 8], F32)
    ltm = P.salloc(f"l1lt_{uu}", [128, 8], F32)
    gnm = P.salloc(f"l1gn_{uu}", [128, 1], F32)
    gs = P.salloc(f"l1gs_{uu}", [128, 1], F32)
    wfm = P.salloc(f"l1wf_{uu}", [8, T], BF16)
    pbl = [P.palloc() for _ in range(2)]
    for tt in range(NT):
        bank = pbl[tt % 2]
        first = True
        for fc in range(NF):
            for uf, rw in ((ufm, Rhi), (ufm, Rlo), (ulofm, Rhi)):
                P.add("pe", lambda e, f=fc, t=tt, uf=uf, rw=rw, b=bank,
                      s=first, p=(fc == NF - 1 and rw is Rhi and uf is ulofm):
                      e.matmul(b[:, 0:8], fmv(uf, f, t * 128, 128),
                               wslice(rw, 8, f, 0, 8), start=s, stop=p),
                      ins=[uf, rw], outs=[bank])
                first = False
        P.add("dve", lambda e, b=bank: e.tensor_add(ltm[:], b[:, 0:8], rb_big[:]),
              ins=[bank, rb_big], outs=[ltm])
        P.add("dve", lambda e: e.reduce_max(gnm[:], ltm[:], axis=AX.X, negate=True),
              ins=[ltm], outs=[gnm])
        P.add("act", lambda e: e.activation(out=ltm[:], in_=ltm[:], func=AF.Exp,
                                            bias=gnm[:], accum_out=gs[:]),
              ins=[ltm, gnm], outs=[ltm, gs])
        P.add("dve", lambda e: e.reciprocal(gs[:], gs[:]), ins=[gs], outs=[gs])
        P.add("dve", lambda e: e.tensor_scalar_mul(ltm[:], ltm[:], gs[:]),
              ins=[ltm, gs], outs=[ltm])
        P.add("dve", lambda e: e.reduce_max(gnm[:], ltm[:], axis=AX.X),
              ins=[ltm], outs=[gnm])
        P.add("dve", lambda e, t=tt: e.tensor_scalar(
            out=wta[:, t * 8:(t + 1) * 8], in0=ltm[:], scalar1=gnm[:],
            scalar2=gnm[:], op0=ALU.is_ge, op1=ALU.mult),
            ins=[ltm, gnm], outs=[wta])
        P.add("pe", lambda e, t=tt, b=bank: e.transpose(
            b[0:8, 0:128], wta[:, t * 8:(t + 1) * 8], C.ident_f32[:]),
            ins=[wta, C.ident_f32], outs=[bank])
        P.add("dve", lambda e, t=tt, b=bank: e.tensor_copy(
            out=wfm[:, t * 128:(t + 1) * 128], in_=b[0:8, 0:128]),
            ins=[bank], outs=[wfm])
    for b in pbl:
        P.pfree(b)
    P.sfree(ulofm)
    for t_ in (Rf, Rhi, Rlo, rb_row, rb_big, ltm, gnm, gs):
        P.sfree(t_)
    moe_d = C.scratch_dram(f"l1moe_{uu}", [D, T], F32)
    Bst = P.salloc(f"l1bst_{uu}", [8, D], BF16)
    P.dma("pool", Bst[:], C.win["L1_ffn_expert_b"])
    wexp = P.salloc(f"l1we_{uu}", [128, 128], F32)
    wbig = P.salloc(f"l1wb_{uu}", [128, 512], F32)
    stage = [P.salloc(f"l1st_{uu}_{i}", [128, 512], F32) for i in range(2)]
    pbw = P.palloc()
    pbo = [P.palloc() for _ in range(2)]
    for tc in range(T // 512):
        for oc in range(NF):
            bank = pbo[oc % 2]
            st = stage[oc % 2]
            P.add("pe", lambda e, o=oc, t=tc, b=bank: e.matmul(
                b[:], Bst[:, o * 128:(o + 1) * 128],
                wfm[:, t * 512:(t + 1) * 512], start=True, stop=True),
                ins=[Bst, wfm], outs=[bank])
            P.add("dve", lambda e, b=bank, st=st: e.tensor_copy(
                out=st[:], in_=b[:]), ins=[bank], outs=[st])
            P.dma("pool", moe_d[oc * 128:(oc + 1) * 128, tc * 512:(tc + 1) * 512],
                  st[:], out_key=f"l1moe_{uu}_{oc}_{tc}")
    for e_ in range(8):
        We = load_weight(C, f"l1w{e_}_{uu}", D, D,
                         dram_ap=C.win["L1_ffn_expert_w"][e_ * D:(e_ + 1) * D, :])
        for tc in range(T // 512):
            for i in range(4):
                tt = tc * 4 + i
                eap = AP(wta, tt * 8 + e_, [[NT * 8, 128], [0, 128]])
                P.add("dve", lambda e, a=eap: e.tensor_copy(out=wexp[:], in_=a),
                      ins=[wta], outs=[wexp])
                P.add("pe", lambda e, i=i, b=pbw: e.transpose(
                    b[:, i * 128:(i + 1) * 128], wexp[:], C.ident_f32[:]),
                    ins=[wexp, C.ident_f32], outs=[pbw])
            P.add("dve", lambda e, b=pbw: e.tensor_copy(out=wbig[:], in_=b[:]),
                  ins=[pbw], outs=[wbig])
            for oc in range(NF):
                bank = pbo[oc % 2]
                st = stage[oc % 2]
                for fc in range(NF):
                    P.add("pe", lambda e, f=fc, o=oc, t=tc, b=bank, W=We,
                          s=(fc == 0), p=(fc == NF - 1):
                          e.matmul(b[:], wslice(W, D, f, o * 128, 128),
                                   fmv(ufm, f, t * 512, 512), start=s, stop=p),
                          ins=[We, ufm], outs=[bank])
                P.add("dve", lambda e, b=bank, st=st: e.tensor_mul(
                    st[:], b[:], wbig[:]), ins=[bank, wbig], outs=[st])
                P.dma("pool",
                      moe_d[oc * 128:(oc + 1) * 128, tc * 512:(tc + 1) * 512],
                      st[:], accum_op=ALU.add, out_key=f"l1moe_{uu}_{oc}_{tc}")
        P.sfree(We)
    for b in pbo + [pbw]:
        P.pfree(b)
    for t_ in (wta, wfm, Bst, wexp, wbig) + tuple(stage):
        P.sfree(t_)
    P.sfree(ufm)
    ds = P.salloc(f"l1d_{uu}", [128, NF * 512], F32)
    for tc in range(T // 512):
        for oc in range(NF):
            P.dma("sp", ds[:, oc * 512:(oc + 1) * 512],
                  moe_d[oc * 128:(oc + 1) * 128, tc * 512:(tc + 1) * 512],
                  in_key=f"l1moe_{uu}_{oc}_{tc}")
        emit_delta_residual_store(C, xin, xout, ds,
                                  tt_range=range(tc * 4, tc * 4 + 4),
                                  tag=tag, tstride=512, t_base=tc * 512)
    P.sfree(ds)


def emit_row_broadcast(C, name, n, uu):
    P = C.P
    row = P.salloc(f"rb_{name}_{uu}", [1, n], F32)
    P.dma("sp", row[:], C.win[name].unsqueeze(0))
    big = P.salloc(f"rbb_{name}_{uu}", [128, n], F32)
    bank = P.palloc()
    for c0 in range(0, n, 512):
        cw = min(512, n - c0)
        P.add("pe", lambda e, c0=c0, cw=cw, b=bank: e.matmul(
            b[:, 0:cw], C.ones_row[:], row[0:1, c0:c0 + cw],
            start=True, stop=True), ins=[C.ones_row, row], outs=[bank])
        P.add("dve", lambda e, c0=c0, cw=cw, b=bank: e.tensor_copy(
            out=big[:, c0:c0 + cw], in_=b[:, 0:cw]), ins=[bank], outs=[big])
    P.pfree(bank)
    P.sfree(row)
    return big




def emit_ln_affine_tile(C, src_ap, src_t, dst_ap, dst_t, gbig, bbig, scr):
    P = C.P
    s1, s2, s3, dump = scr["s1"], scr["s2"], scr["s3"], scr["dump"]
    P.add("dve", lambda e: e.reduce_sum(s1[:], src_ap, axis=AX.X),
          ins=[src_t], outs=[s1])
    P.add("act", lambda e: e.activation(out=dump[:], in_=src_ap, func=AF.Square,
                                        accum_out=s2[:]),
          ins=[src_t], outs=[dump, s2])
    P.add("dve", lambda e: e.tensor_scalar_mul(s1[:], s1[:], 1.0 / D),
          ins=[s1], outs=[s1])
    P.add("dve", lambda e: e.tensor_mul(s3[:], s1[:], s1[:]), ins=[s1], outs=[s3])
    P.add("dve", lambda e: e.scalar_tensor_tensor(
        s2[:], s2[:], 1.0 / D, s3[:], ALU.mult, ALU.subtract),
        ins=[s2, s3], outs=[s2])
    P.add("act", lambda e: e.activation(out=s2[:], in_=s2[:], func=AF.Ln,
                                        bias=constf(C, LN_EPS)[:]),
          ins=[s2, constf(C, LN_EPS)], outs=[s2])
    P.add("act", lambda e: e.activation(out=s2[:], in_=s2[:], func=AF.Exp,
                                        scale=constf(C, -0.5)[:]),
          ins=[s2, constf(C, -0.5)], outs=[s2])
    P.add("dve", lambda e: e.scalar_tensor_tensor(
        s3[:], s1[:], -1.0, s2[:], ALU.mult, ALU.mult), ins=[s1, s2], outs=[s3])
    P.add("dve", lambda e: e.tensor_scalar(
        out=dst_ap, in0=src_ap, scalar1=s2[:], scalar2=s3[:],
        op0=ALU.mult, op1=ALU.add), ins=[src_t, s2, s3], outs=[dst_t])
    P.add("dve", lambda e: e.tensor_mul(dst_ap, dst_ap, gbig[:]),
          ins=[dst_t, gbig], outs=[dst_t])
    P.add("dve", lambda e: e.tensor_add(dst_ap, dst_ap, bbig[:]),
          ins=[dst_t, bbig], outs=[dst_t])


def emit_l0_attn(C, xin, xout, tag=""):
    P = C.P
    uu = P.uid()
    HID = 4096
    NH32 = HID // 128
    onesb = emit_ones_col_bf(C)
    u = emit_outer_ln(C, xin, want_fm=True, want_tm=True, tag=tag)
    ufm, utm = u["u_fm"], u["u_tm"]
    Wkqv = load_weight(C, "L0_attn_wkqv", D, 3072)
    bq = load_bias(C, "L0_attn_bq", D)
    bk = load_bias(C, "L0_attn_bk", D)
    P.add("dve", lambda e: e.tensor_scalar_mul(bq[:], bq[:], 0.125),
          ins=[bq], outs=[bq])
    bvbig32 = emit_row_broadcast(C, "L0_attn_bv", D, uu)
    bvbig = P.salloc(f"l0bvb_{uu}", [128, D], BF16)
    P.add("dve", lambda e: e.tensor_copy(out=bvbig[:], in_=bvbig32[:]),
          ins=[bvbig32], outs=[bvbig])
    P.sfree(bvbig32)
    qd, kd, vd = emit_qkv_spill(C, ufm, Wkqv, uu, bq=bq, bk=bk, bvbig=bvbig)
    P.sfree(ufm)
    P.sfree(Wkqv)
    for t_ in (bq, bk, bvbig):
        P.sfree(t_)
    u_d = C.scratch_dram(f"l0ud_{uu}", [128, NT * D], BF16)
    P.dma("sp", u_d[:, :], utm[:])
    P.sfree(utm)
    NKT = SEQ // 128
    BB = []
    for par in range(2):
        BB.append(dict(
            qsb=P.salloc(f"l0q_{uu}_{par}", [64, SEQ], BF16),
            ksb=P.salloc(f"l0k_{uu}_{par}", [64, SEQ], BF16),
            sq=P.salloc(f"l0sqt_{uu}_{par}", [64, 512], BF16),
            nrm=P.salloc(f"l0n_{uu}_{par}", [1, 2 * SEQ], F32),
            c0n=P.salloc(f"l0c0_{uu}_{par}", [1, 1], F32),
            c0big=P.salloc(f"l0c0b_{uu}_{par}", [128, 1], F32),
            ptil=P.salloc(f"l0p_{uu}_{par}", [128, NKT * 512], BF16),
            rsum=P.salloc(f"l0rsm_{uu}_{par}", [1, 512], F32),
            rbig=P.salloc(f"l0rbg_{uu}_{par}", [64, 512], F32),
            pbS=[P.palloc() for _ in range(2)],
            pbA=P.palloc(),
            pbrb=P.palloc(),
        ))
    vtmb = P.salloc(f"l0v_{uu}", [128, NKT * 16 * 65], BF16)
    afm = P.salloc(f"l0a_{uu}", [128, NF * T], BF16)

    def head_body(b_, h):
        B = BB[h % 2]
        qsb, ksb, sq, nrm = B["qsb"], B["ksb"], B["sq"], B["nrm"]
        c0n, c0big, ptil = B["c0n"], B["c0big"], B["ptil"]
        rsum, rbig = B["rsum"], B["rbig"]
        pbS, pbA, pbrb = B["pbS"], B["pbA"], B["pbrb"]
        bank_n = pbA
        P.dma("sp", qsb[:], qd[h * 64:(h + 1) * 64, b_ * SEQ:(b_ + 1) * SEQ])
        P.dma("sp", ksb[:], kd[h * 64:(h + 1) * 64, b_ * SEQ:(b_ + 1) * SEQ])
        for side, src_ in ((0, qsb), (1, ksb)):
            for tc in range(SEQ // 512):
                P.add("act", lambda e, s=src_, t=tc, sq=sq: e.activation(
                    out=sq[:], in_=s[:, t * 512:(t + 1) * 512], func=AF.Square),
                    ins=[src_], outs=[sq])
                P.add("pe", lambda e, bn=bank_n, sq=sq: e.matmul(
                    bn[0:1, :], onesb[0:64, :], sq[:], start=True, stop=True),
                    ins=[onesb, sq], outs=[bank_n])
                P.add("dve", lambda e, sd=side, t=tc, bn=bank_n, nrm=nrm:
                      e.tensor_copy(
                          out=nrm[0:1, sd * SEQ + t * 512: sd * SEQ + (t + 1) * 512],
                          in_=bn[0:1, :]), ins=[bank_n], outs=[nrm])
        P.add("dve", lambda e, nrm=nrm, c0n=c0n: e.reduce_max(
            c0n[:], nrm[0:1, 0:SEQ], axis=AX.X), ins=[nrm], outs=[c0n])
        P.add("dve", lambda e, nrm=nrm: e.reduce_max(
            nrm[0:1, 0:1], nrm[0:1, SEQ:2 * SEQ], axis=AX.X),
            ins=[nrm], outs=[nrm])
        P.add("dve", lambda e, nrm=nrm, c0n=c0n: e.tensor_mul(
            c0n[:], c0n[:], nrm[0:1, 0:1]), ins=[c0n, nrm], outs=[c0n])
        P.add("act", lambda e, c0n=c0n: e.activation(
            out=c0n[:], in_=c0n[:], func=AF.Ln), ins=[c0n], outs=[c0n])
        P.add("act", lambda e, c0n=c0n: e.activation(
            out=c0n[:], in_=c0n[:], func=AF.Exp, scale=constf(C, 0.5)[0:1, :]),
            ins=[c0n, constf(C, 0.5)], outs=[c0n])
        P.add("dve", lambda e, c0n=c0n: e.tensor_scalar_mul(c0n[:], c0n[:], -1.0),
              ins=[c0n], outs=[c0n])
        P.add("pe", lambda e, bn=bank_n, c0n=c0n: e.matmul(
            bn[:, 0:1], C.ones_row[:], c0n[:], start=True, stop=True),
            ins=[C.ones_row, c0n], outs=[bank_n])
        P.add("dve", lambda e, bn=bank_n, c0big=c0big: e.tensor_copy(
            out=c0big[:], in_=bn[:, 0:1]), ins=[bank_n], outs=[c0big])
        for qch in range(SEQ // 512):
            kt_lo = max(0, qch * 4 - 2)
            kt_hi = min(NKT, qch * 4 + 6)
            for kt in range(kt_lo, kt_hi):
                bank = pbS[kt % 2]
                P.add("pe", lambda e, k=kt, q=qch, b=bank, qsb=qsb, ksb=ksb:
                      e.matmul(b[:], ksb[:, k * 128:(k + 1) * 128],
                               qsb[:, q * 512:(q + 1) * 512],
                               start=True, stop=True),
                      ins=[ksb, qsb], outs=[bank])
                P.add("act", lambda e, b=bank, k=kt, ptil=ptil, c0big=c0big:
                      e.activation(
                          out=ptil[:, k * 512:(k + 1) * 512], in_=b[:],
                          func=AF.Exp, bias=c0big[:]),
                      ins=[bank, c0big], outs=[ptil])
                P.add("pool", lambda g, k=kt, q=qch, ptil=ptil: g.affine_select(
                    out=ptil[:, k * 512:(k + 1) * 512],
                    in_=ptil[:, k * 512:(k + 1) * 512],
                    pattern=[[1, 512]], compare_op=ALU.is_ge, fill=0.0,
                    base=q * 512 - k * 128 + 256, channel_multiplier=-1),
                    ins=[ptil], outs=[ptil])
                P.add("pool", lambda g, k=kt, q=qch, ptil=ptil: g.affine_select(
                    out=ptil[:, k * 512:(k + 1) * 512],
                    in_=ptil[:, k * 512:(k + 1) * 512],
                    pattern=[[-1, 512]], compare_op=ALU.is_ge, fill=0.0,
                    base=-(q * 512) + k * 128 + 256, channel_multiplier=1),
                    ins=[ptil], outs=[ptil])
            nk = kt_hi - kt_lo
            for j, kt in enumerate(range(kt_lo, kt_hi)):
                P.add("pe", lambda e, k=kt, h=h, b=pbA, ptil=ptil,
                      s=(j == 0), p=(j == nk - 1):
                      e.matmul(b[0:65, :],
                               vtmb[:, k * 1040 + h * 65: k * 1040 + (h + 1) * 65],
                               ptil[:, k * 512:(k + 1) * 512], start=s, stop=p),
                      ins=[vtmb, ptil], outs=[pbA])
            P.add("dve", lambda e, b=pbA, rsum=rsum: e.tensor_copy(
                out=rsum[:], in_=b[64:65, :]), ins=[pbA], outs=[rsum])
            P.add("dve", lambda e, rsum=rsum: e.reciprocal(rsum[:], rsum[:]),
                  ins=[rsum], outs=[rsum])
            P.add("pe", lambda e, pb_=pbrb, rsum=rsum: e.matmul(
                pb_[0:64, :], C.ones_row[0:1, 0:64], rsum[:],
                start=True, stop=True), ins=[C.ones_row, rsum], outs=[pbrb])
            P.add("dve", lambda e, pb_=pbrb, rbig=rbig: e.tensor_copy(
                out=rbig[:], in_=pb_[0:64, :]), ins=[pbrb], outs=[rbig])
            oc = h // 2
            pb0 = (h % 2) * 64
            tg = b_ * 4 + qch
            P.add("dve", lambda e, b=pbA, o=oc, p0=pb0, q=qch, bb=b_, rbig=rbig:
                  e.tensor_mul(
                      afm[p0:p0 + 64,
                          o * T + bb * SEQ + q * 512: o * T + bb * SEQ + (q + 1) * 512],
                      b[0:64, :], rbig[:]),
                  ins=[pbA, rbig], outs=[f"l0afm{uu}_{h}_{tg}"])

    for b_ in range(BPC):
        P.dma("sp", vtmb[:], vd[:, b_ * NKT * 1040:(b_ + 1) * NKT * 1040])
        for h in range(16):
            head_body(b_, h)
    for B in BB:
        for k_ in ("qsb", "ksb", "sq", "nrm", "c0n", "c0big", "ptil", "rsum",
                   "rbig"):
            P.sfree(B[k_])
        for bb2 in B["pbS"] + [B["pbA"], B["pbrb"]]:
            P.pfree(bb2)
    P.sfree(vtmb)
    a_d = C.scratch_dram(f"l0ad_{uu}", [128, NF * T], BF16)
    P.dma("sp", a_d[:, :], afm[:],
          in_key=[f"l0afm{uu}_{h}_{t}" for h in range(16) for t in range(8)])
    P.sfree(afm)
    hfm = P.salloc(f"l0hf_{uu}", [128, NF * T], BF16)
    W1 = load_weight(C, "L0_attn_W1", D, 4096)
    Wo = load_weight(C, "L0_attn_Wo", D, D)
    bobig = emit_row_broadcast(C, "L0_attn_bo", D, uu)
    g1big = emit_row_broadcast(C, "L0_attn_g1", D, uu)
    b1big = emit_row_broadcast(C, "L0_attn_b1", D, uu)
    h_d = C.scratch_dram(f"l0hd_{uu}", [128, NT * D], BF16)
    afs = P.salloc(f"l0afs_{uu}", [128, NF * 512], BF16)
    a2s = P.salloc(f"l0a2_{uu}", [128, NF * 512], BF16)
    hpre = P.salloc(f"l0hp_{uu}", [128, D], F32)
    htile = P.salloc(f"l0htl_{uu}", [128, D], BF16)
    utr = P.salloc(f"l0utr_{uu}", [128, D], BF16)
    scr = {"s1": P.salloc(f"l0s1_{uu}", [128, 1], F32),
           "s2": P.salloc(f"l0s2_{uu}", [128, 1], F32),
           "s3": P.salloc(f"l0s3_{uu}", [128, 1], F32),
           "dump": P.salloc(f"l0du_{uu}", [128, D], BF16)}
    pbo = [P.palloc() for _ in range(2)]
    pbt = [P.palloc() for _ in range(2)]
    pbh = [P.palloc() for _ in range(2)]
    for tc in range(T // 512):
        for fc in range(NF):
            P.dma("sp", afs[:, fc * 512:(fc + 1) * 512],
                  a_d[:, fc * T + tc * 512: fc * T + (tc + 1) * 512])
        for oc in range(NF):
            bank = pbo[oc % 2]
            for fc in range(NF):
                P.add("pe", lambda e, f=fc, o=oc, b=bank,
                      s=(fc == 0), p=(fc == NF - 1):
                      e.matmul(b[:], wslice(Wo, D, f, o * 128, 128),
                               afs[:, f * 512:(f + 1) * 512], start=s, stop=p),
                      ins=[Wo, afs], outs=[bank])
            P.add("dve", lambda e, o=oc, b=bank: e.tensor_copy(
                out=a2s[:, o * 512:(o + 1) * 512], in_=b[:]),
                ins=[bank], outs=[a2s])
        for i in range(4):
            tt = tc * 4 + i
            bank = pbt[i % 2]
            bb = bank.ap().bitcast(BF16)
            for fc in range(NF):
                P.add("pe", lambda e, f=fc, i=i, b=bb: e.transpose(
                    b[:, f * 128:(f + 1) * 128],
                    a2s[:, f * 512 + i * 128: f * 512 + (i + 1) * 128],
                    C.ident_bf[:]), ins=[a2s, C.ident_bf], outs=[bank])
            P.dma("sp", utr[:], u_d[:, tt * D:(tt + 1) * D])
            P.add("dve", lambda e, b=bb: e.tensor_add(
                hpre[:], b[:, 0:D], utr[:]),
                ins=[bank, utr], outs=[hpre])
            P.add("dve", lambda e: e.tensor_add(hpre[:], hpre[:], bobig[:]),
                  ins=[hpre, bobig], outs=[hpre])
            emit_ln_affine_tile(C, hpre[:], hpre, htile[:], htile,
                                g1big, b1big, scr)
            P.dma("sp", h_d[:, tt * D:(tt + 1) * D], htile[:],
                  out_key=f"l0h_{uu}_{tt}")
            bank2 = pbh[i % 2]
            bh = bank2.ap().bitcast(BF16)
            for fc in range(NF):
                P.add("pe", lambda e, f=fc, b=bh: e.transpose(
                    b[:, f * 128:(f + 1) * 128], htile[:, f * 128:(f + 1) * 128],
                    C.ident_bf[:]), ins=[htile, C.ident_bf], outs=[bank2])
            ofm = hfm.ap().rearrange("p (c t) -> p c t", c=NF)[
                :, :, tt * 128:(tt + 1) * 128]
            ib2 = bh.rearrange("p (c t) -> p c t", c=NF)
            P.add("dve", lambda e, o=ofm, i2=ib2: e.tensor_copy(out=o, in_=i2),
                  ins=[bank2], outs=[hfm])
    for b in pbo + pbt + pbh:
        P.pfree(b)
    for t_ in (afs, a2s, hpre, htile, utr, Wo, bobig, g1big, b1big):
        P.sfree(t_)
    c1 = load_bias(C, "L0_attn_c1", HID)
    f1_d = C.scratch_dram(f"l0f1_{uu}", [HID, T], BF16)
    f1s = [P.salloc(f"l0f1s_{uu}_{i}", [128, 512], BF16) for i in range(2)]
    pb1 = [P.palloc() for _ in range(2)]
    for tc in range(T // 512):
        for oc in range(NH32):
            bank = pb1[oc % 2]
            st = f1s[oc % 2]
            for fc in range(NF):
                P.add("pe", lambda e, f=fc, o=oc, t=tc, b=bank,
                      s=(fc == 0), p=(fc == NF - 1):
                      e.matmul(b[:], wslice(W1, HID, f, o * 128, 128),
                               fmv(hfm, f, t * 512, 512), start=s, stop=p),
                      ins=[W1, hfm], outs=[bank])
            P.add("act", lambda e, b=bank, st=st, o=oc: e.activation(
                out=st[:], in_=b[:], func=AF.Gelu_apprx_tanh,
                bias=c1[:, o:o + 1]), ins=[bank, c1], outs=[st])
            P.dma("sp", f1_d[oc * 128:(oc + 1) * 128, tc * 512:(tc + 1) * 512],
                  st[:], out_key=f"l0f1_{uu}_{oc}_{tc}")
    for b in pb1:
        P.pfree(b)
    for t_ in f1s + [W1, c1, hfm]:
        P.sfree(t_)
    W2 = load_weight(C, "L0_attn_W2", HID, D)
    c2 = load_bias(C, "L0_attn_c2", D)
    g2big = emit_row_broadcast(C, "L0_attn_g2", D, uu)
    b2big = emit_row_broadcast(C, "L0_attn_b2", D, uu)
    f1a = [P.salloc(f"l0f1a_{uu}_{i}", [128, NH32 * 512], BF16)
           for i in range(2)]
    f2s = P.salloc(f"l0f2_{uu}", [128, NF * 512], BF16)
    hrt = P.salloc(f"l0hr_{uu}", [128, D], BF16)
    xrt = P.salloc(f"l0xr_{uu}", [128, D], F32)
    f2pre = P.salloc(f"l0fp_{uu}", [128, D], F32)
    outt = P.salloc(f"l0out_{uu}", [128, D], F32)
    scr2 = {"s1": P.salloc(f"l0s1b_{uu}", [128, 1], F32),
            "s2": P.salloc(f"l0s2b_{uu}", [128, 1], F32),
            "s3": P.salloc(f"l0s3b_{uu}", [128, 1], F32),
            "dump": P.salloc(f"l0dub_{uu}", [128, D], BF16)}
    pb2 = [P.palloc() for _ in range(2)]
    pbt2 = [P.palloc() for _ in range(2)]
    P.add("dve", lambda e: e.engine_nop(), ins=[], outs=[f1a[0], f1a[1]])
    for tc in range(T // 512):
        fa = f1a[tc % 2]
        for hc in range(NH32):
            P.dma("sp" if hc % 2 == 0 else "pool",
                  fa[:, hc * 512:(hc + 1) * 512],
                  f1_d[hc * 128:(hc + 1) * 128, tc * 512:(tc + 1) * 512],
                  ins=[fa],
                  out_key=f"l0f1a_{uu}_{tc % 2}_{hc}",
                  in_key=f"l0f1_{uu}_{hc}_{tc}")
        for oc in range(NF):
            bank = pb2[oc % 2]
            for hc in range(NH32):
                P.add("pe", lambda e, b=bank, fa=fa, o=oc, h2=hc,
                      s=(hc == 0), p=(hc == NH32 - 1):
                      e.matmul(b[:], wslice(W2, D, h2, o * 128, 128),
                               fa[:, h2 * 512:(h2 + 1) * 512], start=s, stop=p),
                      ins=[W2, f"l0f1a_{uu}_{tc % 2}_{hc}"], outs=[bank])
            P.add("dve", lambda e, o=oc, b=bank: e.tensor_scalar(
                out=f2s[:, o * 512:(o + 1) * 512], in0=b[:],
                scalar1=c2[:, o:o + 1], scalar2=None, op0=ALU.add),
                ins=[bank, c2], outs=[f2s])
        for i in range(4):
            tt = tc * 4 + i
            bank = pbt2[i % 2]
            bb = bank.ap().bitcast(BF16)
            for fc in range(NF):
                P.add("pe", lambda e, f=fc, i=i, b=bb: e.transpose(
                    b[:, f * 128:(f + 1) * 128],
                    f2s[:, f * 512 + i * 128: f * 512 + (i + 1) * 128],
                    C.ident_bf[:]), ins=[f2s, C.ident_bf], outs=[bank])
            P.dma("sp", hrt[:], h_d[:, tt * D:(tt + 1) * D],
                  in_key=f"l0h_{uu}_{tt}")
            P.add("dve", lambda e, b=bb: e.tensor_add(
                f2pre[:], b[:, 0:D], hrt[:]), ins=[bank, hrt], outs=[f2pre])
            emit_ln_affine_tile(C, f2pre[:], f2pre, outt[:], outt,
                                g2big, b2big, scr2)
            P.dma("sp", xrt[:], xin[tt * 128:(tt + 1) * 128, :])
            P.add("dve", lambda e: e.tensor_add(outt[:], outt[:], xrt[:]),
                  ins=[outt, xrt], outs=[outt])
            P.dma("sp", xout[tt * 128:(tt + 1) * 128, :], outt[:], tag=tag)
    P.add("dve", lambda e: e.engine_nop(),
          ins=[f"l0f1a_{uu}_{p}_{h}" for p in range(2) for h in range(NH32)],
          outs=[f1a[0], f1a[1]])
    for b in pb2 + pbt2:
        P.pfree(b)
    for t_ in f1a + [f2s, hrt, xrt, f2pre, outt, W2, c2, g2big, b2big]:
        P.sfree(t_)
    for t_ in scr.values():
        P.sfree(t_)
    for t_ in scr2.values():
        P.sfree(t_)


def emit_rms_fm(C, src_fm, dst_fm, wcol, uu, eps, tagn, lo_cb=None):
    P = C.P
    sq = P.salloc(f"rms_sq_{tagn}_{uu}", [128, 512], BF16)
    rrow = P.salloc(f"rms_r_{tagn}_{uu}", [1, T], F32)
    onesb = emit_ones_col_bf(C)
    bank = P.palloc()
    bankb = P.palloc()
    xf32 = P.salloc(f"rms_x_{tagn}_{uu}", [128, 512], F32) if lo_cb else None
    for tc in range(T // 512):
        for fc in range(NF):
            P.add("act", lambda e, f=fc, t=tc: e.activation(
                out=sq[:], in_=fmv(src_fm, f, t * 512, 512), func=AF.Square),
                ins=[src_fm], outs=[sq])
            P.add("pe", lambda e, b=bank, s=(fc == 0), p=(fc == NF - 1):
                  e.matmul(b[0:1, :], onesb[:], sq[:], start=s, stop=p),
                  ins=[onesb, sq], outs=[bank])
        P.add("dve", lambda e, t=tc, b=bank: e.tensor_copy(
            out=rrow[0:1, t * 512:(t + 1) * 512], in_=b[0:1, :]),
            ins=[bank], outs=[rrow])
    P.add("act", lambda e: e.activation(out=rrow[:], in_=rrow[:], func=AF.Ln,
                                        scale=constf(C, 1.0 / D)[0:1, :],
                                        bias=constf(C, eps)[0:1, :]),
          ins=[rrow, constf(C, 1.0 / D), constf(C, eps)], outs=[rrow])
    P.add("act", lambda e: e.activation(out=rrow[:], in_=rrow[:], func=AF.Exp,
                                        scale=constf(C, -0.5)[0:1, :]),
          ins=[rrow, constf(C, -0.5)], outs=[rrow])
    for tc in range(T // 512):
        P.add("pe", lambda e, t=tc, b=bankb: e.matmul(
            b[:], C.ones_row[:], rrow[0:1, t * 512:(t + 1) * 512],
            start=True, stop=True), ins=[C.ones_row, rrow], outs=[bankb])
        for fc in range(NF):
            if lo_cb is not None:
                P.add("dve", lambda e, f=fc, t=tc, b=bankb: e.scalar_tensor_tensor(
                    xf32[:], fmv(src_fm, f, t * 512, 512), wcol[:, f:f + 1],
                    b[:], ALU.mult, ALU.mult),
                    ins=[src_fm, wcol, bankb], outs=[xf32])
                P.add("dve", lambda e, f=fc, t=tc: e.tensor_copy(
                    out=fmv(dst_fm, f, t * 512, 512), in_=xf32[:]),
                    ins=[xf32], outs=[dst_fm])
                lo_cb(fc, tc, xf32)
            else:
                P.add("dve", lambda e, f=fc, t=tc, b=bankb: e.scalar_tensor_tensor(
                    fmv(dst_fm, f, t * 512, 512), fmv(src_fm, f, t * 512, 512),
                    wcol[:, f:f + 1], b[:], ALU.mult, ALU.mult),
                    ins=[src_fm, wcol, bankb], outs=[dst_fm])
    P.pfree(bank)
    P.pfree(bankb)
    P.sfree(sq)
    P.sfree(rrow)
    if xf32 is not None:
        P.sfree(xf32)


def emit_l2_ffn(C, xin, xout, tag=""):
    P = C.P
    uu = P.uid()
    NKT = SEQ // 128
    onesb = emit_ones_col_bf(C)
    u_d = C.scratch_dram(f"l2f_ud_{uu}", [T, D], F32)
    u = emit_outer_ln(C, xin, want_fm=True, u32_dram=u_d, tag=tag)
    ufm = u["u_fm"]
    n1c = load_bias(C, "L2_ffn_n1_w", D)
    hq = P.salloc(f"l2f_hq_{uu}", [128, NF * T], BF16)
    emit_rms_fm(C, ufm, hq, n1c, uu, 1.1920929e-07, "n1")
    P.sfree(n1c)
    Wkqv = load_weight(C, "L2_ffn_wkqv", D, 3072)
    qd, kd, vd = emit_qkv_spill(C, hq, Wkqv, uu, scale_q=1.0)
    P.sfree(hq)
    P.sfree(Wkqv)
    qn_c = P.salloc(f"l2f_qn_{uu}", [64, 1], F32)
    kn_c = P.salloc(f"l2f_kn_{uu}", [64, 1], F32)
    P.dma("sp", qn_c[:], C.win["L2_ffn_qn_w"].unsqueeze(1))
    P.dma("sp", kn_c[:], C.win["L2_ffn_kn_w"].unsqueeze(1))
    P.add("dve", lambda e: e.tensor_scalar_mul(qn_c[:], qn_c[:], 0.125),
          ins=[qn_c], outs=[qn_c])
    BB = []
    for par in range(2):
        BB.append(dict(
            qsb=P.salloc(f"l2f_q_{uu}_{par}", [64, SEQ], BF16),
            ksb=P.salloc(f"l2f_k_{uu}_{par}", [64, SEQ], BF16),
            sq=P.salloc(f"l2f_sq_{uu}_{par}", [64, 512], BF16),
            nrm=P.salloc(f"l2f_n_{uu}_{par}", [1, 16], F32),
            c0n=P.salloc(f"l2f_c0_{uu}_{par}", [1, 1], F32),
            c0big=P.salloc(f"l2f_c0b_{uu}_{par}", [128, 1], F32),
            ptil=P.salloc(f"l2f_p_{uu}_{par}", [128, (SEQ // 128) * 512], BF16),
            rsum=P.salloc(f"l2f_rs_{uu}_{par}", [1, 512], F32),
            rbig=P.salloc(f"l2f_rb_{uu}_{par}", [64, 512], F32),
            rrow=P.salloc(f"l2f_rr_{uu}_{par}", [1, SEQ], F32),
            pbS=[P.palloc() for _ in range(2)],
            pbA=P.palloc(),
            pbrb=P.palloc(),
        ))
    vtmb = P.salloc(f"l2f_v_{uu}", [128, NKT * 16 * 65], BF16)
    a_d = C.scratch_dram(f"l2f_ad_{uu}", [128, NF * T], BF16)
    astg = [P.salloc(f"l2f_astg_{uu}_{i}", [128, 512], BF16) for i in range(2)]

    def head_body(b_, h):
        B = BB[h % 2]
        qsb, ksb, sq, nrm = B["qsb"], B["ksb"], B["sq"], B["nrm"]
        c0n, c0big, ptil = B["c0n"], B["c0big"], B["ptil"]
        rsum, rbig, rrow = B["rsum"], B["rbig"], B["rrow"]
        pbS, pbA, pbrb = B["pbS"], B["pbA"], B["pbrb"]
        bank_n, bank_b = pbA, pbrb
        ast = astg[h % 2]
        P.dma("sp", qsb[:], qd[h * 64:(h + 1) * 64, b_ * SEQ:(b_ + 1) * SEQ])
        P.dma("sp", ksb[:], kd[h * 64:(h + 1) * 64, b_ * SEQ:(b_ + 1) * SEQ])
        for src_, wc in ((qsb, qn_c), (ksb, kn_c)):
            for tc in range(SEQ // 512):
                P.add("act", lambda e, s=src_, t=tc, sq=sq: e.activation(
                    out=sq[:], in_=s[:, t * 512:(t + 1) * 512], func=AF.Square),
                    ins=[src_], outs=[sq])
                P.add("pe", lambda e, bn=bank_n, sq=sq: e.matmul(
                    bn[0:1, :], onesb[0:64, :], sq[:], start=True, stop=True),
                    ins=[onesb, sq], outs=[bank_n])
                P.add("dve", lambda e, t=tc, bn=bank_n, rrow=rrow: e.tensor_copy(
                    out=rrow[0:1, t * 512:(t + 1) * 512], in_=bn[0:1, :]),
                    ins=[bank_n], outs=[rrow])
            P.add("act", lambda e, rrow=rrow: e.activation(
                out=rrow[:], in_=rrow[:], func=AF.Ln,
                scale=constf(C, 1.0 / 64.0)[0:1, :],
                bias=constf(C, 1.1920929e-07)[0:1, :]),
                ins=[rrow, constf(C, 1.0 / 64.0), constf(C, 1.1920929e-07)],
                outs=[rrow])
            P.add("act", lambda e, rrow=rrow: e.activation(
                out=rrow[:], in_=rrow[:], func=AF.Exp,
                scale=constf(C, -0.5)[0:1, :]),
                ins=[rrow, constf(C, -0.5)], outs=[rrow])
            for tc in range(SEQ // 512):
                P.add("pe", lambda e, t=tc, bb=bank_b, rrow=rrow: e.matmul(
                    bb[0:64, :], C.ones_row[0:1, 0:64],
                    rrow[0:1, t * 512:(t + 1) * 512], start=True, stop=True),
                    ins=[C.ones_row, rrow], outs=[bank_b])
                P.add("dve", lambda e, s=src_, t=tc, w=wc, bb=bank_b:
                      e.scalar_tensor_tensor(
                          s[:, t * 512:(t + 1) * 512],
                          s[:, t * 512:(t + 1) * 512], w[:], bb[0:64, :],
                          ALU.mult, ALU.mult),
                      ins=[src_, wc, bank_b], outs=[src_])
        for side, src_ in ((0, qsb), (1, ksb)):
            for tc in range(SEQ // 512):
                P.add("act", lambda e, s=src_, t=tc, sq=sq: e.activation(
                    out=sq[:], in_=s[:, t * 512:(t + 1) * 512], func=AF.Square),
                    ins=[src_], outs=[sq])
                P.add("pe", lambda e, bn=bank_n, sq=sq: e.matmul(
                    bn[0:1, :], onesb[0:64, :], sq[:], start=True, stop=True),
                    ins=[onesb, sq], outs=[bank_n])
                P.add("dve", lambda e, sd=side, t=tc, bn=bank_n, nrm=nrm:
                      e.reduce_max(
                          nrm[0:1, sd * 8 + t: sd * 8 + t + 1],
                          bn[0:1, :], axis=AX.X), ins=[bank_n], outs=[nrm])
        P.add("dve", lambda e, nrm=nrm, c0n=c0n: e.reduce_max(
            c0n[:], nrm[0:1, 0:SEQ // 512], axis=AX.X), ins=[nrm], outs=[c0n])
        P.add("dve", lambda e, nrm=nrm: e.reduce_max(
            nrm[0:1, 0:1], nrm[0:1, 8:8 + SEQ // 512], axis=AX.X),
            ins=[nrm], outs=[nrm])
        P.add("dve", lambda e, nrm=nrm, c0n=c0n: e.tensor_mul(
            c0n[:], c0n[:], nrm[0:1, 0:1]), ins=[c0n, nrm], outs=[c0n])
        P.add("act", lambda e, c0n=c0n: e.activation(
            out=c0n[:], in_=c0n[:], func=AF.Ln), ins=[c0n], outs=[c0n])
        P.add("act", lambda e, c0n=c0n: e.activation(
            out=c0n[:], in_=c0n[:], func=AF.Exp, scale=constf(C, 0.5)[0:1, :]),
            ins=[c0n, constf(C, 0.5)], outs=[c0n])
        P.add("dve", lambda e, c0n=c0n: e.tensor_scalar_mul(c0n[:], c0n[:], -1.0),
              ins=[c0n], outs=[c0n])
        P.add("pe", lambda e, bn=bank_n, c0n=c0n: e.matmul(
            bn[:, 0:1], C.ones_row[:], c0n[:], start=True, stop=True),
            ins=[C.ones_row, c0n], outs=[bank_n])
        P.add("dve", lambda e, bn=bank_n, c0big=c0big: e.tensor_copy(
            out=c0big[:], in_=bn[:, 0:1]), ins=[bank_n], outs=[c0big])
        for qch in range(SEQ // 512):
            ktmax = qch * 4 + 4
            for kt in range(ktmax):
                bank = pbS[kt % 2]
                P.add("pe", lambda e, k=kt, q=qch, b=bank, qsb=qsb, ksb=ksb:
                      e.matmul(b[:], ksb[:, k * 128:(k + 1) * 128],
                               qsb[:, q * 512:(q + 1) * 512],
                               start=True, stop=True),
                      ins=[ksb, qsb], outs=[bank])
                P.add("act", lambda e, b=bank, k=kt, ptil=ptil, c0big=c0big:
                      e.activation(
                          out=ptil[:, k * 512:(k + 1) * 512], in_=b[:],
                          func=AF.Exp, bias=c0big[:]),
                      ins=[bank, c0big], outs=[ptil])
                if kt >= qch * 4:
                    P.add("pool", lambda g, k=kt, q=qch, ptil=ptil:
                          g.affine_select(
                              out=ptil[:, k * 512:(k + 1) * 512],
                              in_=ptil[:, k * 512:(k + 1) * 512],
                              pattern=[[1, 512]], compare_op=ALU.is_ge, fill=0.0,
                              base=q * 512 - k * 128, channel_multiplier=-1),
                          ins=[ptil], outs=[ptil])
            for kt in range(ktmax):
                P.add("pe", lambda e, k=kt, h=h, b=pbA, ptil=ptil,
                      s=(kt == 0), p=(kt == ktmax - 1):
                      e.matmul(b[0:65, :],
                               vtmb[:, k * 1040 + h * 65: k * 1040 + (h + 1) * 65],
                               ptil[:, k * 512:(k + 1) * 512], start=s, stop=p),
                      ins=[vtmb, ptil], outs=[pbA])
            P.add("dve", lambda e, b=pbA, rsum=rsum: e.tensor_copy(
                out=rsum[:], in_=b[64:65, :]), ins=[pbA], outs=[rsum])
            P.add("dve", lambda e, rsum=rsum: e.reciprocal(rsum[:], rsum[:]),
                  ins=[rsum], outs=[rsum])
            P.add("pe", lambda e, pb_=pbrb, rsum=rsum: e.matmul(
                pb_[0:64, :], C.ones_row[0:1, 0:64], rsum[:],
                start=True, stop=True), ins=[C.ones_row, rsum], outs=[pbrb])
            P.add("dve", lambda e, pb_=pbrb, rbig=rbig: e.tensor_copy(
                out=rbig[:], in_=pb_[0:64, :]), ins=[pbrb], outs=[rbig])
            oc = h // 2
            pb0 = (h % 2) * 64
            P.add("dve", lambda e, b=pbA, rbig=rbig, ast=ast: e.tensor_mul(
                ast[0:64, :], b[0:64, :], rbig[:]),
                ins=[pbA, rbig], outs=[ast])
            P.dma("sp",
                  a_d[pb0:pb0 + 64,
                      oc * T + b_ * SEQ + qch * 512:
                      oc * T + b_ * SEQ + (qch + 1) * 512],
                  ast[0:64, :],
                  out_key=f"l2fad_{uu}_{oc}_{pb0}_{b_ * 4 + qch}")

    for b_ in range(BPC):
        P.dma("sp", vtmb[:], vd[:, b_ * NKT * 1040:(b_ + 1) * NKT * 1040])
        for h in range(16):
            head_body(b_, h)
    for B in BB:
        for k_ in ("qsb", "ksb", "sq", "nrm", "c0n", "c0big", "ptil", "rsum",
                   "rbig", "rrow"):
            P.sfree(B[k_])
        for bb2 in B["pbS"] + [B["pbA"], B["pbrb"]]:
            P.pfree(bb2)
    for t_ in (qn_c, kn_c, vtmb):
        P.sfree(t_)
    for t_ in astg:
        P.sfree(t_)
    Wo = load_weight(C, "L2_ffn_wo", D, D)
    ao_d = C.scratch_dram(f"l2f_ao_{uu}", [128, NF * T], BF16)
    aos = P.salloc(f"l2f_aos_{uu}", [128, 512], BF16)
    afs = P.salloc(f"l2f_afs_{uu}", [128, NF * 512], BF16)
    pbo = [P.palloc() for _ in range(2)]
    for tc in range(T // 512):
        for fc in range(NF):
            P.dma("sp", afs[:, fc * 512:(fc + 1) * 512],
                  a_d[:, fc * T + tc * 512: fc * T + (tc + 1) * 512],
                  in_key=[f"l2fad_{uu}_{fc}_0_{tc}", f"l2fad_{uu}_{fc}_64_{tc}"])
        for oc in range(2, NF):
            bank = pbo[oc % 2]
            for fc in range(NF):
                P.add("pe", lambda e, f=fc, o=oc, t=tc, b=bank,
                      s=(fc == 0), p=(fc == NF - 1):
                      e.matmul(b[:], wslice(Wo, D, f, o * 128, 128),
                               afs[:, f * 512:(f + 1) * 512], start=s, stop=p),
                      ins=[Wo, afs], outs=[bank])
            P.add("dve", lambda e, b=bank: e.tensor_copy(out=aos[:], in_=b[:]),
                  ins=[bank], outs=[aos])
            P.dma("sp", ao_d[:, oc * T + tc * 512: oc * T + (tc + 1) * 512],
                  aos[:], out_key=f"l2fao_{uu}_{oc}_{tc}")
            P.add("dve", lambda e, o=oc, t=tc, b=bank: e.tensor_add(
                fmv(ufm, o, t * 512, 512), fmv(ufm, o, t * 512, 512), b[:]),
                ins=[ufm, bank], outs=[ufm])
    for b in pbo:
        P.pfree(b)
    P.sfree(aos)
    P.sfree(afs)
    P.sfree(Wo)
    n2c = load_bias(C, "L2_ffn_n2_w", D)
    Gf = P.salloc(f"l2f_gf_{uu}", [128, NF * 4], F32)
    Ghi = P.salloc(f"l2f_gh_{uu}", [128, NF * 4], BF16)
    Glo = P.salloc(f"l2f_gl_{uu}", [128, NF * 4], BF16)
    P.dma("sp", Gf[:].rearrange("p (c o) -> p c o", c=NF),
          C.win["L2_ffn_gate_w"].rearrange("(c p) o -> p c o", p=128))
    P.dma("pool", Ghi[:].rearrange("p (c o) -> p c o", c=NF),
          C.win["L2_ffn_gate_w"].rearrange("(c p) o -> p c o", p=128))
    P.add("dve", lambda e: e.tensor_sub(Glo[:], Gf[:], Ghi[:]),
          ins=[Gf, Ghi], outs=[Glo])
    xlo = P.salloc(f"l2f_xlo_{uu}", [128, 512], BF16)
    lta = P.salloc(f"l2f_lta_{uu}", [128, NT * 4], F32)
    pbg = [P.palloc() for _ in range(4)]

    def gate_cb(fc, tc, xf32):
        P.add("dve", lambda e: e.tensor_sub(xlo[:], xf32[:],
                                            fmv(ufm, fc, tc * 512, 512)),
              ins=[xf32, ufm], outs=[xlo])
        for i in range(4):
            bank = pbg[i]
            for j, (xa, xt, gw) in enumerate((
                    (fmv(ufm, fc, tc * 512 + i * 128, 128), ufm, Ghi),
                    (fmv(ufm, fc, tc * 512 + i * 128, 128), ufm, Glo),
                    (xlo[:, i * 128:(i + 1) * 128], xlo, Ghi))):
                P.add("pe", lambda e, xa=xa, gw=gw, f=fc, b=bank,
                      s=(fc == 0 and j == 0), p=(fc == NF - 1 and j == 2):
                      e.matmul(b[:, 0:4], xa, wslice(gw, 4, f, 0, 4),
                               start=s, stop=p),
                      ins=[xt, gw], outs=[bank])
            if fc == NF - 1:
                tt = tc * 4 + i
                P.add("dve", lambda e, b=bank, t=tt: e.tensor_copy(
                    out=lta[:, t * 4:(t + 1) * 4], in_=b[:, 0:4]),
                    ins=[bank], outs=[lta])

    emit_rms_fm(C, ufm, ufm, n2c, uu, 1.1920929e-07, "n2", lo_cb=gate_cb)
    xf = ufm
    P.sfree(n2c)
    for t_ in (Gf, Ghi, Glo, xlo):
        P.sfree(t_)
    wta = P.salloc(f"l2f_wa_{uu}", [128, NT * 4], F32)
    ltm = P.salloc(f"l2f_lt_{uu}", [128, 4], F32)
    m1 = P.salloc(f"l2f_m1_{uu}", [128, 1], F32)
    m2 = P.salloc(f"l2f_m2_{uu}", [128, 1], F32)
    den = P.salloc(f"l2f_dn_{uu}", [128, 1], F32)
    msk = P.salloc(f"l2f_mk_{uu}", [128, 4], F32)
    for tt in range(NT):
        P.add("dve", lambda e, t=tt: e.reduce_max(
            m1[:], lta[:, t * 4:(t + 1) * 4], axis=AX.X, negate=True),
            ins=[lta], outs=[m1])
        P.add("act", lambda e, t=tt: e.activation(
            out=ltm[:], in_=lta[:, t * 4:(t + 1) * 4], func=AF.Exp, bias=m1[:],
            accum_out=den[:]), ins=[lta, m1], outs=[ltm, den])
        P.add("dve", lambda e: e.reciprocal(den[:], den[:]), ins=[den], outs=[den])
        P.add("dve", lambda e: e.tensor_scalar_mul(ltm[:], ltm[:], den[:]),
              ins=[ltm, den], outs=[ltm])
        P.add("dve", lambda e: e.reduce_max(m1[:], ltm[:], axis=AX.X),
              ins=[ltm], outs=[m1])
        P.add("dve", lambda e: e.tensor_scalar(
            out=msk[:], in0=ltm[:], scalar1=m1[:], scalar2=-1e9,
            op0=ALU.is_ge, op1=ALU.mult), ins=[ltm, m1], outs=[msk])
        P.add("dve", lambda e: e.tensor_add(msk[:], msk[:], ltm[:]),
              ins=[msk, ltm], outs=[msk])
        P.add("dve", lambda e: e.reduce_max(m2[:], msk[:], axis=AX.X),
              ins=[msk], outs=[m2])
        P.add("dve", lambda e: e.tensor_add(den[:], m1[:], m2[:]),
              ins=[m1, m2], outs=[den])
        P.add("dve", lambda e: e.tensor_scalar_add(den[:], den[:], 1e-8),
              ins=[den], outs=[den])
        P.add("dve", lambda e: e.reciprocal(den[:], den[:]), ins=[den], outs=[den])
        P.add("dve", lambda e, t=tt: e.tensor_scalar(
            out=wta[:, t * 4:(t + 1) * 4], in0=ltm[:], scalar1=m2[:],
            scalar2=None, op0=ALU.is_ge), ins=[ltm, m2], outs=[wta])
        P.add("dve", lambda e, t=tt: e.tensor_mul(
            wta[:, t * 4:(t + 1) * 4], wta[:, t * 4:(t + 1) * 4], ltm[:]),
            ins=[wta, ltm], outs=[wta])
        P.add("dve", lambda e, t=tt: e.tensor_scalar_mul(
            wta[:, t * 4:(t + 1) * 4], wta[:, t * 4:(t + 1) * 4], den[:]),
            ins=[wta, den], outs=[wta])
    for b in pbg:
        P.pfree(b)
    for t_ in (ltm, m1, m2, den, msk, lta):
        P.sfree(t_)
    moe_d = C.scratch_dram(f"l2f_moe_{uu}", [D, T], F32)
    wexp = P.salloc(f"l2f_we_{uu}", [128, 128], F32)
    wbig = P.salloc(f"l2f_wb_{uu}", [128, 512], F32)
    stg = [P.salloc(f"l2f_st_{uu}_{i}", [128, 512], F32) for i in range(2)]
    pbw = P.palloc()

    def emit_wbig(e_, tc):
        for i in range(4):
            tt = tc * 4 + i
            eap = AP(wta, tt * 4 + e_, [[NT * 4, 128], [0, 128]])
            P.add("dve", lambda e, a=eap: e.tensor_copy(out=wexp[:], in_=a),
                  ins=[wta], outs=[wexp])
            P.add("pe", lambda e, i=i, b=pbw: e.transpose(
                b[:, i * 128:(i + 1) * 128], wexp[:], C.ident_f32[:]),
                ins=[wexp, C.ident_f32], outs=[pbw])
        P.add("dve", lambda e, b=pbw: e.tensor_copy(out=wbig[:], in_=b[:]),
              ins=[pbw], outs=[wbig])

    def expert_out_evac(bank, oc, tc, e_, first):
        st = stg[oc % 2]
        P.add("dve", lambda e, b=bank, st=st: e.tensor_mul(
            st[:], b[:], wbig[:]), ins=[bank, wbig], outs=[st])
        kw = {} if first else {"accum_op": ALU.add}
        P.dma("pool", moe_d[oc * 128:(oc + 1) * 128, tc * 512:(tc + 1) * 512],
              st[:], out_key=f"l2fmoe_{uu}_{oc}_{tc}", **kw)

    pbx = [P.palloc() for _ in range(2)]
    e0u = load_weight(C, "L2_ffn_e0_up", D, 2048)
    e0d = load_weight(C, "L2_ffn_e0_down", 2048, D)
    h1 = P.salloc(f"l2f_h1_{uu}", [128, 16 * 512], BF16)
    for tc in range(T // 512):
        emit_wbig(0, tc)
        for oc in range(16):
            bank = pbx[oc % 2]
            for fc in range(NF):
                P.add("pe", lambda e, f=fc, o=oc, t=tc, b=bank,
                      s=(fc == 0), p=(fc == NF - 1):
                      e.matmul(b[:], wslice(e0u, 2048, f, o * 128, 128),
                               fmv(xf, f, t * 512, 512), start=s, stop=p),
                      ins=[e0u, xf], outs=[bank])
            P.add("act", lambda e, o=oc, b=bank: e.activation(
                out=h1[:, o * 512:(o + 1) * 512], in_=b[:], func=AF.Gelu),
                ins=[bank], outs=[h1])
        for oc in range(2, NF):
            bank = pbx[oc % 2]
            for hc in range(16):
                P.add("pe", lambda e, h2=hc, o=oc, b=bank,
                      s=(hc == 0), p=(hc == 15):
                      e.matmul(b[:], wslice(e0d, D, h2, o * 128, 128),
                               h1[:, h2 * 512:(h2 + 1) * 512], start=s, stop=p),
                      ins=[e0d, h1], outs=[bank])
            expert_out_evac(bank, oc, tc, 0, True)
    P.sfree(e0u)
    P.sfree(e0d)
    e1a = load_weight(C, "L2_ffn_e1_w1", D, 2048)
    e1b = load_weight(C, "L2_ffn_e1_w2", D, 2048)
    for tc in range(T // 512):
        emit_wbig(1, tc)
        for oc in range(16):
            bank = pbx[0]
            bank2 = pbx[1]
            for fc in range(NF):
                P.add("pe", lambda e, f=fc, o=oc, t=tc, b=bank,
                      s=(fc == 0), p=(fc == NF - 1):
                      e.matmul(b[:], wslice(e1a, 2048, f, o * 128, 128),
                               fmv(xf, f, t * 512, 512), start=s, stop=p),
                      ins=[e1a, xf], outs=[bank])
            for fc in range(NF):
                P.add("pe", lambda e, f=fc, o=oc, t=tc, b=bank2,
                      s=(fc == 0), p=(fc == NF - 1):
                      e.matmul(b[:], wslice(e1b, 2048, f, o * 128, 128),
                               fmv(xf, f, t * 512, 512), start=s, stop=p),
                      ins=[e1b, xf], outs=[bank2])
            P.add("act", lambda e, o=oc, b=bank: e.activation(
                out=h1[:, o * 512:(o + 1) * 512], in_=b[:], func=AF.Silu),
                ins=[bank], outs=[h1])
            P.add("dve", lambda e, o=oc, b=bank2: e.tensor_mul(
                h1[:, o * 512:(o + 1) * 512], h1[:, o * 512:(o + 1) * 512],
                b[:]), ins=[h1, bank2], outs=[h1])
        if tc == 0:
            e1d = load_weight(C, "L2_ffn_e1_down", 2048, D)
        for oc in range(2, NF):
            bank = pbx[oc % 2]
            for hc in range(16):
                P.add("pe", lambda e, h2=hc, o=oc, b=bank,
                      s=(hc == 0), p=(hc == 15):
                      e.matmul(b[:], wslice(e1d, D, h2, o * 128, 128),
                               h1[:, h2 * 512:(h2 + 1) * 512], start=s, stop=p),
                      ins=[e1d, h1], outs=[bank])
            expert_out_evac(bank, oc, tc, 1, False)
    P.sfree(e1a)
    P.sfree(e1b)
    P.sfree(e1d)
    m_d = C.scratch_dram(f"l2f_m_{uu}", [2048, T], BF16)
    m2_d = C.scratch_dram(f"l2f_m2_{uu}", [2048, T], BF16)
    e2w = load_weight(C, "L2_ffn_e2_l1", D, 2048)
    for tc in range(T // 512):
        for oc in range(16):
            bank = pbx[oc % 2]
            for fc in range(NF):
                P.add("pe", lambda e, f=fc, o=oc, t=tc, b=bank, w2=e2w,
                      s=(fc == 0), p=(fc == NF - 1):
                      e.matmul(b[:], wslice(w2, 2048, f, o * 128, 128),
                               fmv(xf, f, t * 512, 512), start=s, stop=p),
                      ins=[e2w, xf], outs=[bank])
            P.add("act", lambda e, o=oc, b=bank: e.activation(
                out=h1[:, o * 512:(o + 1) * 512], in_=b[:], func=AF.Gelu),
                ins=[bank], outs=[h1])
            P.dma("sp", m_d[oc * 128:(oc + 1) * 128, tc * 512:(tc + 1) * 512],
                  h1[:, oc * 512:(oc + 1) * 512], out_key=f"l2fm_{uu}_{oc}_{tc}")
    P.sfree(e2w)
    for stage, (win_, wn, src_d, src_key, dst_d, dst_key) in enumerate((
            ("L2_ffn_e2_l2", 2048, m_d, "l2fm", m2_d, "l2fm2"),
            ("L2_ffn_e2_l3", 2048, m2_d, "l2fm2", m_d, "l2fm"))):
        e2w = load_weight(C, win_, 2048, 2048)
        h1r = [P.salloc(f"l2f_h1r_{uu}_{stage}_{i}", [128, 512], BF16)
               for i in range(2)]
        for tc in range(T // 512):
            for oc in range(16):
                bank = pbx[oc % 2]
                for hc in range(16):
                    st = h1r[hc % 2]
                    if oc == 0:
                        P.dma("sp", st[:],
                              src_d[hc * 128:(hc + 1) * 128,
                                    tc * 512:(tc + 1) * 512],
                              in_key=f"{src_key}_{uu}_{hc}_{tc}")
                        P.add("dve", lambda e, st=st, h2=hc: e.tensor_copy(
                            out=h1[:, h2 * 512:(h2 + 1) * 512], in_=st[:]),
                            ins=[st], outs=[h1])
                    P.add("pe", lambda e, h2=hc, o=oc, b=bank, w2=e2w,
                          s=(hc == 0), p=(hc == 15):
                          e.matmul(b[:], wslice(w2, 2048, h2, o * 128, 128),
                                   h1[:, h2 * 512:(h2 + 1) * 512],
                                   start=s, stop=p),
                          ins=[e2w, h1], outs=[bank])
                P.add("act", lambda e, o=oc, b=bank, st=h1r[oc % 2]: e.activation(
                    out=st[:], in_=b[:], func=AF.Gelu), ins=[bank],
                    outs=[h1r[oc % 2]])
                P.dma("sp", dst_d[oc * 128:(oc + 1) * 128,
                                  tc * 512:(tc + 1) * 512],
                      h1r[oc % 2][:], out_key=f"{dst_key}_{uu}_{oc}_{tc}")
        P.sfree(e2w)
        for t_ in h1r:
            P.sfree(t_)
    e2w = load_weight(C, "L2_ffn_e2_l4", 2048, D)
    h1r = [P.salloc(f"l2f_h1r4_{uu}_{i}", [128, 512], BF16) for i in range(2)]
    for tc in range(T // 512):
        emit_wbig(2, tc)
        for hc in range(16):
            st = h1r[hc % 2]
            P.dma("sp", st[:], m_d[hc * 128:(hc + 1) * 128,
                                   tc * 512:(tc + 1) * 512],
                  in_key=f"l2fm_{uu}_{hc}_{tc}")
            P.add("dve", lambda e, st=st, h2=hc: e.tensor_copy(
                out=h1[:, h2 * 512:(h2 + 1) * 512], in_=st[:]),
                ins=[st], outs=[h1])
        for oc in range(2, NF):
            bank = pbx[oc % 2]
            for hc in range(16):
                P.add("pe", lambda e, h2=hc, o=oc, b=bank, w2=e2w,
                      s=(hc == 0), p=(hc == 15):
                      e.matmul(b[:], wslice(w2, D, h2, o * 128, 128),
                               h1[:, h2 * 512:(h2 + 1) * 512], start=s, stop=p),
                      ins=[e2w, h1], outs=[bank])
            expert_out_evac(bank, oc, tc, 2, False)
    P.sfree(e2w)
    for t_ in h1r:
        P.sfree(t_)
    e3d = load_weight(C, "L2_ffn_e3_down", D, D)
    e3u = load_weight(C, "L2_ffn_e3_up", D, 2048)
    e3o = load_weight(C, "L2_ffn_e3_out", 2048, D)
    t1 = P.salloc(f"l2f_t1_{uu}", [128, NF * 512], BF16)
    for tc in range(T // 512):
        emit_wbig(3, tc)
        for oc in range(NF):
            bank = pbx[oc % 2]
            for fc in range(NF):
                P.add("pe", lambda e, f=fc, o=oc, t=tc, b=bank,
                      s=(fc == 0), p=(fc == NF - 1):
                      e.matmul(b[:], wslice(e3d, D, f, o * 128, 128),
                               fmv(xf, f, t * 512, 512), start=s, stop=p),
                      ins=[e3d, xf], outs=[bank])
            P.add("act", lambda e, o=oc, b=bank: e.activation(
                out=t1[:, o * 512:(o + 1) * 512], in_=b[:], func=AF.Gelu),
                ins=[bank], outs=[t1])
        for oc in range(16):
            bank = pbx[oc % 2]
            for fc in range(NF):
                P.add("pe", lambda e, f=fc, o=oc, b=bank,
                      s=(fc == 0), p=(fc == NF - 1):
                      e.matmul(b[:], wslice(e3u, 2048, f, o * 128, 128),
                               t1[:, f * 512:(f + 1) * 512], start=s, stop=p),
                      ins=[e3u, t1], outs=[bank])
            P.add("act", lambda e, o=oc, b=bank: e.activation(
                out=h1[:, o * 512:(o + 1) * 512], in_=b[:], func=AF.Gelu),
                ins=[bank], outs=[h1])
        for oc in range(2, NF):
            bank = pbx[oc % 2]
            for hc in range(16):
                P.add("pe", lambda e, h2=hc, o=oc, b=bank,
                      s=(hc == 0), p=(hc == 15):
                      e.matmul(b[:], wslice(e3o, D, h2, o * 128, 128),
                               h1[:, h2 * 512:(h2 + 1) * 512], start=s, stop=p),
                      ins=[e3o, h1], outs=[bank])
            expert_out_evac(bank, oc, tc, 3, False)
    for t_ in (e3d, e3u, e3o, t1, h1, wexp, wbig, wta, xf):
        P.sfree(t_)
    for t_ in stg:
        P.sfree(t_)
    for b in pbx + [pbw]:
        P.pfree(b)
    xr = P.salloc(f"l2f_xr_{uu}", [128, D], F32)
    ur = P.salloc(f"l2f_ur_{uu}", [128, D], F32)
    aor = P.salloc(f"l2f_aor_{uu}", [128, 6 * 128], BF16)
    mor = P.salloc(f"l2f_mor_{uu}", [128, 6 * 128], F32)
    ot = P.salloc(f"l2f_ot_{uu}", [128, D], F32)
    pbf = [P.palloc() for _ in range(2)]
    pbab = P.palloc()
    pbab_bf = pbab.ap().bitcast(BF16)
    for tt in range(NT):
        P.dma("sp", xr[:], xin[tt * 128:(tt + 1) * 128, :])
        P.dma("sp", ur[:], u_d[tt * 128:(tt + 1) * 128, :])
        for j, oc in enumerate(range(2, NF)):
            P.dma("sp", aor[:, j * 128:(j + 1) * 128],
                  ao_d[:, oc * T + tt * 128: oc * T + (tt + 1) * 128],
                  in_key=f"l2fao_{uu}_{oc}_{tt // 4}")
            P.dma("sp", mor[:, j * 128:(j + 1) * 128],
                  moe_d[oc * 128:(oc + 1) * 128, tt * 128:(tt + 1) * 128],
                  in_key=f"l2fmoe_{uu}_{oc}_{tt // 4}")
        for j in range(6):
            bank = pbf[j // 4]
            off = (j % 4) * 128
            P.add("pe", lambda e, j=j, b=bank, o=off: e.transpose(
                b[:, o:o + 128], mor[:, j * 128:(j + 1) * 128], C.ident_f32[:]),
                ins=[mor, C.ident_f32], outs=[bank])
            P.add("pe", lambda e, j=j: e.transpose(
                pbab_bf[:, j * 128:(j + 1) * 128], aor[:, j * 128:(j + 1) * 128],
                C.ident_bf[:]), ins=[aor, C.ident_bf], outs=[pbab])
        P.add("dve", lambda e: e.tensor_add(ot[:], xr[:], ur[:]),
              ins=[xr, ur], outs=[ot])
        P.add("dve", lambda e: e.tensor_add(ot[:, 256:768], ot[:, 256:768],
                                            pbf[0][:]),
              ins=[ot, pbf[0]], outs=[ot])
        P.add("dve", lambda e: e.tensor_add(ot[:, 768:1024], ot[:, 768:1024],
                                            pbf[1][:, 0:256]),
              ins=[ot, pbf[1]], outs=[ot])
        P.add("dve", lambda e: e.tensor_add(ot[:, 256:1024], ot[:, 256:1024],
                                            pbab_bf[:, 0:768]),
              ins=[ot, pbab], outs=[ot])
        P.dma("sp", xout[tt * 128:(tt + 1) * 128, :], ot[:], tag=tag)
    for b in pbf + [pbab]:
        P.pfree(b)
    for t_ in (xr, ur, aor, mor, ot):
        P.sfree(t_)


def emit_sqrt11(C, dst_ap, dst_t, src_ap, src_t):
    P = C.P
    P.add("act", lambda e: e.activation(out=dst_ap, in_=src_ap, func=AF.Ln),
          ins=[src_t], outs=[dst_t])
    P.add("act", lambda e: e.activation(out=dst_ap, in_=dst_ap, func=AF.Exp,
                                        scale=constf(C, 0.5)[0:1, :]),
          ins=[dst_t, constf(C, 0.5)], outs=[dst_t])


def emit_l3_attn(C, xin, xout, tag=""):
    P = C.P
    uu = P.uid()
    NKT = SEQ // 128
    JSPAN = 2304
    onesb = emit_ones_col_bf(C)
    u = emit_outer_ln(C, xin, want_fm=True, tag=tag)
    ufm = u["u_fm"]
    Wkqv = load_weight(C, "L3_attn_wkqv", D, 3072)
    bq = load_bias(C, "L3_attn_bq", D)
    bk = load_bias(C, "L3_attn_bk", D)
    P.add("dve", lambda e: e.tensor_scalar_mul(bq[:], bq[:], 0.125),
          ins=[bq], outs=[bq])
    bvbig32 = emit_row_broadcast(C, "L3_attn_bv", D, uu)
    bvbig = P.salloc(f"l3a_bvb_{uu}", [128, D], BF16)
    P.add("dve", lambda e: e.tensor_copy(out=bvbig[:], in_=bvbig32[:]),
          ins=[bvbig32], outs=[bvbig])
    P.sfree(bvbig32)
    qd, kd, vd = emit_qkv_spill(C, ufm, Wkqv, uu, bq=bq, bk=bk, bvbig=bvbig)
    P.sfree(ufm)
    P.sfree(Wkqv)
    for t_ in (bq, bk, bvbig):
        P.sfree(t_)
    peRv = P.salloc(f"l3a_per_{uu}", [64, 4224], BF16)
    peK = P.salloc(f"l3a_pek_{uu}", [64, 4224], BF16)
    P.dma("pool", peRv[:], C.win["L3_attn_peT_rev"])
    P.dma("pool", peK[:], C.win["L3_attn_peT"])
    P.add("dve", lambda e: e.tensor_scalar_mul(peK[:], peK[:], 0.125),
          ins=[peK], outs=[peK])
    sqpe = P.salloc(f"l3a_sqp_{uu}", [64, 512], BF16)
    pn = P.salloc(f"l3a_pn_{uu}", [1, 4096], F32)
    mpe2 = P.salloc(f"l3a_mpe_{uu}", [1, 1], F32)
    bank = P.palloc()
    for c0_ in range(0, 4096, 512):
        P.add("act", lambda e, c=c0_: e.activation(
            out=sqpe[:], in_=peRv[:, c:c + 512], func=AF.Square),
            ins=[peRv], outs=[sqpe])
        P.add("pe", lambda e, b=bank: e.matmul(b[0:1, :], onesb[0:64, :],
                                               sqpe[:], start=True, stop=True),
              ins=[onesb, sqpe], outs=[bank])
        P.add("dve", lambda e, c=c0_, b=bank: e.tensor_copy(
            out=pn[0:1, c:c + 512], in_=b[0:1, :]), ins=[bank], outs=[pn])
    P.add("dve", lambda e: e.reduce_max(mpe2[:], pn[0:1, 0:4095], axis=AX.X),
          ins=[pn], outs=[mpe2])
    P.pfree(bank)
    P.sfree(sqpe)
    P.sfree(pn)
    R_d = [C.scratch_dram(f"l3a_rq_{uu}_{i}", [SEQ, JSPAN], BF16)
           for i in range(2)]
    KR_d = [C.scratch_dram(f"l3a_kr_{uu}_{i}", [SEQ, JSPAN], BF16)
            for i in range(2)]
    BB = []
    for par in range(2):
        BB.append(dict(
            qsb=P.salloc(f"l3a_q_{uu}_{par}", [64, SEQ], BF16),
            ksb=P.salloc(f"l3a_k_{uu}_{par}", [64, SEQ], BF16),
            sq=P.salloc(f"l3a_sq_{uu}_{par}", [64, 512], BF16),
            nrm=P.salloc(f"l3a_n_{uu}_{par}", [1, 16], F32),
            sA=P.salloc(f"l3a_sa_{uu}_{par}", [1, 1], F32),
            sB=P.salloc(f"l3a_sb_{uu}_{par}", [1, 1], F32),
            sC=P.salloc(f"l3a_sc_{uu}_{par}", [1, 4], F32),
            c0n=P.salloc(f"l3a_c0_{uu}_{par}", [1, 1], F32),
            c0big=P.salloc(f"l3a_c0b_{uu}_{par}", [128, 1], F32),
            ptil=P.salloc(f"l3a_p_{uu}_{par}", [128, 2 * 512], BF16),
            rsum=P.salloc(f"l3a_rs_{uu}_{par}", [1, 512], F32),
            rbig=P.salloc(f"l3a_rb_{uu}_{par}", [64, 512], F32),

            rk_sb=[P.salloc(f"l3a_rks_{uu}_{par}_{i}", [128, 512], BF16)
                   for i in range(4)],
            pbS=[P.palloc() for _ in range(2)],
            pbA=P.palloc(),
            pbrb=P.palloc(),
        ))
    vtmb = P.salloc(f"l3a_v_{uu}", [128, NKT * 16 * 65], BF16)
    afm = P.salloc(f"l3a_a_{uu}", [128, NF * T], BF16)
    rstg_sh = [P.salloc(f"l3a_rst_{uu}_{i}", [128, JSPAN], BF16)
               for i in range(2)]
    rq_sh = P.salloc(f"l3a_rqs_{uu}", [128, 4 * SEQ], BF16)

    def head_body(b_, h):
        B = BB[h % 2]
        par = h % 2
        qsb, ksb, sq, nrm = B["qsb"], B["ksb"], B["sq"], B["nrm"]
        sA, sB, sC, c0n, c0big = B["sA"], B["sB"], B["sC"], B["c0n"], B["c0big"]
        ptil, rsum, rbig = B["ptil"], B["rsum"], B["rbig"]
        rq_sb, rk_sb = rq_sh, B["rk_sb"]
        pbS, pbA, pbrb = B["pbS"], B["pbA"], B["pbrb"]
        bank_n = pbA
        P.dma("sp", qsb[:], qd[h * 64:(h + 1) * 64, b_ * SEQ:(b_ + 1) * SEQ])
        P.dma("sp", ksb[:], kd[h * 64:(h + 1) * 64, b_ * SEQ:(b_ + 1) * SEQ])
        for qt in range(NKT):
            rstg = rstg_sh[qt % 2]
            for jc in range(5):
                w = 512 if jc < 4 else JSPAN - 4 * 512
                bk2 = pbS[jc % 2]
                P.add("pe", lambda e, q=qt, j=jc, w=w, b=bk2, qsb=qsb: e.matmul(
                    b[:, 0:w], qsb[:, q * 128:(q + 1) * 128],
                    peRv[:, 1920 - q * 128 + j * 512:
                         1920 - q * 128 + j * 512 + w],
                    start=True, stop=True), ins=[qsb, peRv], outs=[bk2])
                P.add("dve", lambda e, j=jc, w=w, b=bk2, rstg=rstg: e.tensor_copy(
                    out=rstg[:, j * 512:j * 512 + w], in_=b[:, 0:w]),
                    ins=[bk2], outs=[rstg])
            P.dma("pool", R_d[par][qt * 128:(qt + 1) * 128, :], rstg[:],
                  out_key=f"l3aR_{par}_{qt}")
        for kt in range(NKT):
            rstg = rstg_sh[kt % 2]
            for jc in range(5):
                w = 512 if jc < 4 else JSPAN - 4 * 512
                bk2 = pbS[jc % 2]
                P.add("pe", lambda e, k=kt, j=jc, w=w, b=bk2, ksb=ksb: e.matmul(
                    b[:, 0:w], ksb[:, k * 128:(k + 1) * 128],
                    peK[:, 1920 - k * 128 + j * 512:
                        1920 - k * 128 + j * 512 + w],
                    start=True, stop=True), ins=[ksb, peK], outs=[bk2])
                P.add("dve", lambda e, j=jc, w=w, b=bk2, rstg=rstg: e.tensor_copy(
                    out=rstg[:, j * 512:j * 512 + w], in_=b[:, 0:w]),
                    ins=[bk2], outs=[rstg])
            P.dma("pool", KR_d[par][kt * 128:(kt + 1) * 128, :], rstg[:],
                  out_key=f"l3aK_{par}_{kt}")
        for side, src_ in ((0, qsb), (1, ksb)):
            for tc in range(SEQ // 512):
                P.add("act", lambda e, s=src_, t=tc, sq=sq: e.activation(
                    out=sq[:], in_=s[:, t * 512:(t + 1) * 512], func=AF.Square),
                    ins=[src_], outs=[sq])
                P.add("pe", lambda e, bn=bank_n, sq=sq: e.matmul(
                    bn[0:1, :], onesb[0:64, :], sq[:], start=True, stop=True),
                    ins=[onesb, sq], outs=[bank_n])
                P.add("dve", lambda e, sd=side, t=tc, bn=bank_n, nrm=nrm:
                      e.reduce_max(nrm[0:1, sd * 8 + t: sd * 8 + t + 1],
                                   bn[0:1, :], axis=AX.X),
                      ins=[bank_n], outs=[nrm])
        P.add("dve", lambda e, nrm=nrm, sA=sA: e.reduce_max(
            sA[:], nrm[0:1, 0:SEQ // 512], axis=AX.X), ins=[nrm], outs=[sA])
        P.add("dve", lambda e, nrm=nrm, sB=sB: e.reduce_max(
            sB[:], nrm[0:1, 8:8 + SEQ // 512], axis=AX.X), ins=[nrm], outs=[sB])
        emit_sqrt11(C, sA[:], sA, sA[:], sA)
        emit_sqrt11(C, sB[:], sB, sB[:], sB)
        emit_sqrt11(C, c0n[:], c0n, mpe2[:], mpe2)
        P.add("dve", lambda e, sA=sA, sB=sB, sC=sC: e.tensor_mul(
            sC[0:1, 0:1], sA[:], sB[:]), ins=[sA, sB], outs=[sC])
        P.add("dve", lambda e, sA=sA, c0n=c0n, sC=sC: e.tensor_mul(
            sC[0:1, 1:2], sA[:], c0n[:]), ins=[sA, c0n], outs=[sC])
        P.add("dve", lambda e, sB=sB, c0n=c0n, sC=sC: e.tensor_mul(
            sC[0:1, 2:3], sB[:], c0n[:]), ins=[sB, c0n], outs=[sC])
        P.add("dve", lambda e, sC=sC: e.tensor_scalar_mul(
            sC[0:1, 2:3], sC[0:1, 2:3], 0.125), ins=[sC], outs=[sC])
        P.add("dve", lambda e, sC=sC, c0n=c0n: e.reduce_sum(
            c0n[:], sC[0:1, 0:3], axis=AX.X), ins=[sC], outs=[c0n])
        P.add("dve", lambda e, c0n=c0n: e.tensor_scalar_mul(c0n[:], c0n[:], -1.0),
              ins=[c0n], outs=[c0n])
        P.add("pe", lambda e, bn=bank_n, c0n=c0n: e.matmul(
            bn[:, 0:1], C.ones_row[:], c0n[:], start=True, stop=True),
            ins=[C.ones_row, c0n], outs=[bank_n])
        P.add("dve", lambda e, bn=bank_n, c0big=c0big: e.tensor_copy(
            out=c0big[:], in_=bn[:, 0:1]), ins=[bank_n], outs=[c0big])
        for qch in range(SEQ // 512):
            q0g = qch * 512
            for i in range(4):
                qt = qch * 4 + i
                gap = AP(R_d[par].tensor, (qt * 128) * JSPAN + 127,
                         [[JSPAN - 1, 128], [1, SEQ]])
                P.dma("sp", rq_sb[:, i * SEQ:(i + 1) * SEQ], gap,
                      in_key=f"l3aR_{par}_{qt}")
            for kt in range(NKT):
                bank = pbS[kt % 2]
                for i in range(4):
                    P.add("pe", lambda e, k=kt, i=i, b=bank, rq_sb=rq_sb:
                          e.matmul(
                              b[:, i * 128:(i + 1) * 128],
                              rq_sb[:, i * SEQ + k * 128: i * SEQ + (k + 1) * 128],
                              C.ident_bf[:], start=(i == 0), stop=False),
                          ins=[rq_sb, C.ident_bf], outs=[bank])
                rk = rk_sb[kt % 4]
                gkp = AP(KR_d[par].tensor, (kt * 128) * JSPAN + 127 + q0g,
                         [[JSPAN - 1, 128], [1, 512]])
                P.dma("sp", rk[:], gkp, in_key=f"l3aK_{par}_{kt}")
                P.add("pe", lambda e, b=bank, rk=rk: e.matmul(
                    b[:], C.ident_bf[:], rk[:], start=False, stop=False),
                    ins=[C.ident_bf, rk], outs=[bank])
                P.add("pe", lambda e, k=kt, q=qch, b=bank, qsb=qsb, ksb=ksb:
                      e.matmul(b[:], ksb[:, k * 128:(k + 1) * 128],
                               qsb[:, q * 512:(q + 1) * 512],
                               start=False, stop=True),
                      ins=[ksb, qsb], outs=[bank])
                P.add("act", lambda e, b=bank, k=kt, ptil=ptil, c0big=c0big:
                      e.activation(
                          out=ptil[:, (k % 2) * 512:(k % 2 + 1) * 512],
                          in_=b[:], func=AF.Exp, bias=c0big[:]),
                      ins=[bank, c0big], outs=[ptil])
                if kt > 0:
                    P.add("pe", lambda e, k=kt - 1, h=h, b=pbA, ptil=ptil,
                          s=(kt - 1 == 0):
                          e.matmul(b[0:65, :],
                                   vtmb[:, k * 1040 + h * 65:
                                        k * 1040 + (h + 1) * 65],
                                   ptil[:, (k % 2) * 512:(k % 2 + 1) * 512],
                                   start=s, stop=False),
                          ins=[vtmb, ptil], outs=[pbA])
            P.add("pe", lambda e, k=NKT - 1, h=h, b=pbA, ptil=ptil:
                  e.matmul(b[0:65, :],
                           vtmb[:, k * 1040 + h * 65: k * 1040 + (h + 1) * 65],
                           ptil[:, (k % 2) * 512:(k % 2 + 1) * 512],
                           start=False, stop=True),
                  ins=[vtmb, ptil], outs=[pbA])
            P.add("dve", lambda e, b=pbA, rsum=rsum: e.tensor_copy(
                out=rsum[:], in_=b[64:65, :]), ins=[pbA], outs=[rsum])
            P.add("dve", lambda e, rsum=rsum: e.reciprocal(rsum[:], rsum[:]),
                  ins=[rsum], outs=[rsum])
            P.add("pe", lambda e, pb_=pbrb, rsum=rsum: e.matmul(
                pb_[0:64, :], C.ones_row[0:1, 0:64], rsum[:],
                start=True, stop=True), ins=[C.ones_row, rsum], outs=[pbrb])
            P.add("dve", lambda e, pb_=pbrb, rbig=rbig: e.tensor_copy(
                out=rbig[:], in_=pb_[0:64, :]), ins=[pbrb], outs=[rbig])
            oc = h // 2
            pb0 = (h % 2) * 64
            tg = b_ * 4 + qch
            P.add("dve", lambda e, b=pbA, o=oc, p0=pb0, q=qch, bb=b_, rbig=rbig:
                  e.tensor_mul(
                      afm[p0:p0 + 64,
                          o * T + bb * SEQ + q * 512: o * T + bb * SEQ + (q + 1) * 512],
                      b[0:64, :], rbig[:]),
                  ins=[pbA, rbig], outs=[f"l3afm{uu}_{h}_{tg}"])

    for b_ in range(BPC):
        P.dma("sp", vtmb[:], vd[:, b_ * NKT * 1040:(b_ + 1) * NKT * 1040])
        for h in range(16):
            head_body(b_, h)
    for B in BB:
        for k_ in ("qsb", "ksb", "sq", "nrm", "sA", "sB", "sC", "c0n", "c0big",
                   "ptil", "rsum", "rbig"):
            P.sfree(B[k_])
        for t_ in B["rk_sb"]:
            P.sfree(t_)
        for bb2 in B["pbS"] + [B["pbA"], B["pbrb"]]:
            P.pfree(bb2)
    for t_ in (peRv, peK, mpe2, vtmb, rstg_sh[0], rstg_sh[1], rq_sh):
        P.sfree(t_)
    akeys = [f"l3afm{uu}_{h}_{t}" for h in range(16) for t in range(8)]
    P.add("dve", lambda e: e.engine_nop(), ins=akeys, outs=[afm])
    emit_delta_residual_store(C, xin, xout, afm, tt_range=range(NT),
                              tag=tag, tstride=T, t_base=0)
    P.sfree(afm)


BLOCK_CHAIN = None


def build_program(w, n_blocks=8):
    nc = bass.Bass("TRN2", target_bir_lowering=False)
    x_in = nc.dram_tensor("x_in", [T, D], F32, kind="ExternalInput").ap()
    y = nc.dram_tensor("y", [T, D], F32, kind="ExternalOutput").ap()
    P = Prog(nc)
    P.mem_init()
    win = LazyWeights(nc, w)
    C = Ctx(nc, P, win)
    emit_dev_consts(C)
    xa = C.scratch_dram("x_chain_a", [T, D], F32)
    xb = C.scratch_dram("x_chain_b", [T, D], F32)
    chain = [emit_l0_attn, emit_l0_ffn, emit_l1_attn, emit_l1_ffn,
             emit_l2_attn, emit_l2_ffn, emit_l3_attn, emit_l3_ffn][:n_blocks]
    bufs = [x_in, xa, xb, xa, xb, xa, xb, xa, y]
    bufs[n_blocks] = y
    for i, f in enumerate(chain):
        f(C, bufs[i], bufs[i + 1], tag=("out" if i == len(chain) - 1 else ""))
    P.finalize()
    P.emit()
    return nc, win, P


def kernel(**inputs):
    w = prep_weights(inputs)
    x = np.ascontiguousarray(np.asarray(inputs["L0_attn_x"], dtype=np.float32))
    shards = [np.ascontiguousarray(x[2 * i:2 * i + 2].reshape(T, D))
              for i in range(NCORES)]
    nc, win, P = build_program(w)
    in_maps = [dict({"x_in": s}, **{k: w[k] for k in win.used()})
               for s in shards]
    res = run_bass_kernel_spmd(nc, in_maps, core_ids=list(range(NCORES)))
    out = np.concatenate(
        [np.asarray(res.results[i]["y"]).reshape(2, SEQ, D)
         for i in range(NCORES)], axis=0)
    return out


# version 08 (final)
# speedup vs baseline: 1.1733x; 1.1733x over previous
import math
import numpy as np
import concourse.bass as bass
import concourse.mybir as mybir
from concourse.bass import AP
from concourse.bass_utils import run_bass_kernel_spmd

F32 = mybir.dt.float32
BF16 = mybir.dt.bfloat16
AF = mybir.ActivationFunctionType
ALU = mybir.AluOpType
AX = mybir.AxisListType

ENGINES = ("pe", "act", "dve", "pool", "sp")
N_DMA_SLOTS = 16
SAME_ENGINE_SAFE = {"pe", "act", "pool", "sp"}


def _names_of(aps):
    out = []
    for a in aps:
        if a is None:
            continue
        if isinstance(a, (list, tuple)):
            out.extend(_names_of(a))
        elif isinstance(a, str):
            out.append(a)
        elif hasattr(a, "tensor"):
            out.append(a.tensor.name)
        else:
            out.append(a.name)
    return out


class _Op:
    __slots__ = ("engine", "fn", "reads", "writes", "idx", "deps", "is_dma",
                 "slot", "slot_use", "marked", "inc_val", "vclock", "tag")

    def __init__(self, engine, fn, reads, writes, idx, is_dma, tag):
        self.engine = engine
        self.fn = fn
        self.reads = reads
        self.writes = writes
        self.idx = idx
        self.deps = []
        self.is_dma = is_dma
        self.slot = -1
        self.slot_use = 0
        self.marked = False
        self.inc_val = 0
        self.vclock = None
        self.tag = tag


class Prog:

    def __init__(self, nc):
        self.nc = nc
        self.ops = []
        self.last_writer = {}
        self.readers = {}
        self.birth_deps = {}
        self.dma_count = 0
        self.sems = {}
        for e in ENGINES:
            self.sems[e] = nc.alloc_semaphore(name=f"sem_{e}")
        self.dma_sems = [nc.alloc_semaphore(name=f"dma{i}") for i in range(N_DMA_SLOTS)]
        self.slot_uses = [0] * N_DMA_SLOTS
        self.slot_last_op = [None] * N_DMA_SLOTS
        self._const_cache = {}
        self._sbuf_region_ops = []
        self._psum_region_ops = []
        self._guards = {}
        self._touch_hist = {}
        self._uid = 0

    def add(self, engine, fn, ins=(), outs=(), dma=False, tag=""):
        reads = _names_of(ins)
        writes = _names_of(outs)
        op = _Op(engine, fn, reads, writes, len(self.ops), dma, tag)
        deps = set()
        for b in reads:
            w = self.last_writer.get(b)
            if w is not None:
                deps.add(w)
            bd = self.birth_deps.pop(b, None)
            if bd:
                deps.update(bd)
        for b in writes:
            w = self.last_writer.get(b)
            if w is not None:
                deps.add(w)
            for r in self.readers.get(b, ()):
                deps.add(r)
            bd = self.birth_deps.pop(b, None)
            if bd:
                deps.update(bd)
        if dma:
            if engine == "pool":
                self._dma_cnt_pool = getattr(self, "_dma_cnt_pool", 0) + 1
                slot = (self._dma_cnt_pool - 1) % (N_DMA_SLOTS // 2)
            else:
                self._dma_cnt_sp = getattr(self, "_dma_cnt_sp", 0) + 1
                slot = N_DMA_SLOTS // 2 + (self._dma_cnt_sp - 1) % (N_DMA_SLOTS // 2)
            op.slot = slot
            prev = self.slot_last_op[slot]
            if prev is not None:
                deps.add(prev)
            self.slot_uses[slot] += 1
            op.slot_use = self.slot_uses[slot]
            self.slot_last_op[slot] = op.idx
            self.dma_count += 1
        op.deps = sorted(deps)
        for b in reads:
            self.readers.setdefault(b, []).append(op.idx)
            self._touch_hist.setdefault(b, []).append(op.idx)
        for b in writes:
            self.last_writer[b] = op.idx
            self.readers[b] = []
            self._touch_hist.setdefault(b, []).append(op.idx)
        self.ops.append(op)
        return op

    def mem_init(self, slack=1024):
        nc = self.nc
        per_part = (nc.sbuf_bytes_remaining - slack) & ~63
        arena = nc.alloc_sbuf_tensor("sb_arena", [128, per_part], mybir.dt.uint8)
        base = None
        for alloc in nc.m.functions[0].allocations:
            if (hasattr(alloc, "memorylocations") and alloc.memorylocations
                    and alloc.memorylocations[0].name == "sb_arena"):
                base = alloc.memorylocations[0].addr
        assert base is not None
        self._sb_base = base
        self._sb_size = per_part
        self._sb_free = [(0, per_part)]
        self._sb_live = {}
        self._sb_dead = []
        self._ps_banks = []
        self._ps_free = []
        for i in range(8):
            b = nc.alloc_psum_tensor(f"pbank{i}", [128, 512], F32)
            self._ps_banks.append(b)
            self._ps_free.append(i)
        self._ps_dead = {}

    def salloc(self, name, shape, dtype):
        per_part = int(np.prod(shape[1:])) * np.dtype(mybir.dt.np(dtype)).itemsize
        per_part = (per_part + 63) & ~63
        best = None
        for i, (a, b) in enumerate(self._sb_free):
            if b - a >= per_part and (best is None or
                                      b - a < self._sb_free[best][1] - self._sb_free[best][0]):
                best = i
        if best is None:
            raise MemoryError(f"SBUF arena full allocating {name} ({per_part}B/part); "
                              f"free={self._sb_free}")
        i = best
        a, b = self._sb_free[i]
        self._sb_free[i] = (a + per_part, b)
        if self._sb_free[i][0] == self._sb_free[i][1]:
            self._sb_free.pop(i)
        t = self.nc.alloc_sbuf_tensor_at(name, list(shape), dtype,
                                         offset=self._sb_base + a)
        name = t.name
        end = a + per_part
        deps = []
        keep = []
        for (a0, b0, opl) in self._sb_dead:
            if a0 < end and b0 > a:
                deps.extend(opl)
                if a0 < a:
                    keep.append((a0, a, opl))
                if b0 > end:
                    keep.append((end, b0, opl))
            else:
                keep.append((a0, b0, opl))
        self._sb_dead = keep
        if deps:
            self.birth_deps[name] = sorted(set(deps))
        for nm2, (a2, b2) in self._sb_live.items():
            assert not (a2 < end and b2 > a), \
                f"OVERLAP {name}[{a},{end}) with live {nm2}[{a2},{b2})"
        self._sb_live[name] = (a, end)
        return t

    def sfree(self, t):
        if isinstance(t, (list, tuple)):
            for x in t:
                self.sfree(x)
            return
        name = t.name if hasattr(t, "name") else t
        a, b = self._sb_live.pop(name)
        opl = sorted(set(self._touch_hist.get(name, ())))
        self._sb_dead.append((a, b, opl))
        self._sb_free.append((a, b))
        self._sb_free.sort()
        merged = []
        for s0, s1 in self._sb_free:
            if merged and merged[-1][1] == s0:
                merged[-1] = (merged[-1][0], s1)
            else:
                merged.append((s0, s1))
        self._sb_free = merged

    def palloc(self, name=None):
        i = self._ps_free.pop(0)
        b = self._ps_banks[i]
        deps = self._ps_dead.pop(i, None)
        if deps:
            pass
        return b

    def pfree(self, t):
        name = t.name
        i = int(name[5:])
        self._ps_free.append(i)

    def uid(self):
        self._uid += 1
        return self._uid

    def finalize(self):
        ops = self.ops
        for op in ops:
            for d in op.deps:
                dop = ops[d]
                if dop.is_dma:
                    continue
                if dop.engine == op.engine and dop.engine in SAME_ENGINE_SAFE:
                    continue
                dop.marked = True
        inc_count = {e: 0 for e in ENGINES}
        for op in ops:
            if op.marked and not op.is_dma:
                inc_count[op.engine] += 1
                op.inc_val = inc_count[op.engine]
        dim = len(ENGINES) + N_DMA_SLOTS
        eidx = {e: i for i, e in enumerate(ENGINES)}
        zero = [0] * dim
        for op in ops:
            vc = list(zero)
            for d in op.deps:
                dvc = ops[d].vclock
                for i in range(dim):
                    if dvc[i] > vc[i]:
                        vc[i] = dvc[i]
            if op.is_dma:
                j = len(ENGINES) + op.slot
                if 16 * op.slot_use > vc[j]:
                    vc[j] = 16 * op.slot_use
            elif op.marked:
                j = eidx[op.engine]
                if op.inc_val > vc[j]:
                    vc[j] = op.inc_val
            op.vclock = vc
        per_engine = {e: [] for e in ENGINES}
        eng_clock = {e: list(zero) for e in ENGINES}
        n_waits = 0
        for op in ops:
            waits = []
            clk = eng_clock[op.engine]
            for d in op.deps:
                dop = ops[d]
                if dop.is_dma:
                    j = len(ENGINES) + dop.slot
                    val = 16 * dop.slot_use
                    if clk[j] < val:
                        waits.append((self.dma_sems[dop.slot], val, j))
                elif dop.engine == op.engine and dop.engine in SAME_ENGINE_SAFE:
                    continue
                else:
                    j = eidx[dop.engine]
                    val = dop.inc_val
                    if clk[j] < val:
                        waits.append((self.sems[dop.engine], val, j))
                dvc = dop.vclock
                for i in range(dim):
                    if dvc[i] > clk[i]:
                        clk[i] = dvc[i]
            by_sem = {}
            for s, v, j in waits:
                k = id(s)
                if k not in by_sem or by_sem[k][1] < v:
                    by_sem[k] = (s, v, j)
            waits = list(by_sem.values())
            n_waits += len(waits)
            per_engine[op.engine].append((op, waits))
        self._per_engine = per_engine
        self.n_waits = n_waits

    def emit(self):
        nc = self.nc
        per_engine = self._per_engine
        sems = self.sems
        dma_sems = self.dma_sems

        def run_engine(eng_obj, ename):
            for op, waits in per_engine[ename]:
                for s, v, _ in waits:
                    eng_obj.wait_ge(s, v)
                ins = op.fn(eng_obj)
                if op.is_dma:
                    ins.then_inc(dma_sems[op.slot], 16)
                elif op.marked:
                    ins.then_inc(sems[op.engine], 1)

        with nc.Block() as block:
            @block.tensor
            def _(t):
                run_engine(t, "pe")

            @block.scalar
            def _(s):
                run_engine(s, "act")

            @block.vector
            def _(v):
                run_engine(v, "dve")

            @block.gpsimd
            def _(g):
                run_engine(g, "pool")

            @block.sync
            def _(s):
                run_engine(s, "sp")
                for k in range(N_DMA_SLOTS):
                    if self.slot_uses[k]:
                        s.wait_ge(dma_sems[k], 16 * self.slot_uses[k])

    def dma(self, q, out_ap, in_ap, ins=(), outs=(), tag="", out_key=None,
            in_key=None, **kw):
        ik = in_key if in_key else in_ap
        ins = list(ins) + (ik if isinstance(ik, list) else [ik])
        ok = out_key if out_key else out_ap
        outs = list(outs) + (ok if isinstance(ok, list) else [ok])

        def fn(e, out_ap=out_ap, in_ap=in_ap, kw=kw):
            return e.dma_start(out=out_ap, in_=in_ap, **kw)
        return self.add(q, fn, ins=ins, outs=outs, dma=True, tag=tag)


D = 1024
SEQ = 2048
BPC = 2
T = BPC * SEQ
NT = T // 128
NF = D // 128
NCORES = 8
LN_EPS = 1e-5


class Ctx:

    def __init__(self, nc, P, win):
        self.nc = nc
        self.P = P
        self.win = win
        self.dram = {}

    def scratch_dram(self, name, shape, dtype=F32):
        if name not in self.dram:
            self.dram[name] = self.nc.dram_tensor(name, shape, dtype,
                                                  kind="Internal").ap()
        return self.dram[name]


def emit_dev_consts(C):
    P = C.P
    C.ident_bf = P.salloc("c_ident_bf", [128, 128], BF16)
    C.ident_f32 = P.salloc("c_ident_f32", [128, 128], F32)
    C.ones_row = P.salloc("c_ones_row", [1, 128], F32)
    C.m0 = P.salloc("c_m0", [1, 128], F32)
    C.m1 = P.salloc("c_m1", [1, 128], F32)
    for t, nm in ((C.ident_bf, "ib"), (C.ident_f32, "if")):
        P.add("pool", lambda e, t=t: e.memset(t[:], 1.0), outs=[t])
        P.add("pool", lambda e, t=t: e.affine_select(
            out=t[:], in_=t[:], pattern=[[1, 128]], compare_op=ALU.is_equal,
            fill=0.0, base=0, channel_multiplier=-1), ins=[t], outs=[t])
    P.add("pool", lambda e: e.memset(C.ones_row[:], 1.0), outs=[C.ones_row])
    P.add("pool", lambda e: e.memset(C.m0[:], 1.0), outs=[C.m0])
    P.add("pool", lambda e: e.affine_select(
        out=C.m0[:], in_=C.m0[:], pattern=[[-1, 128]], compare_op=ALU.is_ge,
        fill=0.0, base=63, channel_multiplier=0), ins=[C.m0], outs=[C.m0])
    P.add("pool", lambda e: e.memset(C.m1[:], 1.0), outs=[C.m1])
    P.add("pool", lambda e: e.affine_select(
        out=C.m1[:], in_=C.m1[:], pattern=[[1, 128]], compare_op=ALU.is_ge,
        fill=0.0, base=-64, channel_multiplier=0), ins=[C.m1], outs=[C.m1])


def constf(C, v):
    P = C.P
    key = float(v)
    if key not in P._const_cache:
        t = P.salloc(f"c_f{P.uid()}", [128, 1], F32)
        P.add("pool", lambda g, t=t, v=key: g.memset(t[:], v), outs=[t])
        P._const_cache[key] = t
    return P._const_cache[key]


def load_weight(C, name, n_in, n_out, dram_ap=None, dt=BF16):
    P = C.P
    src = dram_ap if dram_ap is not None else C.win[name]
    nchunk = n_in // 128
    u_ = P.uid()
    Ws = [P.salloc(f"w_{name}_{u_}_{c}", [128, n_out], dt)
          for c in range(nchunk)]
    for c in range(nchunk):
        P.dma("pool", Ws[c][:], src[c * 128:(c + 1) * 128, :])
    return Ws


def wslice(W, n_out, fc, o0, ow):
    if isinstance(W, list):
        return W[fc][:, o0:o0 + ow]
    return W[:, fc * n_out + o0: fc * n_out + o0 + ow]


def load_bias(C, name, n_out, dram_ap=None):
    P = C.P
    src = dram_ap if dram_ap is not None else C.win[name + "_pc"]
    B = P.salloc(f"b_{name}_{P.uid()}", [128, n_out // 128], F32)
    P.dma("sp", B[:], src)
    return B


def fmv(fm, c, t0, tw, tstride=T):
    return fm[:, c * tstride + t0: c * tstride + t0 + tw]


def emit_outer_ln(C, xin, *, want_fm=True, want_tm=False, tm_dtype=BF16,
                  u32_dram=None, want_lo_fm=False, tag=""):
    P = C.P
    u = P.uid()
    xt = [P.salloc(f"lnx{u}_{i}", [128, D], F32) for i in range(4)]
    sums = P.salloc(f"lnsum{u}", [128, NT], F32)
    sqs = P.salloc(f"lnsq{u}", [128, NT], F32)
    dump = P.salloc(f"lndump{u}", [128, D], BF16)
    res = {}
    if want_fm:
        res["u_fm"] = P.salloc(f"ufm{u}{tag}", [128, NF * T], BF16)
    if want_tm:
        res["u_tm"] = P.salloc(f"utm{u}{tag}", [128, NT * D], tm_dtype)
    if want_lo_fm:
        res["ulo_fm"] = P.salloc(f"ulo{u}{tag}", [128, NF * T], BF16)
    P.add("act", lambda e: e.memzero(sqs[:]), outs=[sqs])
    for tt in range(NT):
        x_sb = xt[tt % 4]
        P.dma("sp" if tt % 2 == 0 else "pool", x_sb[:],
              xin[tt * 128:(tt + 1) * 128, :])
        P.add("dve", lambda e, a=x_sb, c=tt: e.reduce_sum(
            sums[:, c:c + 1], a[:], axis=AX.X), ins=[x_sb], outs=[sums])
        P.add("act", lambda e, a=x_sb, c=tt: e.activation(
            out=dump[:], in_=a[:], func=AF.Square,
            accum_out=sqs[:, c:c + 1]), ins=[x_sb], outs=[dump, sqs])
    mean = P.salloc(f"lnmean{u}", [128, NT], F32)
    var = P.salloc(f"lnvar{u}", [128, NT], F32)
    rstd = P.salloc(f"lnrstd{u}", [128, NT], F32)
    nmr = P.salloc(f"lnnmr{u}", [128, NT], F32)
    P.add("dve", lambda e: e.tensor_scalar_mul(mean[:], sums[:], 1.0 / D),
          ins=[sums], outs=[mean])
    P.add("dve", lambda e: e.tensor_mul(var[:], mean[:], mean[:]),
          ins=[mean], outs=[var])
    P.add("dve", lambda e: e.scalar_tensor_tensor(
        var[:], sqs[:], 1.0 / D, var[:], ALU.mult, ALU.subtract),
        ins=[sqs, var], outs=[var])
    epsc = constf(C, LN_EPS)
    nhalf = constf(C, 0.0)
    P.add("act", lambda e: e.activation(out=rstd[:], in_=var[:], func=AF.Ln,
                                        bias=epsc[:]),
          ins=[var, epsc], outs=[rstd])
    P.add("act", lambda e: e.activation(out=rstd[:], in_=rstd[:], func=AF.Exp,
                                        scale=constf(C, -0.5)[:]),
          ins=[rstd, constf(C, -0.5)], outs=[rstd])
    P.add("dve", lambda e: e.scalar_tensor_tensor(
        nmr[:], mean[:], -1.0, rstd[:], ALU.mult, ALU.mult),
        ins=[mean, rstd], outs=[nmr])
    if want_lo_fm and u32_dram is None:
        u32_dram = False
    u32t = [P.salloc(f"lnu32{u}_{i}", [128, D], F32) for i in range(2)] \
        if u32_dram is not None else None
    lot = [P.salloc(f"lnlo{u}_{i}", [128, D], BF16) for i in range(2)] \
        if want_lo_fm else None
    ubf = None
    if want_fm and not want_tm:
        ubf = [P.salloc(f"lnubf{u}_{i}", [128, D], BF16) for i in range(2)]
    pb = [P.palloc() for _ in range(2)] if want_fm else []
    pb_bf = [b.ap().bitcast(BF16) for b in pb]
    for tt in range(NT):
        x_sb = xt[tt % 4]
        P.dma("sp" if tt % 2 == 0 else "pool", x_sb[:],
              xin[tt * 128:(tt + 1) * 128, :])
        if want_tm:
            dst = res["u_tm"][:, tt * D:(tt + 1) * D]
            dstt = res["u_tm"]
        elif want_fm:
            dst = ubf[tt % 2][:]
            dstt = ubf[tt % 2]
        if u32_dram is not None:
            u32 = u32t[tt % 2]
            P.add("dve", lambda e, a=x_sb, c=tt, o=u32: e.tensor_scalar(
                out=o[:], in0=a[:], scalar1=rstd[:, c:c + 1],
                scalar2=nmr[:, c:c + 1], op0=ALU.mult, op1=ALU.add),
                ins=[x_sb, rstd, nmr], outs=[u32])
            if u32_dram is not False:
                P.dma("sp", u32_dram[tt * 128:(tt + 1) * 128, :], u32[:])
            if want_fm or want_tm:
                P.add("dve", lambda e, i=u32, o=dst, ot=dstt: e.tensor_copy(
                    out=o, in_=i[:]), ins=[u32], outs=[dstt])
            if want_lo_fm:
                lo = lot[tt % 2]
                P.add("dve", lambda e, i=u32, o=lo, b=dst: e.tensor_sub(
                    o[:], i[:], b), ins=[u32, dstt], outs=[lo])
        else:
            P.add("dve", lambda e, a=x_sb, c=tt, o=dst, ot=dstt: e.tensor_scalar(
                out=o, in0=a[:], scalar1=rstd[:, c:c + 1],
                scalar2=nmr[:, c:c + 1], op0=ALU.mult, op1=ALU.add),
                ins=[x_sb, rstd, nmr], outs=[dstt])
        if want_fm:
            src_t = res["u_tm"] if want_tm else ubf[tt % 2]
            src_ap = (res["u_tm"][:, tt * D:(tt + 1) * D] if want_tm
                      else ubf[tt % 2][:])
            bank = pb[tt % 2]
            bb = pb_bf[tt % 2]
            for fc in range(NF):
                P.add("pe", lambda e, s=src_ap, f=fc, b=bb: e.transpose(
                    b[:, f * 128:(f + 1) * 128], s[:, f * 128:(f + 1) * 128],
                    C.ident_bf[:]),
                    ins=[src_t, C.ident_bf], outs=[bank])
            ofm = res["u_fm"].ap().rearrange("p (c t) -> p c t", c=NF)[
                :, :, tt * 128:(tt + 1) * 128]
            ibk = bb.rearrange("p (c t) -> p c t", c=NF)
            P.add("dve", lambda e, o=ofm, i=ibk: e.tensor_copy(out=o, in_=i),
                  ins=[bank], outs=[res["u_fm"]])
            if want_lo_fm:
                lo = lot[tt % 2]
                bank2 = pb[(tt + 1) % 2]
                bb2 = pb_bf[(tt + 1) % 2]
                for fc in range(NF):
                    P.add("pe", lambda e, s=lo, f=fc, b=bb2: e.transpose(
                        b[:, f * 128:(f + 1) * 128], s[:, f * 128:(f + 1) * 128],
                        C.ident_bf[:]),
                        ins=[lo, C.ident_bf], outs=[bank2])
                olo = res["ulo_fm"].ap().rearrange("p (c t) -> p c t", c=NF)[
                    :, :, tt * 128:(tt + 1) * 128]
                ib2 = bb2.rearrange("p (c t) -> p c t", c=NF)
                P.add("dve", lambda e, o=olo, i=ib2: e.tensor_copy(out=o, in_=i),
                      ins=[bank2], outs=[res["ulo_fm"]])
    for t_ in xt:
        P.sfree(t_)
    for t_ in (sums, sqs, dump, mean, var, rstd, nmr):
        P.sfree(t_)
    if u32t:
        for t_ in u32t:
            P.sfree(t_)
    if want_lo_fm:
        for t_ in lot:
            P.sfree(t_)
    if ubf:
        for t_ in ubf:
            P.sfree(t_)
    for b in pb:
        P.pfree(b)
    return res


def emit_delta_residual_store(C, xin, xout, delta_sb, tt_range=None,
                              skip_chunks=(), tag="", tstride=T, t_base=0):
    P = C.P
    u = P.uid()
    bf = (delta_sb.dtype == BF16) if hasattr(delta_sb, "dtype") else False
    nbuf = 2
    xt = [P.salloc(f"rsx{u}_{i}", [128, D], F32) for i in range(nbuf)]
    ot = [P.salloc(f"rso{u}_{i}", [128, D], F32) for i in range(nbuf)]
    delta = delta_sb
    ident = C.ident_bf if bf else C.ident_f32
    if bf:
        pbs = [P.palloc() for _ in range(2)]
        views = [b.ap().bitcast(BF16) for b in pbs]
    else:
        pbs = [P.palloc() for _ in range(4)]
        views = [b[:] for b in pbs]
    for tt in (range(NT) if tt_range is None else tt_range):
        x_sb = xt[tt % nbuf]
        o_sb = ot[tt % nbuf]
        P.dma("sp" if tt % 2 == 0 else "pool", x_sb[:],
              xin[tt * 128:(tt + 1) * 128, :])
        if bf:
            banksof = lambda fc: (pbs[tt % 2], views[tt % 2], fc * 128)
        else:
            banksof = lambda fc: (pbs[(tt % 2) * 2 + (0 if fc < 4 else 1)],
                                  views[(tt % 2) * 2 + (0 if fc < 4 else 1)],
                                  (fc % 4) * 128)
        for fc in range(NF):
            if fc in skip_chunks:
                continue
            bank, view, off = banksof(fc)
            P.add("pe", lambda e, f=fc, v=view, o=off, t=tt: e.transpose(
                v[:, o:o + 128], fmv(delta, f, t * 128 - t_base, 128, tstride),
                ident[:]), ins=[delta, ident], outs=[bank])
        for half in (0, 1):
            if bf:
                bank, view = pbs[tt % 2], views[tt % 2]
                voff = half * 512
            else:
                bank, view = pbs[(tt % 2) * 2 + half], views[(tt % 2) * 2 + half]
                voff = 0
            sk = [fc for fc in range(half * 4, half * 4 + 4) if fc in skip_chunks]
            if not sk:
                c = slice(half * 512, (half + 1) * 512)
                P.add("dve", lambda e, o=o_sb, x=x_sb, v=view, c=c, vo=voff:
                      e.tensor_add(o[:, c], x[:, c], v[:, vo:vo + 512]),
                      ins=[x_sb, bank], outs=[o_sb])
            else:
                for fc in range(half * 4, half * 4 + 4):
                    c1 = slice(fc * 128, (fc + 1) * 128)
                    o1 = slice(voff + (fc % 4) * 128, voff + ((fc % 4) + 1) * 128)
                    if fc in skip_chunks:
                        P.add("dve", lambda e, o=o_sb, x=x_sb, c=c1: e.tensor_copy(
                            out=o[:, c], in_=x[:, c]), ins=[x_sb], outs=[o_sb])
                    else:
                        P.add("dve", lambda e, o=o_sb, x=x_sb, v=view, c=c1, oo=o1:
                              e.tensor_add(o[:, c], x[:, c], v[:, oo[0] if False else oo]),
                              ins=[x_sb, bank], outs=[o_sb])
        P.dma("sp", xout[tt * 128:(tt + 1) * 128, :], o_sb[:], tag=tag)
    for t_ in xt + ot:
        P.sfree(t_)
    for b in pbs:
        P.pfree(b)


def build_sel16(C):
    P = C.P
    sels = []
    for e in range(8):
        s = P.salloc(f"c_sel16_{e}", [1, 128], F32)
        P.add("pool", lambda g, s=s: g.memset(s[:], 1.0 / 16.0), outs=[s])
        P.add("pool", lambda g, s=s, e=e: g.affine_select(
            out=s[:], in_=s[:], pattern=[[1, 128]], compare_op=ALU.is_ge,
            fill=0.0, base=-16 * e, channel_multiplier=0), ins=[s], outs=[s])
        P.add("pool", lambda g, s=s, e=e: g.affine_select(
            out=s[:], in_=s[:], pattern=[[-1, 128]], compare_op=ALU.is_ge,
            fill=0.0, base=16 * e + 15, channel_multiplier=0), ins=[s], outs=[s])
        sels.append(s)
    return sels


def emit_l3_ffn(C, xin, xout, tag=""):
    P = C.P
    Wb = load_weight(C, "L3_ffn_weight", D, D)
    Ug = load_weight(C, "L3_ffn_lora_U", D, 128)
    Vs = load_weight(C, "L3_ffn_lora_V", 128, D)
    Gw = load_weight(C, "L3_ffn_gate_w", D, 8)
    u = emit_outer_ln(C, xin, want_fm=True, tag=tag)
    ufm = u["u_fm"]
    uu = P.uid()
    tfm = P.salloc(f"tfm_{uu}", [128, T], BF16)
    pbt = [P.palloc() for _ in range(2)]
    for tc in range(T // 512):
        bank = pbt[tc % 2]
        for fc in range(NF):
            P.add("pe", lambda e, f=fc, t=tc, b=bank, s=(fc == 0), p=(fc == NF - 1):
                  e.matmul(b[:], wslice(Ug, 128, f, 0, 128),
                           fmv(ufm, f, t * 512, 512), start=s, stop=p),
                  ins=[Ug, ufm], outs=[bank])
        P.add("dve", lambda e, t=tc, b=bank: e.tensor_copy(
            out=tfm[:, t * 512:(t + 1) * 512], in_=b[:]), ins=[bank], outs=[tfm])
    for b in pbt:
        P.pfree(b)
    gtm = P.salloc(f"gtm_{uu}", [128, 8], F32)
    gnm = P.salloc(f"gnm_{uu}", [128, 1], F32)
    gs = P.salloc(f"gs_{uu}", [128, 1], F32)
    tg = P.salloc(f"tg_{uu}", [128, T], BF16)
    gexp = P.salloc(f"gexp_{uu}", [128, 128], F32)
    pb = P.palloc()
    pbg = [P.palloc() for _ in range(2)]
    gtm_exp = AP(gtm, 0, [[8, 128], [1, 8], [0, 16]])
    for tc in range(T // 512):
        bank = pbg[tc % 2]
        for i in range(4):
            tt = tc * 4 + i
            for fc in range(NF):
                P.add("pe", lambda e, f=fc, t=tt, s=(fc == 0), p=(fc == NF - 1):
                      e.matmul(pb[:, 0:8], fmv(ufm, f, t * 128, 128),
                               wslice(Gw, 8, f, 0, 8), start=s, stop=p),
                      ins=[ufm, Gw], outs=[pb])
            P.add("dve", lambda e: e.reduce_max(gnm[:], pb[:, 0:8], axis=AX.X,
                                                negate=True), ins=[pb], outs=[gnm])
            P.add("act", lambda e: e.activation(out=gtm[:], in_=pb[:, 0:8],
                                                func=AF.Exp, bias=gnm[:],
                                                accum_out=gs[:]),
                  ins=[pb, gnm], outs=[gtm, gs])
            P.add("dve", lambda e: e.reciprocal(gs[:], gs[:]), ins=[gs], outs=[gs])
            P.add("dve", lambda e: e.tensor_scalar_mul(gtm[:], gtm[:], gs[:]),
                  ins=[gtm, gs], outs=[gtm])
            P.add("dve", lambda e: e.tensor_copy(out=gexp[:], in_=gtm_exp),
                  ins=[gtm], outs=[gexp])
            P.add("pe", lambda e, b=bank, i=i: e.transpose(
                b[:, i * 128:(i + 1) * 128], gexp[:], C.ident_f32[:]),
                ins=[gexp, C.ident_f32], outs=[bank])
        P.add("dve", lambda e, t=tc, b=bank: e.scalar_tensor_tensor(
            tg[:, t * 512:(t + 1) * 512], tfm[:, t * 512:(t + 1) * 512],
            1.0 / 16.0, b[:], ALU.mult, ALU.mult),
            ins=[tfm, bank], outs=[tg])
    P.pfree(pb)
    for b in pbg:
        P.pfree(b)
    dsb = [P.salloc(f"l3d_{uu}_{i}", [128, NF * 512], F32) for i in range(2)]
    pbo = [P.palloc() for _ in range(4)]
    for tc in range(T // 512):
        ds = dsb[tc % 2]
        for oc in range(NF):
            bank = pbo[(tc * NF + oc) % 4]
            for fc in range(NF):
                P.add("pe", lambda e, f=fc, o=oc, t=tc, b=bank, s=(fc == 0):
                      e.matmul(b[:], wslice(Wb, D, f, o * 128, 128),
                               fmv(ufm, f, t * 512, 512), start=s, stop=False),
                      ins=[Wb, ufm], outs=[bank])
            P.add("pe", lambda e, o=oc, t=tc, b=bank:
                  e.matmul(b[:], wslice(Vs, D, 0, o * 128, 128),
                           tg[:, t * 512:(t + 1) * 512], start=False, stop=True),
                  ins=[Vs, tg], outs=[bank])
            P.add("dve", lambda e, o=oc, b=bank, d=ds: e.tensor_copy(
                out=d[:, o * 512:(o + 1) * 512], in_=b[:]), ins=[bank], outs=[ds])
        emit_delta_residual_store(C, xin, xout, ds,
                                  tt_range=range(tc * 4, tc * 4 + 4),
                                  tag=tag, tstride=512, t_base=tc * 512)
    for b in pbo:
        P.pfree(b)
    for t_ in (gtm, gnm, gs, gexp, tfm, tg):
        P.sfree(t_)
    for t_ in dsb:
        P.sfree(t_)
    P.sfree(ufm)
    for w in (Wb, Ug, Vs, Gw):
        P.sfree(w)


def prep_weights(inp):
    w = {}
    f32 = lambda a: np.ascontiguousarray(a, dtype=np.float32)
    for nm in ("Wq", "Wk", "Wv", "Wo", "W1", "W2"):
        w[f"L0_attn_{nm}"] = f32(inp[f"L0_attn_{nm}"])
    w["L0_attn_wkqv"] = f32(np.concatenate(
        [np.asarray(inp["L0_attn_Wk"]), np.asarray(inp["L0_attn_Wq"]),
         np.asarray(inp["L0_attn_Wv"])], axis=1))
    for nm in ("bq", "bk", "bv", "bo", "g1", "b1", "c1", "c2", "g2", "b2"):
        w[f"L0_attn_{nm}"] = f32(inp[f"L0_attn_{nm}"])
    w["L0_ffn_W_fc"] = f32(inp["L0_ffn_W_fc"].T)
    w["L0_ffn_W_proj"] = f32(inp["L0_ffn_W_proj"].T)
    w["L0_ffn_ln_w"] = f32(inp["L0_ffn_ln_w"])
    w["L0_ffn_ln_b"] = f32(inp["L0_ffn_ln_b"])
    wk = np.asarray(inp["L1_attn_w_kqv"]).T.reshape(1024, 16, 64, 3)
    w["L1_attn_wkqv"] = f32(np.transpose(wk, (0, 3, 1, 2)).reshape(1024, 3072))
    w["L1_attn_w_proj"] = f32(np.asarray(inp["L1_attn_w_proj"]).T)
    w["L1_attn_b_proj"] = f32(inp["L1_attn_b_proj"])
    w["L1_ffn_router_w"] = f32(inp["L1_ffn_router_w"])
    w["L1_ffn_router_b"] = f32(inp["L1_ffn_router_b"])
    w["L1_ffn_expert_w"] = f32(np.asarray(inp["L1_ffn_expert_w"]).reshape(8192, 1024))
    w["L1_ffn_expert_b"] = f32(inp["L1_ffn_expert_b"])
    w["L2_attn_w_lhs"] = f32(np.asarray(inp["L2_attn_w_lhs"]).reshape(1024))
    w["L2_attn_w_rhs"] = f32(np.asarray(inp["L2_attn_w_rhs"]).reshape(1024))
    for nm in ("wq", "wk", "wv", "wo", "e0_up", "e0_down", "e1_w1", "e1_w2",
               "e1_down", "e2_l1", "e2_l2", "e2_l3", "e2_l4", "e3_down",
               "e3_up", "e3_out", "gate_w"):
        w[f"L2_ffn_{nm}"] = f32(np.asarray(inp[f"L2_ffn_{nm}"]).T)
    w["L2_ffn_wkqv"] = f32(np.concatenate(
        [w["L2_ffn_wk"], w["L2_ffn_wq"], w["L2_ffn_wv"]], axis=1))
    for nm in ("n1_w", "n2_w", "qn_w", "kn_w"):
        w[f"L2_ffn_{nm}"] = f32(inp[f"L2_ffn_{nm}"])
    for nm in ("Wq", "Wk", "Wv"):
        w[f"L3_attn_{nm}"] = f32(np.asarray(inp[f"L3_attn_{nm}"]).T)
    for nm in ("bq", "bk", "bv"):
        w[f"L3_attn_{nm}"] = f32(inp[f"L3_attn_{nm}"])
    de = np.asarray(inp["L3_attn_dist_emb"])
    pad = np.zeros((129, 64), np.float32)
    w["L3_attn_peT"] = f32(np.concatenate([de, pad], 0).T)
    w["L3_attn_peT_rev"] = f32(np.concatenate([de[::-1], pad], 0).T)
    w["L3_attn_wkqv"] = f32(np.concatenate(
        [np.asarray(inp["L3_attn_Wk"]).T, np.asarray(inp["L3_attn_Wq"]).T,
         np.asarray(inp["L3_attn_Wv"]).T], axis=1))
    w["L3_ffn_weight"] = f32(np.asarray(inp["L3_ffn_weight"]).T)
    w["L3_ffn_gate_w"] = f32(np.asarray(inp["L3_ffn_gate_w"]).T)
    lu = np.asarray(inp["L3_ffn_lora_U"])
    w["L3_ffn_lora_U"] = f32(np.transpose(lu, (2, 0, 1)).reshape(1024, 128))
    lv = np.asarray(inp["L3_ffn_lora_V"])
    w["L3_ffn_lora_V"] = f32(np.transpose(lv, (0, 2, 1)).reshape(128, 1024))
    for k in list(w.keys()):
        v = w[k]
        if v.ndim == 1 and v.size % 128 == 0:
            w[k + "_pc"] = f32(v.reshape(-1, 128).T)
    return w


class LazyWeights:

    def __init__(self, nc, w):
        self.nc = nc
        self.w = w
        self.aps = {}

    def __getitem__(self, k):
        if k not in self.aps:
            v = self.w[k]
            self.aps[k] = self.nc.dram_tensor(
                k, list(v.shape), F32, kind="ExternalInput").ap()
        return self.aps[k]

    def used(self):
        return list(self.aps)


def emit_ones_col_bf(C):
    if not hasattr(C, "ones_col_bf"):
        P = C.P
        C.ones_col_bf = P.salloc("c_ones_col_bf", [128, 1], BF16)
        P.add("pool", lambda g: g.memset(C.ones_col_bf[:], 1.0),
              outs=[C.ones_col_bf])
    return C.ones_col_bf


def emit_l2_attn(C, xin, xout, tag="", cut=0):
    P = C.P
    wl = load_bias(C, "L2_attn_w_lhs", D)
    wr = load_bias(C, "L2_attn_w_rhs", D)
    onesb = emit_ones_col_bf(C)
    uu = P.uid()
    wlr = P.salloc(f"l2wlr_{uu}", [128, NF], F32)
    P.add("dve", lambda e: e.tensor_mul(wlr[:], wl[:], wr[:]),
          ins=[wl, wr], outs=[wlr])
    u = emit_outer_ln(C, xin, want_fm=True, want_tm=True, tag=tag)
    ufm, utm = u["u_fm"], u["u_tm"]
    utm_d = C.scratch_dram(f"l2utm_{uu}", [128, NT * D], BF16)
    P.dma("sp", utm_d[:, :], utm[:])
    P.sfree(utm)
    utmb = P.salloc(f"l2utmb_{uu}", [128, (NT // BPC) * D], BF16)
    sc = P.salloc(f"l2sc_{uu}", [128, NF * SEQ], BF16)
    sqs = P.salloc(f"l2sq_{uu}", [128, 512], BF16)
    nrm = P.salloc(f"l2n_{uu}", [1, SEQ], F32)
    mx = P.salloc(f"l2mx_{uu}", [1, 2], F32)
    c0n = P.salloc(f"l2c0_{uu}", [1, 1], F32)
    c0big = P.salloc(f"l2c0b_{uu}", [128, 1], F32)
    NKT = SEQ // 128
    ptil = P.salloc(f"l2p_{uu}", [128, NKT * 512], BF16)
    rb = P.salloc(f"l2rb_{uu}", [128, 512], F32)
    rsum = P.salloc(f"l2rs_{uu}", [1, 512], F32)
    ds = P.salloc(f"l2d_{uu}", [128, NF * 512], BF16)
    if cut == 1:
        P.add("dve", lambda e: e.memset(ds[:], 0.0), outs=[ds])
        for tt in range(NT):
            emit_delta_residual_store(C, xin, xout, ds,
                                      tt_range=[tt], tag=tag, tstride=512,
                                      t_base=(tt // 4) * 512)
        for t_ in (wlr, sc, sqs, nrm, mx, c0n, c0big, ptil, rb, rsum, ds, ufm, utmb):
            P.sfree(t_)
        P.sfree(wl); P.sfree(wr)
        return
    for b_ in range(BPC):
        P.dma("sp", utmb[:],
              utm_d[:, b_ * (NT // BPC) * D:(b_ + 1) * (NT // BPC) * D])
        for fc in range(NF):
            P.add("dve", lambda e, f=fc, b=b_: e.tensor_scalar_mul(
                fmv(sc, f, 0, SEQ, SEQ), fmv(ufm, f, b * SEQ, SEQ),
                wlr[:, f:f + 1]), ins=[ufm, wlr], outs=[sc])
        bank_n = P.palloc()
        for side, (src_t, colmax) in enumerate((("u", 0), ("s", 1))):
            for tc in range(SEQ // 512):
                for fc in range(NF):
                    if side == 0:
                        sap = fmv(ufm, fc, b_ * SEQ + tc * 512, 512)
                    else:
                        sap = fmv(sc, fc, tc * 512, 512, SEQ)
                    P.add("act", lambda e, s=sap: e.activation(
                        out=sqs[:], in_=s, func=AF.Square),
                        ins=[ufm if side == 0 else sc], outs=[sqs])
                    P.add("pe", lambda e, f=fc, bn=bank_n, s=(fc == 0), p=(fc == NF - 1):
                          e.matmul(bn[0:1, :], onesb[:], sqs[:],
                                   start=s, stop=p), ins=[onesb, sqs], outs=[bank_n])
                P.add("dve", lambda e, t=tc, bn=bank_n: e.tensor_copy(
                    out=nrm[0:1, t * 512:(t + 1) * 512], in_=bn[0:1, :]),
                    ins=[bank_n], outs=[nrm])
            P.add("dve", lambda e, c=side: e.reduce_max(
                mx[0:1, c:c + 1], nrm[0:1, :], axis=AX.X), ins=[nrm], outs=[mx])
        P.add("dve", lambda e: e.tensor_mul(c0n[:], mx[0:1, 0:1], mx[0:1, 1:2]),
              ins=[mx], outs=[c0n])
        P.add("act", lambda e: e.activation(out=c0n[:], in_=c0n[:], func=AF.Ln),
              ins=[c0n], outs=[c0n])
        P.add("act", lambda e: e.activation(out=c0n[:], in_=c0n[:], func=AF.Exp,
                                            scale=constf(C, 0.5)[0:1, :]),
              ins=[c0n, constf(C, 0.5)], outs=[c0n])
        P.add("dve", lambda e: e.tensor_scalar_mul(c0n[:], c0n[:], -1.0),
              ins=[c0n], outs=[c0n])
        P.add("pe", lambda e, bn=bank_n: e.matmul(bn[:, 0:1], C.ones_row[:], c0n[:],
                                       start=True, stop=True),
              ins=[C.ones_row, c0n], outs=[bank_n])
        P.add("dve", lambda e, bn=bank_n: e.tensor_copy(out=c0big[:], in_=bn[:, 0:1]),
              ins=[bank_n], outs=[c0big])
        P.pfree(bank_n)
        if cut == 2:
            continue
        pbS = [P.palloc() for _ in range(2)]
        pbA = [P.palloc() for _ in range(2)]
        pbsum = P.palloc()
        pbrb = P.palloc()
        for qch in range(SEQ // 512):
            q0 = b_ * SEQ + qch * 512
            for kt in range(NKT):
                bank = pbS[kt % 2]
                for fc in range(NF):
                    P.add("pe", lambda e, f=fc, k=kt, q0=q0, b=bank,
                          s=(fc == 0), p=(fc == NF - 1):
                          e.matmul(b[:], fmv(sc, f, k * 128, 128, SEQ),
                                   fmv(ufm, f, q0, 512), start=s, stop=p),
                          ins=[sc, ufm], outs=[bank])
                P.add("act", lambda e, b=bank, k=kt: e.activation(
                    out=ptil[:, k * 512:(k + 1) * 512], in_=b[:], func=AF.Exp,
                    bias=c0big[:]),
                    ins=[bank, c0big], outs=[ptil])
            if cut == 3:
                continue
            for kt in range(NKT):
                P.add("pe", lambda e, k=kt, pb_=pbsum, s=(kt == 0), p=(kt == NKT - 1):
                      e.matmul(pb_[0:1, :], onesb[:],
                               ptil[:, k * 512:(k + 1) * 512], start=s, stop=p),
                      ins=[onesb, ptil], outs=[pbsum])
            P.add("dve", lambda e, pb_=pbsum: e.tensor_copy(out=rsum[:], in_=pb_[0:1, :]),
                  ins=[pbsum], outs=[rsum])
            P.add("dve", lambda e: e.reciprocal(rsum[:], rsum[:]),
                  ins=[rsum], outs=[rsum])
            P.add("pe", lambda e, pb_=pbrb: e.matmul(pb_[:], C.ones_row[:], rsum[:],
                                           start=True, stop=True),
                  ins=[C.ones_row, rsum], outs=[pbrb])
            P.add("dve", lambda e, pb_=pbrb: e.tensor_copy(out=rb[:], in_=pb_[:]),
                  ins=[pbrb], outs=[rb])
            if cut == 4:
                continue
            if cut == 6:
                for dc in range(NF):
                    bank = pbA[dc % 2]
                    for kt in range(NKT):
                        P.add("pe", lambda e, k=kt, d=dc, b=bank,
                              s=(kt == 0), p=(kt == NKT - 1):
                              e.matmul(b[:], utmb[:, k * D + d * 128: k * D + (d + 1) * 128],
                                       ptil[:, k * 512:(k + 1) * 512], start=s, stop=p),
                              ins=[utmb, ptil], outs=[bank])
                    P.add("dve", lambda e, b=bank: e.tensor_copy(
                        out=rb[:], in_=b[:]), ins=[bank], outs=[rb])
                continue
            for dc in range(NF):
                bank = pbA[dc % 2]
                for kt in range(NKT):
                    P.add("pe", lambda e, k=kt, d=dc, b=bank,
                          s=(kt == 0), p=(kt == NKT - 1):
                          e.matmul(b[:], utmb[:, k * D + d * 128: k * D + (d + 1) * 128],
                                   ptil[:, k * 512:(k + 1) * 512], start=s, stop=p),
                          ins=[utmb, ptil], outs=[bank])
                if cut == 5:
                    P.add("dve", lambda e, d=dc, b=bank: e.tensor_copy(
                        out=ds[:, d * 512:(d + 1) * 512], in_=b[:]),
                        ins=[bank], outs=[ds])
                else:
                    P.add("dve", lambda e, d=dc, b=bank: e.tensor_mul(
                        ds[:, d * 512:(d + 1) * 512], b[:], rb[:]),
                        ins=[bank, rb], outs=[ds])
            emit_delta_residual_store(
                C, xin, xout, ds,
                tt_range=range(q0 // 128, q0 // 128 + 4),
                tag=tag, tstride=512, t_base=q0)
        for b in pbS + pbA + [pbsum, pbrb]:
            P.pfree(b)
    if cut in (2, 3, 4, 6):
        P.add("dve", lambda e: e.memset(ds[:], 0.0), outs=[ds])
        for tt in range(NT):
            emit_delta_residual_store(C, xin, xout, ds,
                                      tt_range=[tt], tag=tag, tstride=512,
                                      t_base=(tt // 4) * 512)
    for t_ in (wlr, sc, sqs, nrm, mx, c0n, c0big, ptil, rb, rsum, ds, ufm, utmb):
        P.sfree(t_)
    P.sfree(wl)
    P.sfree(wr)


def emit_qkv_spill(C, ufm, Wkqv, uu, scale_q=0.125, bq=None, bk=None,
                   bvbig=None):
    P = C.P
    qd = C.scratch_dram(f"qd_{uu}", [D, T], BF16)
    kd = C.scratch_dram(f"kd_{uu}", [D, T], BF16)
    vd = C.scratch_dram(f"vd_{uu}", [128, NT * 16 * 65], BF16)
    stage = [P.salloc(f"qkvs_{uu}_{i}", [128, 512], BF16) for i in range(2)]
    vstage = [P.salloc(f"vst_{uu}_{i}", [128, 16 * 65], BF16) for i in range(2)]
    pb = [P.palloc() for _ in range(2)]
    sc = constf(C, scale_q)
    for sblk, dram, scaled in ((0, kd, False), (1, qd, True)):
        for oc in range(NF):
            for tc in range(T // 512):
                bank = pb[(oc * 8 + tc) % 2]
                st = stage[(oc * 8 + tc) % 2]
                for fc in range(NF):
                    P.add("pe", lambda e, f=fc, o=oc, t=tc, b=bank, sb=sblk,
                          s=(fc == 0), p=(fc == NF - 1):
                          e.matmul(b[:], wslice(Wkqv, 3072, f, sb * 1024 + o * 128, 128),
                                   fmv(ufm, f, t * 512, 512), start=s, stop=p),
                          ins=[Wkqv, ufm], outs=[bank])
                bias = bq if scaled else bk
                if bias is not None:
                    P.add("dve", lambda e, b=bank, st=st, o=oc, bi=bias, s2=scaled:
                          e.tensor_scalar(out=st[:], in0=b[:],
                                          scalar1=(scale_q if s2 else 1.0),
                                          scalar2=bi[:, o:o + 1],
                                          op0=ALU.mult, op1=ALU.add),
                          ins=[bank, bias], outs=[st])
                elif scaled:
                    P.add("act", lambda e, b=bank, st=st: e.activation(
                        out=st[:], in_=b[:], func=AF.Copy, scale=sc[:]),
                        ins=[bank, sc], outs=[st])
                else:
                    P.add("act", lambda e, b=bank, st=st: e.activation(
                        out=st[:], in_=b[:], func=AF.Copy),
                        ins=[bank, sc], outs=[st])
                P.dma("sp", dram[oc * 128:(oc + 1) * 128, tc * 512:(tc + 1) * 512],
                      st[:])
    for tt in range(NT):
        vs = vstage[tt % 2]
        ones_ap = AP(vs, 64, [[16 * 65, 128], [65, 16], [0, 1]])
        P.add("pool", lambda g, v=vs, oa=ones_ap: g.memset(oa, 1.0), outs=[vs])
        for half in range(2):
            bank = pb[half]
            for fc in range(NF):
                P.add("pe", lambda e, f=fc, t=tt, b=bank, hf=half,
                      s=(fc == 0), p=(fc == NF - 1):
                      e.matmul(b[:], fmv(ufm, f, t * 128, 128),
                               wslice(Wkqv, 3072, f, 2048 + hf * 512, 512),
                               start=s, stop=p),
                      ins=[ufm, Wkqv], outs=[bank])
            oap = AP(vs, half * 8 * 65, [[16 * 65, 128], [65, 8], [1, 64]])
            ibk = bank.ap().rearrange("p (h j) -> p h j", h=8)
            if bvbig is not None:
                bvv = bvbig.ap().rearrange("p (h j) -> p h j", h=16)[
                    :, half * 8:(half + 1) * 8, :]
                P.add("dve", lambda e, o=oap, i=ibk, b=bvv: e.tensor_tensor(
                    out=o, in0=i, in1=b, op=ALU.add),
                    ins=[bank, bvbig], outs=[vs])
            else:
                P.add("dve", lambda e, o=oap, i=ibk: e.tensor_copy(out=o, in_=i),
                      ins=[bank], outs=[vs])
        P.dma("sp", vd[:, tt * 1040:(tt + 1) * 1040], vs[:])
    for t_ in stage + vstage:
        P.sfree(t_)
    for b in pb:
        P.pfree(b)
    return qd, kd, vd


def emit_l1_attn(C, xin, xout, tag=""):
    P = C.P
    uu = P.uid()
    Wkqv = load_weight(C, "L1_attn_wkqv", D, 3072)
    onesb = emit_ones_col_bf(C)
    u = emit_outer_ln(C, xin, want_fm=True, tag=tag)
    ufm = u["u_fm"]
    qd, kd, vd = emit_qkv_spill(C, ufm, Wkqv, uu)
    P.sfree(ufm)
    P.sfree(Wkqv)
    NH, HD = 16, 64
    NKT = SEQ // 128
    BB = []
    pbS = [P.palloc() for _ in range(2)]
    pbrb = P.palloc()
    for par in range(3):
        BB.append(dict(
            qsb=P.salloc(f"l1q_{uu}_{par}", [64, SEQ], BF16),
            ksb=P.salloc(f"l1k_{uu}_{par}", [64, SEQ], BF16),
            sq=P.salloc(f"l1sq_{uu}_{par}", [64, 512], BF16),
            nrm=P.salloc(f"l1n_{uu}_{par}", [1, 16], F32),
            c0n=P.salloc(f"l1c0_{uu}_{par}", [1, 1], F32),
            c0big=P.salloc(f"l1c0b_{uu}_{par}", [128, 1], F32),
            ptil=P.salloc(f"l1p_{uu}_{par}", [128, NKT * 512], BF16),
            rsum=P.salloc(f"l1rs_{uu}_{par}", [1, 512], F32),
            rbig=P.salloc(f"l1rb_{uu}_{par}", [64, 512], F32),
            pbA=P.palloc(),
        ))
    vtmb = P.salloc(f"l1v_{uu}", [128, NKT * 16 * 65], BF16)
    afm = P.salloc(f"l1a_{uu}", [128, NF * T], BF16)

    def head_body(b_, h):
        B = BB[h % 3]
        qsb, ksb, sq, nrm = B["qsb"], B["ksb"], B["sq"], B["nrm"]
        c0n, c0big, ptil = B["c0n"], B["c0big"], B["ptil"]
        rsum, rbig = B["rsum"], B["rbig"]
        pbA = B["pbA"]
        P.dma("sp", qsb[:], qd[h * 64:(h + 1) * 64, b_ * SEQ:(b_ + 1) * SEQ])
        P.dma("sp", ksb[:], kd[h * 64:(h + 1) * 64, b_ * SEQ:(b_ + 1) * SEQ])
        bank_n = pbA
        for side, src_ in ((0, qsb), (1, ksb)):
            for tc in range(SEQ // 512):
                P.add("act", lambda e, s=src_, t=tc, sq=sq: e.activation(
                    out=sq[:], in_=s[:, t * 512:(t + 1) * 512], func=AF.Square),
                    ins=[src_], outs=[sq])
                P.add("pe", lambda e, bn=bank_n, sq=sq: e.matmul(
                    bn[0:1, :], onesb[0:64, :], sq[:],
                    start=True, stop=True), ins=[onesb, sq], outs=[bank_n])
                P.add("dve", lambda e, sd=side, t=tc, bn=bank_n, nrm=nrm:
                      e.reduce_max(
                          nrm[0:1, sd * 8 + t: sd * 8 + t + 1],
                          bn[0:1, :], axis=AX.X), ins=[bank_n], outs=[nrm])
        P.add("dve", lambda e, nrm=nrm, c0n=c0n: e.reduce_max(
            c0n[:], nrm[0:1, 0:SEQ // 512], axis=AX.X), ins=[nrm], outs=[c0n])
        P.add("dve", lambda e, nrm=nrm: e.reduce_max(
            nrm[0:1, 0:1], nrm[0:1, 8:8 + SEQ // 512], axis=AX.X),
            ins=[nrm], outs=[nrm])
        P.add("dve", lambda e, nrm=nrm, c0n=c0n: e.tensor_mul(
            c0n[:], c0n[:], nrm[0:1, 0:1]), ins=[c0n, nrm], outs=[c0n])
        P.add("act", lambda e, c0n=c0n: e.activation(
            out=c0n[:], in_=c0n[:], func=AF.Ln), ins=[c0n], outs=[c0n])
        P.add("act", lambda e, c0n=c0n: e.activation(
            out=c0n[:], in_=c0n[:], func=AF.Exp,
            scale=constf(C, 0.5)[0:1, :]),
            ins=[c0n, constf(C, 0.5)], outs=[c0n])
        P.add("dve", lambda e, c0n=c0n: e.tensor_scalar_mul(c0n[:], c0n[:], -1.0),
              ins=[c0n], outs=[c0n])
        P.add("pe", lambda e, bn=bank_n, c0n=c0n: e.matmul(
            bn[:, 0:1], C.ones_row[:], c0n[:], start=True, stop=True),
            ins=[C.ones_row, c0n], outs=[bank_n])
        P.add("dve", lambda e, bn=bank_n, c0big=c0big: e.tensor_copy(
            out=c0big[:], in_=bn[:, 0:1]), ins=[bank_n], outs=[c0big])
        for qch in range(SEQ // 512):
            ktmax = qch * 4 + 4
            for kt in range(ktmax):
                bank = pbS[kt % 2]
                P.add("pe", lambda e, k=kt, q=qch, b=bank, qsb=qsb, ksb=ksb:
                      e.matmul(b[:], ksb[:, k * 128:(k + 1) * 128],
                               qsb[:, q * 512:(q + 1) * 512],
                               start=True, stop=True),
                      ins=[ksb, qsb], outs=[bank])
                P.add("act", lambda e, b=bank, k=kt, ptil=ptil, c0big=c0big:
                      e.activation(
                          out=ptil[:, k * 512:(k + 1) * 512], in_=b[:],
                          func=AF.Exp, bias=c0big[:]),
                      ins=[bank, c0big], outs=[ptil])
                if kt >= qch * 4:
                    P.add("pool", lambda g, k=kt, q=qch, ptil=ptil:
                          g.affine_select(
                              out=ptil[:, k * 512:(k + 1) * 512],
                              in_=ptil[:, k * 512:(k + 1) * 512],
                              pattern=[[1, 512]], compare_op=ALU.is_ge, fill=0.0,
                              base=q * 512 - k * 128, channel_multiplier=-1),
                          ins=[ptil], outs=[ptil])
            for kt in range(ktmax):
                P.add("pe", lambda e, k=kt, h=h, b=pbA, ptil=ptil,
                      s=(kt == 0), p=(kt == ktmax - 1):
                      e.matmul(b[0:65, :],
                               vtmb[:, k * 1040 + h * 65: k * 1040 + (h + 1) * 65],
                               ptil[:, k * 512:(k + 1) * 512], start=s, stop=p),
                      ins=[vtmb, ptil], outs=[pbA])
            P.add("dve", lambda e, b=pbA, rsum=rsum: e.tensor_copy(
                out=rsum[:], in_=b[64:65, :]), ins=[pbA], outs=[rsum])
            P.add("dve", lambda e, rsum=rsum: e.reciprocal(rsum[:], rsum[:]),
                  ins=[rsum], outs=[rsum])
            P.add("pe", lambda e, pb_=pbrb, rsum=rsum: e.matmul(
                pb_[0:64, :], C.ones_row[0:1, 0:64], rsum[:],
                start=True, stop=True), ins=[C.ones_row, rsum], outs=[pbrb])
            P.add("dve", lambda e, pb_=pbrb, rbig=rbig: e.tensor_copy(
                out=rbig[:], in_=pb_[0:64, :]), ins=[pbrb], outs=[rbig])
            oc = h // 2
            pb0 = (h % 2) * 64
            tg = b_ * 4 + qch
            P.add("dve", lambda e, b=pbA, o=oc, p0=pb0, q=qch, bb=b_, rbig=rbig:
                  e.tensor_mul(
                      afm[p0:p0 + 64,
                          o * T + bb * SEQ + q * 512: o * T + bb * SEQ + (q + 1) * 512],
                      b[0:64, :], rbig[:]),
                  ins=[pbA, rbig], outs=[f"l1afm{uu}_{h}_{tg}"])

    for b_ in range(BPC):
        P.dma("sp", vtmb[:], vd[:, b_ * NKT * 1040:(b_ + 1) * NKT * 1040])
        for h in range(NH):
            head_body(b_, h)
    for B in BB:
        for k_ in ("qsb", "ksb", "sq", "nrm", "c0n", "c0big", "ptil", "rsum",
                   "rbig"):
            P.sfree(B[k_])
        P.pfree(B["pbA"])
    for b in pbS + [pbrb]:
        P.pfree(b)
    P.sfree(vtmb)
    Wp = load_weight(C, "L1_attn_w_proj", D, D)
    bp = load_bias(C, "L1_attn_b_proj", D)
    ds = P.salloc(f"l1d_{uu}", [128, NF * 512], F32)
    pbo = [P.palloc() for _ in range(2)]
    for tc in range(T // 512):
        for oc in range(NF):
            bank = pbo[oc % 2]
            for fc in range(NF):
                P.add("pe", lambda e, f=fc, o=oc, t=tc, b=bank,
                      s=(fc == 0), p=(fc == NF - 1):
                      e.matmul(b[:], wslice(Wp, D, f, o * 128, 128),
                               fmv(afm, f, t * 512, 512), start=s, stop=p),
                      ins=[Wp, f"l1afm{uu}_{2 * fc}_{tc}",
                           f"l1afm{uu}_{2 * fc + 1}_{tc}"], outs=[bank])
            P.add("dve", lambda e, o=oc, b=bank: e.tensor_scalar(
                out=ds[:, o * 512:(o + 1) * 512], in0=b[:],
                scalar1=bp[:, o:o + 1], scalar2=None, op0=ALU.add),
                ins=[bank, bp], outs=[ds])
        emit_delta_residual_store(C, xin, xout, ds,
                                  tt_range=range(tc * 4, tc * 4 + 4),
                                  tag=tag, tstride=512, t_base=tc * 512)
    for b in pbo:
        P.pfree(b)
    for t_ in (afm, ds, Wp, bp):
        P.sfree(t_)


def emit_quantize_weight(C, name, n_in, n_out, uu):
    P = C.P
    src = C.win[name]
    nch = n_in // 128
    Wq = [P.salloc(f"wq_{name}_{uu}_{c}", [128, n_out], BF16)
          for c in range(nch)]
    wst = [P.salloc(f"wst_{name}_{i}", [128, n_out], F32) for i in range(2)]
    gsum = P.salloc(f"gs_{name}", [128, nch], F32)
    dump = P.salloc(f"gd_{name}", [128, n_out], BF16)
    tots = P.salloc(f"gt_{name}", [128, 1], F32)
    onesf = constf(C, 1.0)
    nthr = P.salloc(f"gn_{name}", [128, 1], F32)
    bank = P.palloc()
    P.add("dve", lambda e: e.memset(gsum[:], 0.0), outs=[gsum])
    for c in range(nch):
        w_sb = wst[c % 2]
        P.dma("sp", w_sb[:], src[c * 128:(c + 1) * 128, :])
        P.add("act", lambda e, w=w_sb, c=c: e.activation(
            out=dump[:], in_=w[:], func=AF.Abs, accum_out=gsum[:, c:c + 1]),
            ins=[w_sb], outs=[dump, gsum])
    P.add("dve", lambda e: e.reduce_sum(tots[:], gsum[:], axis=AX.X),
          ins=[gsum], outs=[tots])
    P.add("pe", lambda e, b=bank: e.matmul(b[0:1, 0:1], tots[:], onesf[:],
                                           start=True, stop=True),
          ins=[tots, onesf], outs=[bank])
    P.add("dve", lambda e, b=bank: e.tensor_scalar(
        out=b[0:1, 0:1], in0=b[0:1, 0:1], scalar1=-0.5 / (n_in * n_out),
        scalar2=-0.5e-6, op0=ALU.mult, op1=ALU.add), ins=[bank], outs=[bank])
    P.add("dve", lambda e, b=bank: e.tensor_copy(out=nthr[0:1, :], in_=b[0:1, 0:1]),
          ins=[bank], outs=[nthr])
    P.add("pe", lambda e, b=bank: e.matmul(b[:, 0:1], C.ones_row[:], nthr[0:1, :],
                                           start=True, stop=True),
          ins=[C.ones_row, nthr], outs=[bank])
    P.add("dve", lambda e, b=bank: e.tensor_copy(out=nthr[:], in_=b[:, 0:1]),
          ins=[bank], outs=[nthr])
    P.pfree(bank)
    sgn = P.salloc(f"gsgn_{name}", [128, n_out], BF16)
    mm = P.salloc(f"gm_{name}", [128, n_out], F32)
    for c in range(nch):
        w_sb = wst[c % 2]
        P.dma("sp", w_sb[:], src[c * 128:(c + 1) * 128, :])
        P.add("act", lambda e, w=w_sb: e.activation(
            out=mm[:], in_=w[:], func=AF.Abs),
            ins=[w_sb], outs=[mm])
        P.add("act", lambda e: e.activation(out=mm[:], in_=mm[:], func=AF.Sign,
                                            bias=nthr[:]),
              ins=[mm, nthr], outs=[mm])
        P.add("dve", lambda e: e.tensor_scalar_max(mm[:], mm[:], 0.0),
              ins=[mm], outs=[mm])
        P.add("act", lambda e, w=w_sb: e.activation(
            out=sgn[:], in_=w[:], func=AF.Sign), ins=[w_sb], outs=[sgn])
        P.add("dve", lambda e, c=c: e.tensor_mul(
            Wq[c][:], sgn[:], mm[:]),
            ins=[sgn, mm], outs=[Wq[c]])
    for t_ in wst + [gsum, dump, tots, nthr, sgn, mm]:
        P.sfree(t_)
    return Wq


def emit_l0_ffn(C, xin, xout, tag=""):
    P = C.P
    uu = P.uid()
    HID = 4096
    NH = HID // 128
    onesb = emit_ones_col_bf(C)
    u = emit_outer_ln(C, xin, want_fm=True, tag=tag)
    ufm = u["u_fm"]
    Wqf = emit_quantize_weight(C, "L0_ffn_W_fc", D, HID, uu)
    h_d = C.scratch_dram(f"l0fh_{uu}", [HID, T], BF16)
    hst = [P.salloc(f"l0hs_{uu}_{i}", [128, 512], BF16) for i in range(2)]
    hsq = P.salloc(f"l0hq_{uu}", [128, 512], BF16)
    mu = P.salloc(f"l0mu_{uu}", [1, T], F32)
    rstd = P.salloc(f"l0rs_{uu}", [1, T], F32)
    pb = [P.palloc() for _ in range(2)]
    pbs1 = P.palloc()
    pbs2 = P.palloc()
    for tc in range(T // 512):
        for oc in range(NH):
            bank = pb[oc % 2]
            st = hst[oc % 2]
            for fc in range(NF):
                P.add("pe", lambda e, f=fc, o=oc, t=tc, b=bank,
                      s=(fc == 0), p=(fc == NF - 1):
                      e.matmul(b[:], wslice(Wqf, HID, f, o * 128, 128),
                               fmv(ufm, f, t * 512, 512), start=s, stop=p),
                      ins=[Wqf, ufm], outs=[bank])
            P.add("act", lambda e, b=bank, st=st: e.activation(
                out=st[:], in_=b[:], func=AF.Gelu), ins=[bank], outs=[st])
            P.add("act", lambda e, st=st: e.activation(
                out=hsq[:], in_=st[:], func=AF.Square), ins=[st], outs=[hsq])
            P.add("pe", lambda e, o=oc, st=st, s=(oc == 0), p=(oc == NH - 1):
                  e.matmul(pbs1[0:1, :], onesb[:], st[:], start=s, stop=p),
                  ins=[onesb, st], outs=[pbs1])
            P.add("pe", lambda e, o=oc, s=(oc == 0), p=(oc == NH - 1):
                  e.matmul(pbs2[0:1, :], onesb[:], hsq[:], start=s, stop=p),
                  ins=[onesb, hsq], outs=[pbs2])
            P.dma("sp", h_d[oc * 128:(oc + 1) * 128, tc * 512:(tc + 1) * 512],
                  st[:])
        P.add("dve", lambda e, t=tc: e.tensor_scalar_mul(
            mu[0:1, t * 512:(t + 1) * 512], pbs1[0:1, :], 1.0 / HID),
            ins=[pbs1], outs=[mu])
        P.add("dve", lambda e, t=tc: e.tensor_mul(
            rstd[0:1, t * 512:(t + 1) * 512], mu[0:1, t * 512:(t + 1) * 512],
            mu[0:1, t * 512:(t + 1) * 512]), ins=[mu], outs=[rstd])
        P.add("dve", lambda e, t=tc: e.scalar_tensor_tensor(
            rstd[0:1, t * 512:(t + 1) * 512], pbs2[0:1, :], 1.0 / HID,
            rstd[0:1, t * 512:(t + 1) * 512], ALU.mult, ALU.subtract),
            ins=[pbs2, rstd], outs=[rstd])
    P.add("act", lambda e: e.activation(out=rstd[:], in_=rstd[:], func=AF.Ln,
                                        bias=constf(C, LN_EPS)[0:1, :]),
          ins=[rstd, constf(C, LN_EPS)], outs=[rstd])
    P.add("act", lambda e: e.activation(out=rstd[:], in_=rstd[:], func=AF.Exp,
                                        scale=constf(C, -0.5)[0:1, :]),
          ins=[rstd, constf(C, -0.5)], outs=[rstd])
    for t_ in hst + [hsq]:
        P.sfree(t_)
    P.sfree(ufm)
    P.sfree(Wqf)
    for b in pb:
        P.pfree(b)
    P.pfree(pbs1)
    P.pfree(pbs2)
    Wqp = emit_quantize_weight(C, "L0_ffn_W_proj", HID, D, uu)
    lnw = load_bias(C, "L0_ffn_ln_w", HID)
    lnb = load_bias(C, "L0_ffn_ln_b", HID)
    bias_fm = P.salloc(f"l0bv_{uu}", [128, NF], F32)
    lnb_bf = P.salloc(f"l0lb_{uu}", [128, NH], BF16)
    P.add("dve", lambda e: e.tensor_copy(out=lnb_bf[:], in_=lnb[:]),
          ins=[lnb], outs=[lnb_bf])
    bank = P.palloc()
    for oc in range(NF):
        for hc in range(NH):
            P.add("pe", lambda e, h=hc, o=oc, b=bank, s=(hc == 0), p=(hc == NH - 1):
                  e.matmul(b[:, o:o + 1], wslice(Wqp, D, h, o * 128, 128),
                           lnb_bf[:, h:h + 1], start=s, stop=p),
                  ins=[Wqp, lnb_bf], outs=[bank])
    P.add("dve", lambda e, b=bank: e.tensor_copy(out=bias_fm[:], in_=b[:, 0:NF]),
          ins=[bank], outs=[bias_fm])
    P.pfree(bank)
    for hc in range(NH):
        P.add("dve", lambda e, h=hc: e.tensor_scalar_mul(
            wslice(Wqp, D, h, 0, D), wslice(Wqp, D, h, 0, D), lnw[:, h:h + 1]),
            ins=[Wqp, lnw], outs=[Wqp])
    hin = [P.salloc(f"l0hi_{uu}_{i}", [128, 512], BF16) for i in range(2)]
    hna = P.salloc(f"l0hn_{uu}", [128, NH * 512], BF16)
    ds = P.salloc(f"l0d_{uu}", [128, NF * 512], F32)
    pbo = [P.palloc() for _ in range(2)]
    pbmu = P.palloc()
    pbrs = P.palloc()
    for tc in range(T // 512):
        P.add("pe", lambda e, t=tc, b=pbmu: e.matmul(
            b[:], C.ones_row[:], mu[0:1, t * 512:(t + 1) * 512],
            start=True, stop=True), ins=[C.ones_row, mu], outs=[pbmu])
        P.add("pe", lambda e, t=tc, b=pbrs: e.matmul(
            b[:], C.ones_row[:], rstd[0:1, t * 512:(t + 1) * 512],
            start=True, stop=True), ins=[C.ones_row, rstd], outs=[pbrs])
        for hc in range(NH):
            h_sb = hin[hc % 2]
            P.dma("sp", h_sb[:], h_d[hc * 128:(hc + 1) * 128,
                                     tc * 512:(tc + 1) * 512])
            P.add("dve", lambda e, h=h_sb, c=hc: e.tensor_sub(
                hna[:, c * 512:(c + 1) * 512], h[:], pbmu[:]),
                ins=[h_sb, pbmu], outs=[hna])
            P.add("dve", lambda e, c=hc: e.tensor_mul(
                hna[:, c * 512:(c + 1) * 512], hna[:, c * 512:(c + 1) * 512],
                pbrs[:]), ins=[hna, pbrs], outs=[hna])
        for oc in range(NF):
            bank = pbo[oc % 2]
            for hc in range(NH):
                P.add("pe", lambda e, h=hc, o=oc, b=bank,
                      s=(hc == 0), p=(hc == NH - 1):
                      e.matmul(b[:], wslice(Wqp, D, h, o * 128, 128),
                               hna[:, h * 512:(h + 1) * 512], start=s, stop=p),
                      ins=[Wqp, hna], outs=[bank])
            P.add("dve", lambda e, o=oc, b=bank: e.tensor_scalar(
                out=ds[:, o * 512:(o + 1) * 512], in0=b[:],
                scalar1=bias_fm[:, o:o + 1], scalar2=None, op0=ALU.add),
                ins=[bank, bias_fm], outs=[ds])
        emit_delta_residual_store(C, xin, xout, ds,
                                  tt_range=range(tc * 4, tc * 4 + 4),
                                  tag=tag, tstride=512, t_base=tc * 512)
    for b in pbo + [pbmu, pbrs]:
        P.pfree(b)
    for t_ in hin + [hna, ds, bias_fm, lnb_bf, Wqp, lnw, lnb, mu, rstd]:
        P.sfree(t_)


def emit_l1_ffn(C, xin, xout, tag=""):
    P = C.P
    uu = P.uid()
    u = emit_outer_ln(C, xin, want_fm=True, want_lo_fm=True, tag=tag)
    ufm, ulofm = u["u_fm"], u["ulo_fm"]
    Rf = P.salloc(f"l1rf_{uu}", [128, NF * 8], F32)
    Rhi = P.salloc(f"l1rh_{uu}", [128, NF * 8], BF16)
    Rlo = P.salloc(f"l1rl_{uu}", [128, NF * 8], BF16)
    P.dma("sp", Rf[:].rearrange("p (c o) -> p c o", c=NF),
          C.win["L1_ffn_router_w"].rearrange("(c p) o -> p c o", p=128))
    P.dma("pool", Rhi[:].rearrange("p (c o) -> p c o", c=NF),
          C.win["L1_ffn_router_w"].rearrange("(c p) o -> p c o", p=128))
    P.add("dve", lambda e: e.tensor_sub(Rlo[:], Rf[:], Rhi[:]),
          ins=[Rf, Rhi], outs=[Rlo])
    rb_row = P.salloc(f"l1rbr_{uu}", [1, 8], F32)
    P.dma("sp", rb_row[:], C.win["L1_ffn_router_b"].unsqueeze(0))
    rb_big = P.salloc(f"l1rbb_{uu}", [128, 8], F32)
    bank = P.palloc()
    P.add("pe", lambda e, b=bank: e.matmul(b[:, 0:8], C.ones_row[:], rb_row[:],
                                           start=True, stop=True),
          ins=[C.ones_row, rb_row], outs=[bank])
    P.add("dve", lambda e, b=bank: e.tensor_copy(out=rb_big[:], in_=b[:, 0:8]),
          ins=[bank], outs=[rb_big])
    P.pfree(bank)
    wta = P.salloc(f"l1wa_{uu}", [128, NT * 8], F32)
    ltm = P.salloc(f"l1lt_{uu}", [128, 8], F32)
    gnm = P.salloc(f"l1gn_{uu}", [128, 1], F32)
    gs = P.salloc(f"l1gs_{uu}", [128, 1], F32)
    wfm = P.salloc(f"l1wf_{uu}", [8, T], BF16)
    pbl = [P.palloc() for _ in range(2)]
    for tt in range(NT):
        bank = pbl[tt % 2]
        first = True
        for fc in range(NF):
            for uf, rw in ((ufm, Rhi), (ufm, Rlo), (ulofm, Rhi)):
                P.add("pe", lambda e, f=fc, t=tt, uf=uf, rw=rw, b=bank,
                      s=first, p=(fc == NF - 1 and rw is Rhi and uf is ulofm):
                      e.matmul(b[:, 0:8], fmv(uf, f, t * 128, 128),
                               wslice(rw, 8, f, 0, 8), start=s, stop=p),
                      ins=[uf, rw], outs=[bank])
                first = False
        P.add("dve", lambda e, b=bank: e.tensor_add(ltm[:], b[:, 0:8], rb_big[:]),
              ins=[bank, rb_big], outs=[ltm])
        P.add("dve", lambda e: e.reduce_max(gnm[:], ltm[:], axis=AX.X, negate=True),
              ins=[ltm], outs=[gnm])
        P.add("act", lambda e: e.activation(out=ltm[:], in_=ltm[:], func=AF.Exp,
                                            bias=gnm[:], accum_out=gs[:]),
              ins=[ltm, gnm], outs=[ltm, gs])
        P.add("dve", lambda e: e.reciprocal(gs[:], gs[:]), ins=[gs], outs=[gs])
        P.add("dve", lambda e: e.tensor_scalar_mul(ltm[:], ltm[:], gs[:]),
              ins=[ltm, gs], outs=[ltm])
        P.add("dve", lambda e: e.reduce_max(gnm[:], ltm[:], axis=AX.X),
              ins=[ltm], outs=[gnm])
        P.add("dve", lambda e, t=tt: e.tensor_scalar(
            out=wta[:, t * 8:(t + 1) * 8], in0=ltm[:], scalar1=gnm[:],
            scalar2=gnm[:], op0=ALU.is_ge, op1=ALU.mult),
            ins=[ltm, gnm], outs=[wta])
        P.add("pe", lambda e, t=tt, b=bank: e.transpose(
            b[0:8, 0:128], wta[:, t * 8:(t + 1) * 8], C.ident_f32[:]),
            ins=[wta, C.ident_f32], outs=[bank])
        P.add("dve", lambda e, t=tt, b=bank: e.tensor_copy(
            out=wfm[:, t * 128:(t + 1) * 128], in_=b[0:8, 0:128]),
            ins=[bank], outs=[wfm])
    for b in pbl:
        P.pfree(b)
    P.sfree(ulofm)
    for t_ in (Rf, Rhi, Rlo, rb_row, rb_big, ltm, gnm, gs):
        P.sfree(t_)
    moe_d = C.scratch_dram(f"l1moe_{uu}", [D, T], F32)
    Bst = P.salloc(f"l1bst_{uu}", [8, D], BF16)
    P.dma("pool", Bst[:], C.win["L1_ffn_expert_b"])
    wexp = P.salloc(f"l1we_{uu}", [128, 128], F32)
    wbig = P.salloc(f"l1wb_{uu}", [128, 512], F32)
    stage = [P.salloc(f"l1st_{uu}_{i}", [128, 512], F32) for i in range(2)]
    pbw = P.palloc()
    pbo = [P.palloc() for _ in range(2)]
    for tc in range(T // 512):
        for oc in range(NF):
            bank = pbo[oc % 2]
            st = stage[oc % 2]
            P.add("pe", lambda e, o=oc, t=tc, b=bank: e.matmul(
                b[:], Bst[:, o * 128:(o + 1) * 128],
                wfm[:, t * 512:(t + 1) * 512], start=True, stop=True),
                ins=[Bst, wfm], outs=[bank])
            P.add("dve", lambda e, b=bank, st=st: e.tensor_copy(
                out=st[:], in_=b[:]), ins=[bank], outs=[st])
            P.dma("pool", moe_d[oc * 128:(oc + 1) * 128, tc * 512:(tc + 1) * 512],
                  st[:], out_key=f"l1moe_{uu}_{oc}_{tc}")
    for e_ in range(8):
        We = load_weight(C, f"l1w{e_}_{uu}", D, D,
                         dram_ap=C.win["L1_ffn_expert_w"][e_ * D:(e_ + 1) * D, :])
        for tc in range(T // 512):
            for i in range(4):
                tt = tc * 4 + i
                eap = AP(wta, tt * 8 + e_, [[NT * 8, 128], [0, 128]])
                P.add("dve", lambda e, a=eap: e.tensor_copy(out=wexp[:], in_=a),
                      ins=[wta], outs=[wexp])
                P.add("pe", lambda e, i=i, b=pbw: e.transpose(
                    b[:, i * 128:(i + 1) * 128], wexp[:], C.ident_f32[:]),
                    ins=[wexp, C.ident_f32], outs=[pbw])
            P.add("dve", lambda e, b=pbw: e.tensor_copy(out=wbig[:], in_=b[:]),
                  ins=[pbw], outs=[wbig])
            for oc in range(NF):
                bank = pbo[oc % 2]
                st = stage[oc % 2]
                for fc in range(NF):
                    P.add("pe", lambda e, f=fc, o=oc, t=tc, b=bank, W=We,
                          s=(fc == 0), p=(fc == NF - 1):
                          e.matmul(b[:], wslice(W, D, f, o * 128, 128),
                                   fmv(ufm, f, t * 512, 512), start=s, stop=p),
                          ins=[We, ufm], outs=[bank])
                P.add("dve", lambda e, b=bank, st=st: e.tensor_mul(
                    st[:], b[:], wbig[:]), ins=[bank, wbig], outs=[st])
                P.dma("pool",
                      moe_d[oc * 128:(oc + 1) * 128, tc * 512:(tc + 1) * 512],
                      st[:], accum_op=ALU.add, out_key=f"l1moe_{uu}_{oc}_{tc}")
        P.sfree(We)
    for b in pbo + [pbw]:
        P.pfree(b)
    for t_ in (wta, wfm, Bst, wexp, wbig) + tuple(stage):
        P.sfree(t_)
    P.sfree(ufm)
    ds = P.salloc(f"l1d_{uu}", [128, NF * 512], F32)
    for tc in range(T // 512):
        for oc in range(NF):
            P.dma("sp", ds[:, oc * 512:(oc + 1) * 512],
                  moe_d[oc * 128:(oc + 1) * 128, tc * 512:(tc + 1) * 512],
                  in_key=f"l1moe_{uu}_{oc}_{tc}")
        emit_delta_residual_store(C, xin, xout, ds,
                                  tt_range=range(tc * 4, tc * 4 + 4),
                                  tag=tag, tstride=512, t_base=tc * 512)
    P.sfree(ds)


def emit_row_broadcast(C, name, n, uu):
    P = C.P
    row = P.salloc(f"rb_{name}_{uu}", [1, n], F32)
    P.dma("sp", row[:], C.win[name].unsqueeze(0))
    big = P.salloc(f"rbb_{name}_{uu}", [128, n], F32)
    bank = P.palloc()
    for c0 in range(0, n, 512):
        cw = min(512, n - c0)
        P.add("pe", lambda e, c0=c0, cw=cw, b=bank: e.matmul(
            b[:, 0:cw], C.ones_row[:], row[0:1, c0:c0 + cw],
            start=True, stop=True), ins=[C.ones_row, row], outs=[bank])
        P.add("dve", lambda e, c0=c0, cw=cw, b=bank: e.tensor_copy(
            out=big[:, c0:c0 + cw], in_=b[:, 0:cw]), ins=[bank], outs=[big])
    P.pfree(bank)
    P.sfree(row)
    return big




def emit_ln_affine_tile(C, src_ap, src_t, dst_ap, dst_t, gbig, bbig, scr):
    P = C.P
    s1, s2, s3, dump = scr["s1"], scr["s2"], scr["s3"], scr["dump"]
    P.add("dve", lambda e: e.reduce_sum(s1[:], src_ap, axis=AX.X),
          ins=[src_t], outs=[s1])
    P.add("act", lambda e: e.activation(out=dump[:], in_=src_ap, func=AF.Square,
                                        accum_out=s2[:]),
          ins=[src_t], outs=[dump, s2])
    P.add("dve", lambda e: e.tensor_scalar_mul(s1[:], s1[:], 1.0 / D),
          ins=[s1], outs=[s1])
    P.add("dve", lambda e: e.tensor_mul(s3[:], s1[:], s1[:]), ins=[s1], outs=[s3])
    P.add("dve", lambda e: e.scalar_tensor_tensor(
        s2[:], s2[:], 1.0 / D, s3[:], ALU.mult, ALU.subtract),
        ins=[s2, s3], outs=[s2])
    P.add("act", lambda e: e.activation(out=s2[:], in_=s2[:], func=AF.Ln,
                                        bias=constf(C, LN_EPS)[:]),
          ins=[s2, constf(C, LN_EPS)], outs=[s2])
    P.add("act", lambda e: e.activation(out=s2[:], in_=s2[:], func=AF.Exp,
                                        scale=constf(C, -0.5)[:]),
          ins=[s2, constf(C, -0.5)], outs=[s2])
    P.add("dve", lambda e: e.scalar_tensor_tensor(
        s3[:], s1[:], -1.0, s2[:], ALU.mult, ALU.mult), ins=[s1, s2], outs=[s3])
    P.add("dve", lambda e: e.tensor_scalar(
        out=dst_ap, in0=src_ap, scalar1=s2[:], scalar2=s3[:],
        op0=ALU.mult, op1=ALU.add), ins=[src_t, s2, s3], outs=[dst_t])
    P.add("dve", lambda e: e.tensor_mul(dst_ap, dst_ap, gbig[:]),
          ins=[dst_t, gbig], outs=[dst_t])
    P.add("dve", lambda e: e.tensor_add(dst_ap, dst_ap, bbig[:]),
          ins=[dst_t, bbig], outs=[dst_t])


def emit_l0_attn(C, xin, xout, tag=""):
    P = C.P
    uu = P.uid()
    HID = 4096
    NH32 = HID // 128
    onesb = emit_ones_col_bf(C)
    u = emit_outer_ln(C, xin, want_fm=True, want_tm=True, tag=tag)
    ufm, utm = u["u_fm"], u["u_tm"]
    Wkqv = load_weight(C, "L0_attn_wkqv", D, 3072)
    bq = load_bias(C, "L0_attn_bq", D)
    bk = load_bias(C, "L0_attn_bk", D)
    P.add("dve", lambda e: e.tensor_scalar_mul(bq[:], bq[:], 0.125),
          ins=[bq], outs=[bq])
    bvbig32 = emit_row_broadcast(C, "L0_attn_bv", D, uu)
    bvbig = P.salloc(f"l0bvb_{uu}", [128, D], BF16)
    P.add("dve", lambda e: e.tensor_copy(out=bvbig[:], in_=bvbig32[:]),
          ins=[bvbig32], outs=[bvbig])
    P.sfree(bvbig32)
    qd, kd, vd = emit_qkv_spill(C, ufm, Wkqv, uu, bq=bq, bk=bk, bvbig=bvbig)
    P.sfree(ufm)
    P.sfree(Wkqv)
    for t_ in (bq, bk, bvbig):
        P.sfree(t_)
    u_d = C.scratch_dram(f"l0ud_{uu}", [128, NT * D], BF16)
    P.dma("sp", u_d[:, :], utm[:])
    P.sfree(utm)
    NKT = SEQ // 128
    BB = []
    for par in range(2):
        BB.append(dict(
            qsb=P.salloc(f"l0q_{uu}_{par}", [64, SEQ], BF16),
            ksb=P.salloc(f"l0k_{uu}_{par}", [64, SEQ], BF16),
            sq=P.salloc(f"l0sqt_{uu}_{par}", [64, 512], BF16),
            nrm=P.salloc(f"l0n_{uu}_{par}", [1, 2 * SEQ], F32),
            c0n=P.salloc(f"l0c0_{uu}_{par}", [1, 1], F32),
            c0big=P.salloc(f"l0c0b_{uu}_{par}", [128, 1], F32),
            ptil=P.salloc(f"l0p_{uu}_{par}", [128, NKT * 512], BF16),
            rsum=P.salloc(f"l0rsm_{uu}_{par}", [1, 512], F32),
            rbig=P.salloc(f"l0rbg_{uu}_{par}", [64, 512], F32),
            pbS=[P.palloc() for _ in range(2)],
            pbA=P.palloc(),
            pbrb=P.palloc(),
        ))
    vtmb = P.salloc(f"l0v_{uu}", [128, NKT * 16 * 65], BF16)
    afm = P.salloc(f"l0a_{uu}", [128, NF * T], BF16)

    def head_body(b_, h):
        B = BB[h % 2]
        qsb, ksb, sq, nrm = B["qsb"], B["ksb"], B["sq"], B["nrm"]
        c0n, c0big, ptil = B["c0n"], B["c0big"], B["ptil"]
        rsum, rbig = B["rsum"], B["rbig"]
        pbS, pbA, pbrb = B["pbS"], B["pbA"], B["pbrb"]
        bank_n = pbA
        P.dma("sp", qsb[:], qd[h * 64:(h + 1) * 64, b_ * SEQ:(b_ + 1) * SEQ])
        P.dma("sp", ksb[:], kd[h * 64:(h + 1) * 64, b_ * SEQ:(b_ + 1) * SEQ])
        for side, src_ in ((0, qsb), (1, ksb)):
            for tc in range(SEQ // 512):
                P.add("act", lambda e, s=src_, t=tc, sq=sq: e.activation(
                    out=sq[:], in_=s[:, t * 512:(t + 1) * 512], func=AF.Square),
                    ins=[src_], outs=[sq])
                P.add("pe", lambda e, bn=bank_n, sq=sq: e.matmul(
                    bn[0:1, :], onesb[0:64, :], sq[:], start=True, stop=True),
                    ins=[onesb, sq], outs=[bank_n])
                P.add("dve", lambda e, sd=side, t=tc, bn=bank_n, nrm=nrm:
                      e.tensor_copy(
                          out=nrm[0:1, sd * SEQ + t * 512: sd * SEQ + (t + 1) * 512],
                          in_=bn[0:1, :]), ins=[bank_n], outs=[nrm])
        P.add("dve", lambda e, nrm=nrm, c0n=c0n: e.reduce_max(
            c0n[:], nrm[0:1, 0:SEQ], axis=AX.X), ins=[nrm], outs=[c0n])
        P.add("dve", lambda e, nrm=nrm: e.reduce_max(
            nrm[0:1, 0:1], nrm[0:1, SEQ:2 * SEQ], axis=AX.X),
            ins=[nrm], outs=[nrm])
        P.add("dve", lambda e, nrm=nrm, c0n=c0n: e.tensor_mul(
            c0n[:], c0n[:], nrm[0:1, 0:1]), ins=[c0n, nrm], outs=[c0n])
        P.add("act", lambda e, c0n=c0n: e.activation(
            out=c0n[:], in_=c0n[:], func=AF.Ln), ins=[c0n], outs=[c0n])
        P.add("act", lambda e, c0n=c0n: e.activation(
            out=c0n[:], in_=c0n[:], func=AF.Exp, scale=constf(C, 0.5)[0:1, :]),
            ins=[c0n, constf(C, 0.5)], outs=[c0n])
        P.add("dve", lambda e, c0n=c0n: e.tensor_scalar_mul(c0n[:], c0n[:], -1.0),
              ins=[c0n], outs=[c0n])
        P.add("pe", lambda e, bn=bank_n, c0n=c0n: e.matmul(
            bn[:, 0:1], C.ones_row[:], c0n[:], start=True, stop=True),
            ins=[C.ones_row, c0n], outs=[bank_n])
        P.add("dve", lambda e, bn=bank_n, c0big=c0big: e.tensor_copy(
            out=c0big[:], in_=bn[:, 0:1]), ins=[bank_n], outs=[c0big])
        for qch in range(SEQ // 512):
            kt_lo = max(0, qch * 4 - 2)
            kt_hi = min(NKT, qch * 4 + 6)
            for kt in range(kt_lo, kt_hi):
                bank = pbS[kt % 2]
                P.add("pe", lambda e, k=kt, q=qch, b=bank, qsb=qsb, ksb=ksb:
                      e.matmul(b[:], ksb[:, k * 128:(k + 1) * 128],
                               qsb[:, q * 512:(q + 1) * 512],
                               start=True, stop=True),
                      ins=[ksb, qsb], outs=[bank])
                P.add("act", lambda e, b=bank, k=kt, ptil=ptil, c0big=c0big:
                      e.activation(
                          out=ptil[:, k * 512:(k + 1) * 512], in_=b[:],
                          func=AF.Exp, bias=c0big[:]),
                      ins=[bank, c0big], outs=[ptil])
                P.add("pool", lambda g, k=kt, q=qch, ptil=ptil: g.affine_select(
                    out=ptil[:, k * 512:(k + 1) * 512],
                    in_=ptil[:, k * 512:(k + 1) * 512],
                    pattern=[[1, 512]], compare_op=ALU.is_ge, fill=0.0,
                    base=q * 512 - k * 128 + 256, channel_multiplier=-1),
                    ins=[ptil], outs=[ptil])
                P.add("pool", lambda g, k=kt, q=qch, ptil=ptil: g.affine_select(
                    out=ptil[:, k * 512:(k + 1) * 512],
                    in_=ptil[:, k * 512:(k + 1) * 512],
                    pattern=[[-1, 512]], compare_op=ALU.is_ge, fill=0.0,
                    base=-(q * 512) + k * 128 + 256, channel_multiplier=1),
                    ins=[ptil], outs=[ptil])
            nk = kt_hi - kt_lo
            for j, kt in enumerate(range(kt_lo, kt_hi)):
                P.add("pe", lambda e, k=kt, h=h, b=pbA, ptil=ptil,
                      s=(j == 0), p=(j == nk - 1):
                      e.matmul(b[0:65, :],
                               vtmb[:, k * 1040 + h * 65: k * 1040 + (h + 1) * 65],
                               ptil[:, k * 512:(k + 1) * 512], start=s, stop=p),
                      ins=[vtmb, ptil], outs=[pbA])
            P.add("dve", lambda e, b=pbA, rsum=rsum: e.tensor_copy(
                out=rsum[:], in_=b[64:65, :]), ins=[pbA], outs=[rsum])
            P.add("dve", lambda e, rsum=rsum: e.reciprocal(rsum[:], rsum[:]),
                  ins=[rsum], outs=[rsum])
            P.add("pe", lambda e, pb_=pbrb, rsum=rsum: e.matmul(
                pb_[0:64, :], C.ones_row[0:1, 0:64], rsum[:],
                start=True, stop=True), ins=[C.ones_row, rsum], outs=[pbrb])
            P.add("dve", lambda e, pb_=pbrb, rbig=rbig: e.tensor_copy(
                out=rbig[:], in_=pb_[0:64, :]), ins=[pbrb], outs=[rbig])
            oc = h // 2
            pb0 = (h % 2) * 64
            tg = b_ * 4 + qch
            P.add("dve", lambda e, b=pbA, o=oc, p0=pb0, q=qch, bb=b_, rbig=rbig:
                  e.tensor_mul(
                      afm[p0:p0 + 64,
                          o * T + bb * SEQ + q * 512: o * T + bb * SEQ + (q + 1) * 512],
                      b[0:64, :], rbig[:]),
                  ins=[pbA, rbig], outs=[f"l0afm{uu}_{h}_{tg}"])

    for b_ in range(BPC):
        P.dma("sp", vtmb[:], vd[:, b_ * NKT * 1040:(b_ + 1) * NKT * 1040])
        for h in range(16):
            head_body(b_, h)
    for B in BB:
        for k_ in ("qsb", "ksb", "sq", "nrm", "c0n", "c0big", "ptil", "rsum",
                   "rbig"):
            P.sfree(B[k_])
        for bb2 in B["pbS"] + [B["pbA"], B["pbrb"]]:
            P.pfree(bb2)
    P.sfree(vtmb)
    a_d = C.scratch_dram(f"l0ad_{uu}", [128, NF * T], BF16)
    P.dma("sp", a_d[:, :], afm[:],
          in_key=[f"l0afm{uu}_{h}_{t}" for h in range(16) for t in range(8)])
    P.sfree(afm)
    hfm = P.salloc(f"l0hf_{uu}", [128, NF * T], BF16)
    W1 = load_weight(C, "L0_attn_W1", D, 4096)
    Wo = load_weight(C, "L0_attn_Wo", D, D)
    bobig = emit_row_broadcast(C, "L0_attn_bo", D, uu)
    g1big = emit_row_broadcast(C, "L0_attn_g1", D, uu)
    b1big = emit_row_broadcast(C, "L0_attn_b1", D, uu)
    h_d = C.scratch_dram(f"l0hd_{uu}", [128, NT * D], BF16)
    afs = P.salloc(f"l0afs_{uu}", [128, NF * 512], BF16)
    a2s = P.salloc(f"l0a2_{uu}", [128, NF * 512], BF16)
    hpre = P.salloc(f"l0hp_{uu}", [128, D], F32)
    htile = P.salloc(f"l0htl_{uu}", [128, D], BF16)
    utr = P.salloc(f"l0utr_{uu}", [128, D], BF16)
    scr = {"s1": P.salloc(f"l0s1_{uu}", [128, 1], F32),
           "s2": P.salloc(f"l0s2_{uu}", [128, 1], F32),
           "s3": P.salloc(f"l0s3_{uu}", [128, 1], F32),
           "dump": P.salloc(f"l0du_{uu}", [128, D], BF16)}
    pbo = [P.palloc() for _ in range(2)]
    pbt = [P.palloc() for _ in range(2)]
    pbh = [P.palloc() for _ in range(2)]
    for tc in range(T // 512):
        for fc in range(NF):
            P.dma("sp", afs[:, fc * 512:(fc + 1) * 512],
                  a_d[:, fc * T + tc * 512: fc * T + (tc + 1) * 512])
        for oc in range(NF):
            bank = pbo[oc % 2]
            for fc in range(NF):
                P.add("pe", lambda e, f=fc, o=oc, b=bank,
                      s=(fc == 0), p=(fc == NF - 1):
                      e.matmul(b[:], wslice(Wo, D, f, o * 128, 128),
                               afs[:, f * 512:(f + 1) * 512], start=s, stop=p),
                      ins=[Wo, afs], outs=[bank])
            P.add("dve", lambda e, o=oc, b=bank: e.tensor_copy(
                out=a2s[:, o * 512:(o + 1) * 512], in_=b[:]),
                ins=[bank], outs=[a2s])
        for i in range(4):
            tt = tc * 4 + i
            bank = pbt[i % 2]
            bb = bank.ap().bitcast(BF16)
            for fc in range(NF):
                P.add("pe", lambda e, f=fc, i=i, b=bb: e.transpose(
                    b[:, f * 128:(f + 1) * 128],
                    a2s[:, f * 512 + i * 128: f * 512 + (i + 1) * 128],
                    C.ident_bf[:]), ins=[a2s, C.ident_bf], outs=[bank])
            P.dma("sp", utr[:], u_d[:, tt * D:(tt + 1) * D])
            P.add("dve", lambda e, b=bb: e.tensor_add(
                hpre[:], b[:, 0:D], utr[:]),
                ins=[bank, utr], outs=[hpre])
            P.add("dve", lambda e: e.tensor_add(hpre[:], hpre[:], bobig[:]),
                  ins=[hpre, bobig], outs=[hpre])
            emit_ln_affine_tile(C, hpre[:], hpre, htile[:], htile,
                                g1big, b1big, scr)
            P.dma("sp", h_d[:, tt * D:(tt + 1) * D], htile[:],
                  out_key=f"l0h_{uu}_{tt}")
            bank2 = pbh[i % 2]
            bh = bank2.ap().bitcast(BF16)
            for fc in range(NF):
                P.add("pe", lambda e, f=fc, b=bh: e.transpose(
                    b[:, f * 128:(f + 1) * 128], htile[:, f * 128:(f + 1) * 128],
                    C.ident_bf[:]), ins=[htile, C.ident_bf], outs=[bank2])
            ofm = hfm.ap().rearrange("p (c t) -> p c t", c=NF)[
                :, :, tt * 128:(tt + 1) * 128]
            ib2 = bh.rearrange("p (c t) -> p c t", c=NF)
            P.add("dve", lambda e, o=ofm, i2=ib2: e.tensor_copy(out=o, in_=i2),
                  ins=[bank2], outs=[hfm])
    for b in pbo + pbt + pbh:
        P.pfree(b)
    for t_ in (afs, a2s, hpre, htile, utr, Wo, bobig, g1big, b1big):
        P.sfree(t_)
    c1 = load_bias(C, "L0_attn_c1", HID)
    f1_d = C.scratch_dram(f"l0f1_{uu}", [HID, T], BF16)
    f1s = [P.salloc(f"l0f1s_{uu}_{i}", [128, 512], BF16) for i in range(2)]
    pb1 = [P.palloc() for _ in range(2)]
    for tc in range(T // 512):
        for oc in range(NH32):
            bank = pb1[oc % 2]
            st = f1s[oc % 2]
            for fc in range(NF):
                P.add("pe", lambda e, f=fc, o=oc, t=tc, b=bank,
                      s=(fc == 0), p=(fc == NF - 1):
                      e.matmul(b[:], wslice(W1, HID, f, o * 128, 128),
                               fmv(hfm, f, t * 512, 512), start=s, stop=p),
                      ins=[W1, hfm], outs=[bank])
            P.add("act", lambda e, b=bank, st=st, o=oc: e.activation(
                out=st[:], in_=b[:], func=AF.Gelu_apprx_tanh,
                bias=c1[:, o:o + 1]), ins=[bank, c1], outs=[st])
            P.dma("sp", f1_d[oc * 128:(oc + 1) * 128, tc * 512:(tc + 1) * 512],
                  st[:], out_key=f"l0f1_{uu}_{oc}_{tc}")
    for b in pb1:
        P.pfree(b)
    for t_ in f1s + [W1, c1, hfm]:
        P.sfree(t_)
    W2 = load_weight(C, "L0_attn_W2", HID, D)
    c2 = load_bias(C, "L0_attn_c2", D)
    g2big = emit_row_broadcast(C, "L0_attn_g2", D, uu)
    b2big = emit_row_broadcast(C, "L0_attn_b2", D, uu)
    f1a = [P.salloc(f"l0f1a_{uu}_{i}", [128, NH32 * 512], BF16)
           for i in range(2)]
    f2s = P.salloc(f"l0f2_{uu}", [128, NF * 512], BF16)
    hrt = P.salloc(f"l0hr_{uu}", [128, D], BF16)
    xrt = P.salloc(f"l0xr_{uu}", [128, D], F32)
    f2pre = P.salloc(f"l0fp_{uu}", [128, D], F32)
    outt = P.salloc(f"l0out_{uu}", [128, D], F32)
    scr2 = {"s1": P.salloc(f"l0s1b_{uu}", [128, 1], F32),
            "s2": P.salloc(f"l0s2b_{uu}", [128, 1], F32),
            "s3": P.salloc(f"l0s3b_{uu}", [128, 1], F32),
            "dump": P.salloc(f"l0dub_{uu}", [128, D], BF16)}
    pb2 = [P.palloc() for _ in range(2)]
    pbt2 = [P.palloc() for _ in range(2)]
    P.add("dve", lambda e: e.engine_nop(), ins=[], outs=[f1a[0], f1a[1]])
    for tc in range(T // 512):
        fa = f1a[tc % 2]
        for hc in range(NH32):
            P.dma("sp" if hc % 2 == 0 else "pool",
                  fa[:, hc * 512:(hc + 1) * 512],
                  f1_d[hc * 128:(hc + 1) * 128, tc * 512:(tc + 1) * 512],
                  ins=[fa],
                  out_key=f"l0f1a_{uu}_{tc % 2}_{hc}",
                  in_key=f"l0f1_{uu}_{hc}_{tc}")
        for oc in range(NF):
            bank = pb2[oc % 2]
            for hc in range(NH32):
                P.add("pe", lambda e, b=bank, fa=fa, o=oc, h2=hc,
                      s=(hc == 0), p=(hc == NH32 - 1):
                      e.matmul(b[:], wslice(W2, D, h2, o * 128, 128),
                               fa[:, h2 * 512:(h2 + 1) * 512], start=s, stop=p),
                      ins=[W2, f"l0f1a_{uu}_{tc % 2}_{hc}"], outs=[bank])
            P.add("dve", lambda e, o=oc, b=bank: e.tensor_scalar(
                out=f2s[:, o * 512:(o + 1) * 512], in0=b[:],
                scalar1=c2[:, o:o + 1], scalar2=None, op0=ALU.add),
                ins=[bank, c2], outs=[f2s])
        for i in range(4):
            tt = tc * 4 + i
            bank = pbt2[i % 2]
            bb = bank.ap().bitcast(BF16)
            for fc in range(NF):
                P.add("pe", lambda e, f=fc, i=i, b=bb: e.transpose(
                    b[:, f * 128:(f + 1) * 128],
                    f2s[:, f * 512 + i * 128: f * 512 + (i + 1) * 128],
                    C.ident_bf[:]), ins=[f2s, C.ident_bf], outs=[bank])
            P.dma("sp", hrt[:], h_d[:, tt * D:(tt + 1) * D],
                  in_key=f"l0h_{uu}_{tt}")
            P.add("dve", lambda e, b=bb: e.tensor_add(
                f2pre[:], b[:, 0:D], hrt[:]), ins=[bank, hrt], outs=[f2pre])
            emit_ln_affine_tile(C, f2pre[:], f2pre, outt[:], outt,
                                g2big, b2big, scr2)
            P.dma("sp", xrt[:], xin[tt * 128:(tt + 1) * 128, :])
            P.add("dve", lambda e: e.tensor_add(outt[:], outt[:], xrt[:]),
                  ins=[outt, xrt], outs=[outt])
            P.dma("sp", xout[tt * 128:(tt + 1) * 128, :], outt[:], tag=tag)
    P.add("dve", lambda e: e.engine_nop(),
          ins=[f"l0f1a_{uu}_{p}_{h}" for p in range(2) for h in range(NH32)],
          outs=[f1a[0], f1a[1]])
    for b in pb2 + pbt2:
        P.pfree(b)
    for t_ in f1a + [f2s, hrt, xrt, f2pre, outt, W2, c2, g2big, b2big]:
        P.sfree(t_)
    for t_ in scr.values():
        P.sfree(t_)
    for t_ in scr2.values():
        P.sfree(t_)


def emit_rms_fm(C, src_fm, dst_fm, wcol, uu, eps, tagn, lo_cb=None):
    P = C.P
    sq = P.salloc(f"rms_sq_{tagn}_{uu}", [128, 512], BF16)
    rrow = P.salloc(f"rms_r_{tagn}_{uu}", [1, T], F32)
    onesb = emit_ones_col_bf(C)
    bank = P.palloc()
    bankb = P.palloc()
    xf32 = P.salloc(f"rms_x_{tagn}_{uu}", [128, 512], F32) if lo_cb else None
    for tc in range(T // 512):
        for fc in range(NF):
            P.add("act", lambda e, f=fc, t=tc: e.activation(
                out=sq[:], in_=fmv(src_fm, f, t * 512, 512), func=AF.Square),
                ins=[src_fm], outs=[sq])
            P.add("pe", lambda e, b=bank, s=(fc == 0), p=(fc == NF - 1):
                  e.matmul(b[0:1, :], onesb[:], sq[:], start=s, stop=p),
                  ins=[onesb, sq], outs=[bank])
        P.add("dve", lambda e, t=tc, b=bank: e.tensor_copy(
            out=rrow[0:1, t * 512:(t + 1) * 512], in_=b[0:1, :]),
            ins=[bank], outs=[rrow])
    P.add("act", lambda e: e.activation(out=rrow[:], in_=rrow[:], func=AF.Ln,
                                        scale=constf(C, 1.0 / D)[0:1, :],
                                        bias=constf(C, eps)[0:1, :]),
          ins=[rrow, constf(C, 1.0 / D), constf(C, eps)], outs=[rrow])
    P.add("act", lambda e: e.activation(out=rrow[:], in_=rrow[:], func=AF.Exp,
                                        scale=constf(C, -0.5)[0:1, :]),
          ins=[rrow, constf(C, -0.5)], outs=[rrow])
    for tc in range(T // 512):
        P.add("pe", lambda e, t=tc, b=bankb: e.matmul(
            b[:], C.ones_row[:], rrow[0:1, t * 512:(t + 1) * 512],
            start=True, stop=True), ins=[C.ones_row, rrow], outs=[bankb])
        for fc in range(NF):
            if lo_cb is not None:
                P.add("dve", lambda e, f=fc, t=tc, b=bankb: e.scalar_tensor_tensor(
                    xf32[:], fmv(src_fm, f, t * 512, 512), wcol[:, f:f + 1],
                    b[:], ALU.mult, ALU.mult),
                    ins=[src_fm, wcol, bankb], outs=[xf32])
                P.add("dve", lambda e, f=fc, t=tc: e.tensor_copy(
                    out=fmv(dst_fm, f, t * 512, 512), in_=xf32[:]),
                    ins=[xf32], outs=[dst_fm])
                lo_cb(fc, tc, xf32)
            else:
                P.add("dve", lambda e, f=fc, t=tc, b=bankb: e.scalar_tensor_tensor(
                    fmv(dst_fm, f, t * 512, 512), fmv(src_fm, f, t * 512, 512),
                    wcol[:, f:f + 1], b[:], ALU.mult, ALU.mult),
                    ins=[src_fm, wcol, bankb], outs=[dst_fm])
    P.pfree(bank)
    P.pfree(bankb)
    P.sfree(sq)
    P.sfree(rrow)
    if xf32 is not None:
        P.sfree(xf32)


def emit_l2_ffn(C, xin, xout, tag=""):
    P = C.P
    uu = P.uid()
    NKT = SEQ // 128
    onesb = emit_ones_col_bf(C)
    u_d = C.scratch_dram(f"l2f_ud_{uu}", [T, D], F32)
    u = emit_outer_ln(C, xin, want_fm=True, u32_dram=u_d, tag=tag)
    ufm = u["u_fm"]
    n1c = load_bias(C, "L2_ffn_n1_w", D)
    hq = P.salloc(f"l2f_hq_{uu}", [128, NF * T], BF16)
    emit_rms_fm(C, ufm, hq, n1c, uu, 1.1920929e-07, "n1")
    P.sfree(n1c)
    Wkqv = load_weight(C, "L2_ffn_wkqv", D, 3072)
    qd, kd, vd = emit_qkv_spill(C, hq, Wkqv, uu, scale_q=1.0)
    P.sfree(hq)
    P.sfree(Wkqv)
    qn_c = P.salloc(f"l2f_qn_{uu}", [64, 1], F32)
    kn_c = P.salloc(f"l2f_kn_{uu}", [64, 1], F32)
    P.dma("sp", qn_c[:], C.win["L2_ffn_qn_w"].unsqueeze(1))
    P.dma("sp", kn_c[:], C.win["L2_ffn_kn_w"].unsqueeze(1))
    P.add("dve", lambda e: e.tensor_scalar_mul(qn_c[:], qn_c[:], 0.125),
          ins=[qn_c], outs=[qn_c])
    BB = []
    for par in range(2):
        BB.append(dict(
            qsb=P.salloc(f"l2f_q_{uu}_{par}", [64, SEQ], BF16),
            ksb=P.salloc(f"l2f_k_{uu}_{par}", [64, SEQ], BF16),
            sq=P.salloc(f"l2f_sq_{uu}_{par}", [64, 512], BF16),
            nrm=P.salloc(f"l2f_n_{uu}_{par}", [1, 16], F32),
            c0n=P.salloc(f"l2f_c0_{uu}_{par}", [1, 1], F32),
            c0big=P.salloc(f"l2f_c0b_{uu}_{par}", [128, 1], F32),
            ptil=P.salloc(f"l2f_p_{uu}_{par}", [128, (SEQ // 128) * 512], BF16),
            rsum=P.salloc(f"l2f_rs_{uu}_{par}", [1, 512], F32),
            rbig=P.salloc(f"l2f_rb_{uu}_{par}", [64, 512], F32),
            rrow=P.salloc(f"l2f_rr_{uu}_{par}", [1, SEQ], F32),
            pbS=[P.palloc() for _ in range(2)],
            pbA=P.palloc(),
            pbrb=P.palloc(),
        ))
    vtmb = P.salloc(f"l2f_v_{uu}", [128, NKT * 16 * 65], BF16)
    a_d = C.scratch_dram(f"l2f_ad_{uu}", [128, NF * T], BF16)
    astg = [P.salloc(f"l2f_astg_{uu}_{i}", [128, 512], BF16) for i in range(2)]

    def head_body(b_, h):
        B = BB[h % 2]
        qsb, ksb, sq, nrm = B["qsb"], B["ksb"], B["sq"], B["nrm"]
        c0n, c0big, ptil = B["c0n"], B["c0big"], B["ptil"]
        rsum, rbig, rrow = B["rsum"], B["rbig"], B["rrow"]
        pbS, pbA, pbrb = B["pbS"], B["pbA"], B["pbrb"]
        bank_n, bank_b = pbA, pbrb
        ast = astg[h % 2]
        P.dma("sp", qsb[:], qd[h * 64:(h + 1) * 64, b_ * SEQ:(b_ + 1) * SEQ])
        P.dma("sp", ksb[:], kd[h * 64:(h + 1) * 64, b_ * SEQ:(b_ + 1) * SEQ])
        for src_, wc in ((qsb, qn_c), (ksb, kn_c)):
            for tc in range(SEQ // 512):
                P.add("act", lambda e, s=src_, t=tc, sq=sq: e.activation(
                    out=sq[:], in_=s[:, t * 512:(t + 1) * 512], func=AF.Square),
                    ins=[src_], outs=[sq])
                P.add("pe", lambda e, bn=bank_n, sq=sq: e.matmul(
                    bn[0:1, :], onesb[0:64, :], sq[:], start=True, stop=True),
                    ins=[onesb, sq], outs=[bank_n])
                P.add("dve", lambda e, t=tc, bn=bank_n, rrow=rrow: e.tensor_copy(
                    out=rrow[0:1, t * 512:(t + 1) * 512], in_=bn[0:1, :]),
                    ins=[bank_n], outs=[rrow])
            P.add("act", lambda e, rrow=rrow: e.activation(
                out=rrow[:], in_=rrow[:], func=AF.Ln,
                scale=constf(C, 1.0 / 64.0)[0:1, :],
                bias=constf(C, 1.1920929e-07)[0:1, :]),
                ins=[rrow, constf(C, 1.0 / 64.0), constf(C, 1.1920929e-07)],
                outs=[rrow])
            P.add("act", lambda e, rrow=rrow: e.activation(
                out=rrow[:], in_=rrow[:], func=AF.Exp,
                scale=constf(C, -0.5)[0:1, :]),
                ins=[rrow, constf(C, -0.5)], outs=[rrow])
            for tc in range(SEQ // 512):
                P.add("pe", lambda e, t=tc, bb=bank_b, rrow=rrow: e.matmul(
                    bb[0:64, :], C.ones_row[0:1, 0:64],
                    rrow[0:1, t * 512:(t + 1) * 512], start=True, stop=True),
                    ins=[C.ones_row, rrow], outs=[bank_b])
                P.add("dve", lambda e, s=src_, t=tc, w=wc, bb=bank_b:
                      e.scalar_tensor_tensor(
                          s[:, t * 512:(t + 1) * 512],
                          s[:, t * 512:(t + 1) * 512], w[:], bb[0:64, :],
                          ALU.mult, ALU.mult),
                      ins=[src_, wc, bank_b], outs=[src_])
        for side, src_ in ((0, qsb), (1, ksb)):
            for tc in range(SEQ // 512):
                P.add("act", lambda e, s=src_, t=tc, sq=sq: e.activation(
                    out=sq[:], in_=s[:, t * 512:(t + 1) * 512], func=AF.Square),
                    ins=[src_], outs=[sq])
                P.add("pe", lambda e, bn=bank_n, sq=sq: e.matmul(
                    bn[0:1, :], onesb[0:64, :], sq[:], start=True, stop=True),
                    ins=[onesb, sq], outs=[bank_n])
                P.add("dve", lambda e, sd=side, t=tc, bn=bank_n, nrm=nrm:
                      e.reduce_max(
                          nrm[0:1, sd * 8 + t: sd * 8 + t + 1],
                          bn[0:1, :], axis=AX.X), ins=[bank_n], outs=[nrm])
        P.add("dve", lambda e, nrm=nrm, c0n=c0n: e.reduce_max(
            c0n[:], nrm[0:1, 0:SEQ // 512], axis=AX.X), ins=[nrm], outs=[c0n])
        P.add("dve", lambda e, nrm=nrm: e.reduce_max(
            nrm[0:1, 0:1], nrm[0:1, 8:8 + SEQ // 512], axis=AX.X),
            ins=[nrm], outs=[nrm])
        P.add("dve", lambda e, nrm=nrm, c0n=c0n: e.tensor_mul(
            c0n[:], c0n[:], nrm[0:1, 0:1]), ins=[c0n, nrm], outs=[c0n])
        P.add("act", lambda e, c0n=c0n: e.activation(
            out=c0n[:], in_=c0n[:], func=AF.Ln), ins=[c0n], outs=[c0n])
        P.add("act", lambda e, c0n=c0n: e.activation(
            out=c0n[:], in_=c0n[:], func=AF.Exp, scale=constf(C, 0.5)[0:1, :]),
            ins=[c0n, constf(C, 0.5)], outs=[c0n])
        P.add("dve", lambda e, c0n=c0n: e.tensor_scalar_mul(c0n[:], c0n[:], -1.0),
              ins=[c0n], outs=[c0n])
        P.add("pe", lambda e, bn=bank_n, c0n=c0n: e.matmul(
            bn[:, 0:1], C.ones_row[:], c0n[:], start=True, stop=True),
            ins=[C.ones_row, c0n], outs=[bank_n])
        P.add("dve", lambda e, bn=bank_n, c0big=c0big: e.tensor_copy(
            out=c0big[:], in_=bn[:, 0:1]), ins=[bank_n], outs=[c0big])
        for qch in range(SEQ // 512):
            ktmax = qch * 4 + 4
            for kt in range(ktmax):
                bank = pbS[kt % 2]
                P.add("pe", lambda e, k=kt, q=qch, b=bank, qsb=qsb, ksb=ksb:
                      e.matmul(b[:], ksb[:, k * 128:(k + 1) * 128],
                               qsb[:, q * 512:(q + 1) * 512],
                               start=True, stop=True),
                      ins=[ksb, qsb], outs=[bank])
                P.add("act", lambda e, b=bank, k=kt, ptil=ptil, c0big=c0big:
                      e.activation(
                          out=ptil[:, k * 512:(k + 1) * 512], in_=b[:],
                          func=AF.Exp, bias=c0big[:]),
                      ins=[bank, c0big], outs=[ptil])
                if kt >= qch * 4:
                    P.add("pool", lambda g, k=kt, q=qch, ptil=ptil:
                          g.affine_select(
                              out=ptil[:, k * 512:(k + 1) * 512],
                              in_=ptil[:, k * 512:(k + 1) * 512],
                              pattern=[[1, 512]], compare_op=ALU.is_ge, fill=0.0,
                              base=q * 512 - k * 128, channel_multiplier=-1),
                          ins=[ptil], outs=[ptil])
            for kt in range(ktmax):
                P.add("pe", lambda e, k=kt, h=h, b=pbA, ptil=ptil,
                      s=(kt == 0), p=(kt == ktmax - 1):
                      e.matmul(b[0:65, :],
                               vtmb[:, k * 1040 + h * 65: k * 1040 + (h + 1) * 65],
                               ptil[:, k * 512:(k + 1) * 512], start=s, stop=p),
                      ins=[vtmb, ptil], outs=[pbA])
            P.add("dve", lambda e, b=pbA, rsum=rsum: e.tensor_copy(
                out=rsum[:], in_=b[64:65, :]), ins=[pbA], outs=[rsum])
            P.add("dve", lambda e, rsum=rsum: e.reciprocal(rsum[:], rsum[:]),
                  ins=[rsum], outs=[rsum])
            P.add("pe", lambda e, pb_=pbrb, rsum=rsum: e.matmul(
                pb_[0:64, :], C.ones_row[0:1, 0:64], rsum[:],
                start=True, stop=True), ins=[C.ones_row, rsum], outs=[pbrb])
            P.add("dve", lambda e, pb_=pbrb, rbig=rbig: e.tensor_copy(
                out=rbig[:], in_=pb_[0:64, :]), ins=[pbrb], outs=[rbig])
            oc = h // 2
            pb0 = (h % 2) * 64
            P.add("dve", lambda e, b=pbA, rbig=rbig, ast=ast: e.tensor_mul(
                ast[0:64, :], b[0:64, :], rbig[:]),
                ins=[pbA, rbig], outs=[ast])
            P.dma("sp",
                  a_d[pb0:pb0 + 64,
                      oc * T + b_ * SEQ + qch * 512:
                      oc * T + b_ * SEQ + (qch + 1) * 512],
                  ast[0:64, :],
                  out_key=f"l2fad_{uu}_{oc}_{pb0}_{b_ * 4 + qch}")

    for b_ in range(BPC):
        P.dma("sp", vtmb[:], vd[:, b_ * NKT * 1040:(b_ + 1) * NKT * 1040])
        for h in range(16):
            head_body(b_, h)
    for B in BB:
        for k_ in ("qsb", "ksb", "sq", "nrm", "c0n", "c0big", "ptil", "rsum",
                   "rbig", "rrow"):
            P.sfree(B[k_])
        for bb2 in B["pbS"] + [B["pbA"], B["pbrb"]]:
            P.pfree(bb2)
    for t_ in (qn_c, kn_c, vtmb):
        P.sfree(t_)
    for t_ in astg:
        P.sfree(t_)
    Wo = load_weight(C, "L2_ffn_wo", D, D)
    ao_d = C.scratch_dram(f"l2f_ao_{uu}", [128, NF * T], BF16)
    aos = P.salloc(f"l2f_aos_{uu}", [128, 512], BF16)
    afs = P.salloc(f"l2f_afs_{uu}", [128, NF * 512], BF16)
    pbo = [P.palloc() for _ in range(2)]
    for tc in range(T // 512):
        for fc in range(NF):
            P.dma("sp", afs[:, fc * 512:(fc + 1) * 512],
                  a_d[:, fc * T + tc * 512: fc * T + (tc + 1) * 512],
                  in_key=[f"l2fad_{uu}_{fc}_0_{tc}", f"l2fad_{uu}_{fc}_64_{tc}"])
        for oc in range(2, NF):
            bank = pbo[oc % 2]
            for fc in range(NF):
                P.add("pe", lambda e, f=fc, o=oc, t=tc, b=bank,
                      s=(fc == 0), p=(fc == NF - 1):
                      e.matmul(b[:], wslice(Wo, D, f, o * 128, 128),
                               afs[:, f * 512:(f + 1) * 512], start=s, stop=p),
                      ins=[Wo, afs], outs=[bank])
            P.add("dve", lambda e, b=bank: e.tensor_copy(out=aos[:], in_=b[:]),
                  ins=[bank], outs=[aos])
            P.dma("sp", ao_d[:, oc * T + tc * 512: oc * T + (tc + 1) * 512],
                  aos[:], out_key=f"l2fao_{uu}_{oc}_{tc}")
            P.add("dve", lambda e, o=oc, t=tc, b=bank: e.tensor_add(
                fmv(ufm, o, t * 512, 512), fmv(ufm, o, t * 512, 512), b[:]),
                ins=[ufm, bank], outs=[ufm])
    for b in pbo:
        P.pfree(b)
    P.sfree(aos)
    P.sfree(afs)
    P.sfree(Wo)
    n2c = load_bias(C, "L2_ffn_n2_w", D)
    Gf = P.salloc(f"l2f_gf_{uu}", [128, NF * 4], F32)
    Ghi = P.salloc(f"l2f_gh_{uu}", [128, NF * 4], BF16)
    Glo = P.salloc(f"l2f_gl_{uu}", [128, NF * 4], BF16)
    P.dma("sp", Gf[:].rearrange("p (c o) -> p c o", c=NF),
          C.win["L2_ffn_gate_w"].rearrange("(c p) o -> p c o", p=128))
    P.dma("pool", Ghi[:].rearrange("p (c o) -> p c o", c=NF),
          C.win["L2_ffn_gate_w"].rearrange("(c p) o -> p c o", p=128))
    P.add("dve", lambda e: e.tensor_sub(Glo[:], Gf[:], Ghi[:]),
          ins=[Gf, Ghi], outs=[Glo])
    xlo = P.salloc(f"l2f_xlo_{uu}", [128, 512], BF16)
    lta = P.salloc(f"l2f_lta_{uu}", [128, NT * 4], F32)
    pbg = [P.palloc() for _ in range(4)]

    def gate_cb(fc, tc, xf32):
        P.add("dve", lambda e: e.tensor_sub(xlo[:], xf32[:],
                                            fmv(ufm, fc, tc * 512, 512)),
              ins=[xf32, ufm], outs=[xlo])
        for i in range(4):
            bank = pbg[i]
            for j, (xa, xt, gw) in enumerate((
                    (fmv(ufm, fc, tc * 512 + i * 128, 128), ufm, Ghi),
                    (fmv(ufm, fc, tc * 512 + i * 128, 128), ufm, Glo),
                    (xlo[:, i * 128:(i + 1) * 128], xlo, Ghi))):
                P.add("pe", lambda e, xa=xa, gw=gw, f=fc, b=bank,
                      s=(fc == 0 and j == 0), p=(fc == NF - 1 and j == 2):
                      e.matmul(b[:, 0:4], xa, wslice(gw, 4, f, 0, 4),
                               start=s, stop=p),
                      ins=[xt, gw], outs=[bank])
            if fc == NF - 1:
                tt = tc * 4 + i
                P.add("dve", lambda e, b=bank, t=tt: e.tensor_copy(
                    out=lta[:, t * 4:(t + 1) * 4], in_=b[:, 0:4]),
                    ins=[bank], outs=[lta])

    emit_rms_fm(C, ufm, ufm, n2c, uu, 1.1920929e-07, "n2", lo_cb=gate_cb)
    xf = ufm
    P.sfree(n2c)
    for t_ in (Gf, Ghi, Glo, xlo):
        P.sfree(t_)
    wta = P.salloc(f"l2f_wa_{uu}", [128, NT * 4], F32)
    ltm = P.salloc(f"l2f_lt_{uu}", [128, 4], F32)
    m1 = P.salloc(f"l2f_m1_{uu}", [128, 1], F32)
    m2 = P.salloc(f"l2f_m2_{uu}", [128, 1], F32)
    den = P.salloc(f"l2f_dn_{uu}", [128, 1], F32)
    msk = P.salloc(f"l2f_mk_{uu}", [128, 4], F32)
    for tt in range(NT):
        P.add("dve", lambda e, t=tt: e.reduce_max(
            m1[:], lta[:, t * 4:(t + 1) * 4], axis=AX.X, negate=True),
            ins=[lta], outs=[m1])
        P.add("act", lambda e, t=tt: e.activation(
            out=ltm[:], in_=lta[:, t * 4:(t + 1) * 4], func=AF.Exp, bias=m1[:],
            accum_out=den[:]), ins=[lta, m1], outs=[ltm, den])
        P.add("dve", lambda e: e.reciprocal(den[:], den[:]), ins=[den], outs=[den])
        P.add("dve", lambda e: e.tensor_scalar_mul(ltm[:], ltm[:], den[:]),
              ins=[ltm, den], outs=[ltm])
        P.add("dve", lambda e: e.reduce_max(m1[:], ltm[:], axis=AX.X),
              ins=[ltm], outs=[m1])
        P.add("dve", lambda e: e.tensor_scalar(
            out=msk[:], in0=ltm[:], scalar1=m1[:], scalar2=-1e9,
            op0=ALU.is_ge, op1=ALU.mult), ins=[ltm, m1], outs=[msk])
        P.add("dve", lambda e: e.tensor_add(msk[:], msk[:], ltm[:]),
              ins=[msk, ltm], outs=[msk])
        P.add("dve", lambda e: e.reduce_max(m2[:], msk[:], axis=AX.X),
              ins=[msk], outs=[m2])
        P.add("dve", lambda e: e.tensor_add(den[:], m1[:], m2[:]),
              ins=[m1, m2], outs=[den])
        P.add("dve", lambda e: e.tensor_scalar_add(den[:], den[:], 1e-8),
              ins=[den], outs=[den])
        P.add("dve", lambda e: e.reciprocal(den[:], den[:]), ins=[den], outs=[den])
        P.add("dve", lambda e, t=tt: e.tensor_scalar(
            out=wta[:, t * 4:(t + 1) * 4], in0=ltm[:], scalar1=m2[:],
            scalar2=None, op0=ALU.is_ge), ins=[ltm, m2], outs=[wta])
        P.add("dve", lambda e, t=tt: e.tensor_mul(
            wta[:, t * 4:(t + 1) * 4], wta[:, t * 4:(t + 1) * 4], ltm[:]),
            ins=[wta, ltm], outs=[wta])
        P.add("dve", lambda e, t=tt: e.tensor_scalar_mul(
            wta[:, t * 4:(t + 1) * 4], wta[:, t * 4:(t + 1) * 4], den[:]),
            ins=[wta, den], outs=[wta])
    for b in pbg:
        P.pfree(b)
    for t_ in (ltm, m1, m2, den, msk, lta):
        P.sfree(t_)
    moe_d = C.scratch_dram(f"l2f_moe_{uu}", [D, T], F32)
    wexp = P.salloc(f"l2f_we_{uu}", [128, 128], F32)
    wbig = P.salloc(f"l2f_wb_{uu}", [128, 512], F32)
    stg = [P.salloc(f"l2f_st_{uu}_{i}", [128, 512], F32) for i in range(2)]
    pbw = P.palloc()

    def emit_wbig(e_, tc):
        for i in range(4):
            tt = tc * 4 + i
            eap = AP(wta, tt * 4 + e_, [[NT * 4, 128], [0, 128]])
            P.add("dve", lambda e, a=eap: e.tensor_copy(out=wexp[:], in_=a),
                  ins=[wta], outs=[wexp])
            P.add("pe", lambda e, i=i, b=pbw: e.transpose(
                b[:, i * 128:(i + 1) * 128], wexp[:], C.ident_f32[:]),
                ins=[wexp, C.ident_f32], outs=[pbw])
        P.add("dve", lambda e, b=pbw: e.tensor_copy(out=wbig[:], in_=b[:]),
              ins=[pbw], outs=[wbig])

    def expert_out_evac(bank, oc, tc, e_, first):
        st = stg[oc % 2]
        P.add("dve", lambda e, b=bank, st=st: e.tensor_mul(
            st[:], b[:], wbig[:]), ins=[bank, wbig], outs=[st])
        kw = {} if first else {"accum_op": ALU.add}
        P.dma("pool", moe_d[oc * 128:(oc + 1) * 128, tc * 512:(tc + 1) * 512],
              st[:], out_key=f"l2fmoe_{uu}_{oc}_{tc}", **kw)

    pbx = [P.palloc() for _ in range(2)]
    e0u = load_weight(C, "L2_ffn_e0_up", D, 2048)
    e0d = load_weight(C, "L2_ffn_e0_down", 2048, D)
    h1 = P.salloc(f"l2f_h1_{uu}", [128, 16 * 512], BF16)
    for tc in range(T // 512):
        emit_wbig(0, tc)
        for oc in range(16):
            bank = pbx[oc % 2]
            for fc in range(NF):
                P.add("pe", lambda e, f=fc, o=oc, t=tc, b=bank,
                      s=(fc == 0), p=(fc == NF - 1):
                      e.matmul(b[:], wslice(e0u, 2048, f, o * 128, 128),
                               fmv(xf, f, t * 512, 512), start=s, stop=p),
                      ins=[e0u, xf], outs=[bank])
            P.add("act", lambda e, o=oc, b=bank: e.activation(
                out=h1[:, o * 512:(o + 1) * 512], in_=b[:], func=AF.Gelu),
                ins=[bank], outs=[h1])
        for oc in range(2, NF):
            bank = pbx[oc % 2]
            for hc in range(16):
                P.add("pe", lambda e, h2=hc, o=oc, b=bank,
                      s=(hc == 0), p=(hc == 15):
                      e.matmul(b[:], wslice(e0d, D, h2, o * 128, 128),
                               h1[:, h2 * 512:(h2 + 1) * 512], start=s, stop=p),
                      ins=[e0d, h1], outs=[bank])
            expert_out_evac(bank, oc, tc, 0, True)
    P.sfree(e0u)
    P.sfree(e0d)
    e1a = load_weight(C, "L2_ffn_e1_w1", D, 2048)
    e1b = load_weight(C, "L2_ffn_e1_w2", D, 2048)
    for tc in range(T // 512):
        emit_wbig(1, tc)
        for oc in range(16):
            bank = pbx[0]
            bank2 = pbx[1]
            for fc in range(NF):
                P.add("pe", lambda e, f=fc, o=oc, t=tc, b=bank,
                      s=(fc == 0), p=(fc == NF - 1):
                      e.matmul(b[:], wslice(e1a, 2048, f, o * 128, 128),
                               fmv(xf, f, t * 512, 512), start=s, stop=p),
                      ins=[e1a, xf], outs=[bank])
            for fc in range(NF):
                P.add("pe", lambda e, f=fc, o=oc, t=tc, b=bank2,
                      s=(fc == 0), p=(fc == NF - 1):
                      e.matmul(b[:], wslice(e1b, 2048, f, o * 128, 128),
                               fmv(xf, f, t * 512, 512), start=s, stop=p),
                      ins=[e1b, xf], outs=[bank2])
            P.add("act", lambda e, o=oc, b=bank: e.activation(
                out=h1[:, o * 512:(o + 1) * 512], in_=b[:], func=AF.Silu),
                ins=[bank], outs=[h1])
            P.add("dve", lambda e, o=oc, b=bank2: e.tensor_mul(
                h1[:, o * 512:(o + 1) * 512], h1[:, o * 512:(o + 1) * 512],
                b[:]), ins=[h1, bank2], outs=[h1])
        if tc == 0:
            e1d = load_weight(C, "L2_ffn_e1_down", 2048, D)
        for oc in range(2, NF):
            bank = pbx[oc % 2]
            for hc in range(16):
                P.add("pe", lambda e, h2=hc, o=oc, b=bank,
                      s=(hc == 0), p=(hc == 15):
                      e.matmul(b[:], wslice(e1d, D, h2, o * 128, 128),
                               h1[:, h2 * 512:(h2 + 1) * 512], start=s, stop=p),
                      ins=[e1d, h1], outs=[bank])
            expert_out_evac(bank, oc, tc, 1, False)
    P.sfree(e1a)
    P.sfree(e1b)
    P.sfree(e1d)
    m_d = C.scratch_dram(f"l2f_m_{uu}", [2048, T], BF16)
    m2_d = C.scratch_dram(f"l2f_m2_{uu}", [2048, T], BF16)
    e2w = load_weight(C, "L2_ffn_e2_l1", D, 2048)
    for tc in range(T // 512):
        for oc in range(16):
            bank = pbx[oc % 2]
            for fc in range(NF):
                P.add("pe", lambda e, f=fc, o=oc, t=tc, b=bank, w2=e2w,
                      s=(fc == 0), p=(fc == NF - 1):
                      e.matmul(b[:], wslice(w2, 2048, f, o * 128, 128),
                               fmv(xf, f, t * 512, 512), start=s, stop=p),
                      ins=[e2w, xf], outs=[bank])
            P.add("act", lambda e, o=oc, b=bank: e.activation(
                out=h1[:, o * 512:(o + 1) * 512], in_=b[:], func=AF.Gelu),
                ins=[bank], outs=[h1])
            P.dma("sp", m_d[oc * 128:(oc + 1) * 128, tc * 512:(tc + 1) * 512],
                  h1[:, oc * 512:(oc + 1) * 512], out_key=f"l2fm_{uu}_{oc}_{tc}")
    P.sfree(e2w)
    for stage, (win_, wn, src_d, src_key, dst_d, dst_key) in enumerate((
            ("L2_ffn_e2_l2", 2048, m_d, "l2fm", m2_d, "l2fm2"),
            ("L2_ffn_e2_l3", 2048, m2_d, "l2fm2", m_d, "l2fm"))):
        e2w = load_weight(C, win_, 2048, 2048)
        h1r = [P.salloc(f"l2f_h1r_{uu}_{stage}_{i}", [128, 512], BF16)
               for i in range(2)]
        for tc in range(T // 512):
            for oc in range(16):
                bank = pbx[oc % 2]
                for hc in range(16):
                    st = h1r[hc % 2]
                    if oc == 0:
                        P.dma("sp", st[:],
                              src_d[hc * 128:(hc + 1) * 128,
                                    tc * 512:(tc + 1) * 512],
                              in_key=f"{src_key}_{uu}_{hc}_{tc}")
                        P.add("dve", lambda e, st=st, h2=hc: e.tensor_copy(
                            out=h1[:, h2 * 512:(h2 + 1) * 512], in_=st[:]),
                            ins=[st], outs=[h1])
                    P.add("pe", lambda e, h2=hc, o=oc, b=bank, w2=e2w,
                          s=(hc == 0), p=(hc == 15):
                          e.matmul(b[:], wslice(w2, 2048, h2, o * 128, 128),
                                   h1[:, h2 * 512:(h2 + 1) * 512],
                                   start=s, stop=p),
                          ins=[e2w, h1], outs=[bank])
                P.add("act", lambda e, o=oc, b=bank, st=h1r[oc % 2]: e.activation(
                    out=st[:], in_=b[:], func=AF.Gelu), ins=[bank],
                    outs=[h1r[oc % 2]])
                P.dma("sp", dst_d[oc * 128:(oc + 1) * 128,
                                  tc * 512:(tc + 1) * 512],
                      h1r[oc % 2][:], out_key=f"{dst_key}_{uu}_{oc}_{tc}")
        P.sfree(e2w)
        for t_ in h1r:
            P.sfree(t_)
    e2w = load_weight(C, "L2_ffn_e2_l4", 2048, D)
    h1r = [P.salloc(f"l2f_h1r4_{uu}_{i}", [128, 512], BF16) for i in range(2)]
    for tc in range(T // 512):
        emit_wbig(2, tc)
        for hc in range(16):
            st = h1r[hc % 2]
            P.dma("sp", st[:], m_d[hc * 128:(hc + 1) * 128,
                                   tc * 512:(tc + 1) * 512],
                  in_key=f"l2fm_{uu}_{hc}_{tc}")
            P.add("dve", lambda e, st=st, h2=hc: e.tensor_copy(
                out=h1[:, h2 * 512:(h2 + 1) * 512], in_=st[:]),
                ins=[st], outs=[h1])
        for oc in range(2, NF):
            bank = pbx[oc % 2]
            for hc in range(16):
                P.add("pe", lambda e, h2=hc, o=oc, b=bank, w2=e2w,
                      s=(hc == 0), p=(hc == 15):
                      e.matmul(b[:], wslice(w2, D, h2, o * 128, 128),
                               h1[:, h2 * 512:(h2 + 1) * 512], start=s, stop=p),
                      ins=[e2w, h1], outs=[bank])
            expert_out_evac(bank, oc, tc, 2, False)
    P.sfree(e2w)
    for t_ in h1r:
        P.sfree(t_)
    e3d = load_weight(C, "L2_ffn_e3_down", D, D)
    e3u = load_weight(C, "L2_ffn_e3_up", D, 2048)
    e3o = load_weight(C, "L2_ffn_e3_out", 2048, D)
    t1 = P.salloc(f"l2f_t1_{uu}", [128, NF * 512], BF16)
    for tc in range(T // 512):
        emit_wbig(3, tc)
        for oc in range(NF):
            bank = pbx[oc % 2]
            for fc in range(NF):
                P.add("pe", lambda e, f=fc, o=oc, t=tc, b=bank,
                      s=(fc == 0), p=(fc == NF - 1):
                      e.matmul(b[:], wslice(e3d, D, f, o * 128, 128),
                               fmv(xf, f, t * 512, 512), start=s, stop=p),
                      ins=[e3d, xf], outs=[bank])
            P.add("act", lambda e, o=oc, b=bank: e.activation(
                out=t1[:, o * 512:(o + 1) * 512], in_=b[:], func=AF.Gelu),
                ins=[bank], outs=[t1])
        for oc in range(16):
            bank = pbx[oc % 2]
            for fc in range(NF):
                P.add("pe", lambda e, f=fc, o=oc, b=bank,
                      s=(fc == 0), p=(fc == NF - 1):
                      e.matmul(b[:], wslice(e3u, 2048, f, o * 128, 128),
                               t1[:, f * 512:(f + 1) * 512], start=s, stop=p),
                      ins=[e3u, t1], outs=[bank])
            P.add("act", lambda e, o=oc, b=bank: e.activation(
                out=h1[:, o * 512:(o + 1) * 512], in_=b[:], func=AF.Gelu),
                ins=[bank], outs=[h1])
        for oc in range(2, NF):
            bank = pbx[oc % 2]
            for hc in range(16):
                P.add("pe", lambda e, h2=hc, o=oc, b=bank,
                      s=(hc == 0), p=(hc == 15):
                      e.matmul(b[:], wslice(e3o, D, h2, o * 128, 128),
                               h1[:, h2 * 512:(h2 + 1) * 512], start=s, stop=p),
                      ins=[e3o, h1], outs=[bank])
            expert_out_evac(bank, oc, tc, 3, False)
    for t_ in (e3d, e3u, e3o, t1, h1, wexp, wbig, wta, xf):
        P.sfree(t_)
    for t_ in stg:
        P.sfree(t_)
    for b in pbx + [pbw]:
        P.pfree(b)
    xr = P.salloc(f"l2f_xr_{uu}", [128, D], F32)
    ur = P.salloc(f"l2f_ur_{uu}", [128, D], F32)
    aor = P.salloc(f"l2f_aor_{uu}", [128, 6 * 128], BF16)
    mor = P.salloc(f"l2f_mor_{uu}", [128, 6 * 128], F32)
    ot = P.salloc(f"l2f_ot_{uu}", [128, D], F32)
    pbf = [P.palloc() for _ in range(2)]
    pbab = P.palloc()
    pbab_bf = pbab.ap().bitcast(BF16)
    for tt in range(NT):
        P.dma("sp", xr[:], xin[tt * 128:(tt + 1) * 128, :])
        P.dma("sp", ur[:], u_d[tt * 128:(tt + 1) * 128, :])
        for j, oc in enumerate(range(2, NF)):
            P.dma("sp", aor[:, j * 128:(j + 1) * 128],
                  ao_d[:, oc * T + tt * 128: oc * T + (tt + 1) * 128],
                  in_key=f"l2fao_{uu}_{oc}_{tt // 4}")
            P.dma("sp", mor[:, j * 128:(j + 1) * 128],
                  moe_d[oc * 128:(oc + 1) * 128, tt * 128:(tt + 1) * 128],
                  in_key=f"l2fmoe_{uu}_{oc}_{tt // 4}")
        for j in range(6):
            bank = pbf[j // 4]
            off = (j % 4) * 128
            P.add("pe", lambda e, j=j, b=bank, o=off: e.transpose(
                b[:, o:o + 128], mor[:, j * 128:(j + 1) * 128], C.ident_f32[:]),
                ins=[mor, C.ident_f32], outs=[bank])
            P.add("pe", lambda e, j=j: e.transpose(
                pbab_bf[:, j * 128:(j + 1) * 128], aor[:, j * 128:(j + 1) * 128],
                C.ident_bf[:]), ins=[aor, C.ident_bf], outs=[pbab])
        P.add("dve", lambda e: e.tensor_add(ot[:], xr[:], ur[:]),
              ins=[xr, ur], outs=[ot])
        P.add("dve", lambda e: e.tensor_add(ot[:, 256:768], ot[:, 256:768],
                                            pbf[0][:]),
              ins=[ot, pbf[0]], outs=[ot])
        P.add("dve", lambda e: e.tensor_add(ot[:, 768:1024], ot[:, 768:1024],
                                            pbf[1][:, 0:256]),
              ins=[ot, pbf[1]], outs=[ot])
        P.add("dve", lambda e: e.tensor_add(ot[:, 256:1024], ot[:, 256:1024],
                                            pbab_bf[:, 0:768]),
              ins=[ot, pbab], outs=[ot])
        P.dma("sp", xout[tt * 128:(tt + 1) * 128, :], ot[:], tag=tag)
    for b in pbf + [pbab]:
        P.pfree(b)
    for t_ in (xr, ur, aor, mor, ot):
        P.sfree(t_)


def emit_sqrt11(C, dst_ap, dst_t, src_ap, src_t):
    P = C.P
    P.add("act", lambda e: e.activation(out=dst_ap, in_=src_ap, func=AF.Ln),
          ins=[src_t], outs=[dst_t])
    P.add("act", lambda e: e.activation(out=dst_ap, in_=dst_ap, func=AF.Exp,
                                        scale=constf(C, 0.5)[0:1, :]),
          ins=[dst_t, constf(C, 0.5)], outs=[dst_t])


def emit_l3_attn(C, xin, xout, tag=""):
    P = C.P
    uu = P.uid()
    NKT = SEQ // 128
    JSPAN = 2304
    onesb = emit_ones_col_bf(C)
    u = emit_outer_ln(C, xin, want_fm=True, tag=tag)
    ufm = u["u_fm"]
    Wkqv = load_weight(C, "L3_attn_wkqv", D, 3072)
    bq = load_bias(C, "L3_attn_bq", D)
    bk = load_bias(C, "L3_attn_bk", D)
    P.add("dve", lambda e: e.tensor_scalar_mul(bq[:], bq[:], 0.125),
          ins=[bq], outs=[bq])
    bvbig32 = emit_row_broadcast(C, "L3_attn_bv", D, uu)
    bvbig = P.salloc(f"l3a_bvb_{uu}", [128, D], BF16)
    P.add("dve", lambda e: e.tensor_copy(out=bvbig[:], in_=bvbig32[:]),
          ins=[bvbig32], outs=[bvbig])
    P.sfree(bvbig32)
    qd, kd, vd = emit_qkv_spill(C, ufm, Wkqv, uu, bq=bq, bk=bk, bvbig=bvbig)
    P.sfree(ufm)
    P.sfree(Wkqv)
    for t_ in (bq, bk, bvbig):
        P.sfree(t_)
    peRv = P.salloc(f"l3a_per_{uu}", [64, 4224], BF16)
    peK = P.salloc(f"l3a_pek_{uu}", [64, 4224], BF16)
    P.dma("pool", peRv[:], C.win["L3_attn_peT_rev"])
    P.dma("pool", peK[:], C.win["L3_attn_peT"])
    P.add("dve", lambda e: e.tensor_scalar_mul(peK[:], peK[:], 0.125),
          ins=[peK], outs=[peK])
    sqpe = P.salloc(f"l3a_sqp_{uu}", [64, 512], BF16)
    pn = P.salloc(f"l3a_pn_{uu}", [1, 4096], F32)
    mpe2 = P.salloc(f"l3a_mpe_{uu}", [1, 1], F32)
    bank = P.palloc()
    for c0_ in range(0, 4096, 512):
        P.add("act", lambda e, c=c0_: e.activation(
            out=sqpe[:], in_=peRv[:, c:c + 512], func=AF.Square),
            ins=[peRv], outs=[sqpe])
        P.add("pe", lambda e, b=bank: e.matmul(b[0:1, :], onesb[0:64, :],
                                               sqpe[:], start=True, stop=True),
              ins=[onesb, sqpe], outs=[bank])
        P.add("dve", lambda e, c=c0_, b=bank: e.tensor_copy(
            out=pn[0:1, c:c + 512], in_=b[0:1, :]), ins=[bank], outs=[pn])
    P.add("dve", lambda e: e.reduce_max(mpe2[:], pn[0:1, 0:4095], axis=AX.X),
          ins=[pn], outs=[mpe2])
    P.pfree(bank)
    P.sfree(sqpe)
    P.sfree(pn)
    R_d = [C.scratch_dram(f"l3a_rq_{uu}_{i}", [SEQ, JSPAN], BF16)
           for i in range(2)]
    KR_d = [C.scratch_dram(f"l3a_kr_{uu}_{i}", [SEQ, JSPAN], BF16)
            for i in range(2)]
    BB = []
    for par in range(2):
        BB.append(dict(
            qsb=P.salloc(f"l3a_q_{uu}_{par}", [64, SEQ], BF16),
            ksb=P.salloc(f"l3a_k_{uu}_{par}", [64, SEQ], BF16),
            sq=P.salloc(f"l3a_sq_{uu}_{par}", [64, 512], BF16),
            nrm=P.salloc(f"l3a_n_{uu}_{par}", [1, 16], F32),
            sA=P.salloc(f"l3a_sa_{uu}_{par}", [1, 1], F32),
            sB=P.salloc(f"l3a_sb_{uu}_{par}", [1, 1], F32),
            sC=P.salloc(f"l3a_sc_{uu}_{par}", [1, 4], F32),
            c0n=P.salloc(f"l3a_c0_{uu}_{par}", [1, 1], F32),
            c0big=P.salloc(f"l3a_c0b_{uu}_{par}", [128, 1], F32),
            ptil=P.salloc(f"l3a_p_{uu}_{par}", [128, 2 * 512], BF16),
            rsum=P.salloc(f"l3a_rs_{uu}_{par}", [1, 512], F32),
            rbig=P.salloc(f"l3a_rb_{uu}_{par}", [64, 512], F32),

            rk_sb=[P.salloc(f"l3a_rks_{uu}_{par}_{i}", [128, 512], BF16)
                   for i in range(4)],
            pbS=[P.palloc() for _ in range(2)],
            pbA=P.palloc(),
            pbrb=P.palloc(),
        ))
    vtmb = P.salloc(f"l3a_v_{uu}", [128, NKT * 16 * 65], BF16)
    afm = P.salloc(f"l3a_a_{uu}", [128, NF * T], BF16)
    rstg_sh = [P.salloc(f"l3a_rst_{uu}_{i}", [128, JSPAN], BF16)
               for i in range(2)]
    rq_sh = P.salloc(f"l3a_rqs_{uu}", [128, 4 * SEQ], BF16)

    def head_body(b_, h):
        B = BB[h % 2]
        par = h % 2
        qsb, ksb, sq, nrm = B["qsb"], B["ksb"], B["sq"], B["nrm"]
        sA, sB, sC, c0n, c0big = B["sA"], B["sB"], B["sC"], B["c0n"], B["c0big"]
        ptil, rsum, rbig = B["ptil"], B["rsum"], B["rbig"]
        rq_sb, rk_sb = rq_sh, B["rk_sb"]
        pbS, pbA, pbrb = B["pbS"], B["pbA"], B["pbrb"]
        bank_n = pbA
        P.dma("sp", qsb[:], qd[h * 64:(h + 1) * 64, b_ * SEQ:(b_ + 1) * SEQ])
        P.dma("sp", ksb[:], kd[h * 64:(h + 1) * 64, b_ * SEQ:(b_ + 1) * SEQ])
        for qt in range(NKT):
            rstg = rstg_sh[qt % 2]
            for jc in range(5):
                w = 512 if jc < 4 else JSPAN - 4 * 512
                bk2 = pbS[jc % 2]
                P.add("pe", lambda e, q=qt, j=jc, w=w, b=bk2, qsb=qsb: e.matmul(
                    b[:, 0:w], qsb[:, q * 128:(q + 1) * 128],
                    peRv[:, 1920 - q * 128 + j * 512:
                         1920 - q * 128 + j * 512 + w],
                    start=True, stop=True), ins=[qsb, peRv], outs=[bk2])
                P.add("dve", lambda e, j=jc, w=w, b=bk2, rstg=rstg: e.tensor_copy(
                    out=rstg[:, j * 512:j * 512 + w], in_=b[:, 0:w]),
                    ins=[bk2], outs=[rstg])
            P.dma("pool", R_d[par][qt * 128:(qt + 1) * 128, :], rstg[:],
                  out_key=f"l3aR_{par}_{qt}")
        for kt in range(NKT):
            rstg = rstg_sh[kt % 2]
            for jc in range(5):
                w = 512 if jc < 4 else JSPAN - 4 * 512
                bk2 = pbS[jc % 2]
                P.add("pe", lambda e, k=kt, j=jc, w=w, b=bk2, ksb=ksb: e.matmul(
                    b[:, 0:w], ksb[:, k * 128:(k + 1) * 128],
                    peK[:, 1920 - k * 128 + j * 512:
                        1920 - k * 128 + j * 512 + w],
                    start=True, stop=True), ins=[ksb, peK], outs=[bk2])
                P.add("dve", lambda e, j=jc, w=w, b=bk2, rstg=rstg: e.tensor_copy(
                    out=rstg[:, j * 512:j * 512 + w], in_=b[:, 0:w]),
                    ins=[bk2], outs=[rstg])
            P.dma("pool", KR_d[par][kt * 128:(kt + 1) * 128, :], rstg[:],
                  out_key=f"l3aK_{par}_{kt}")
        for side, src_ in ((0, qsb), (1, ksb)):
            for tc in range(SEQ // 512):
                P.add("act", lambda e, s=src_, t=tc, sq=sq: e.activation(
                    out=sq[:], in_=s[:, t * 512:(t + 1) * 512], func=AF.Square),
                    ins=[src_], outs=[sq])
                P.add("pe", lambda e, bn=bank_n, sq=sq: e.matmul(
                    bn[0:1, :], onesb[0:64, :], sq[:], start=True, stop=True),
                    ins=[onesb, sq], outs=[bank_n])
                P.add("dve", lambda e, sd=side, t=tc, bn=bank_n, nrm=nrm:
                      e.reduce_max(nrm[0:1, sd * 8 + t: sd * 8 + t + 1],
                                   bn[0:1, :], axis=AX.X),
                      ins=[bank_n], outs=[nrm])
        P.add("dve", lambda e, nrm=nrm, sA=sA: e.reduce_max(
            sA[:], nrm[0:1, 0:SEQ // 512], axis=AX.X), ins=[nrm], outs=[sA])
        P.add("dve", lambda e, nrm=nrm, sB=sB: e.reduce_max(
            sB[:], nrm[0:1, 8:8 + SEQ // 512], axis=AX.X), ins=[nrm], outs=[sB])
        emit_sqrt11(C, sA[:], sA, sA[:], sA)
        emit_sqrt11(C, sB[:], sB, sB[:], sB)
        emit_sqrt11(C, c0n[:], c0n, mpe2[:], mpe2)
        P.add("dve", lambda e, sA=sA, sB=sB, sC=sC: e.tensor_mul(
            sC[0:1, 0:1], sA[:], sB[:]), ins=[sA, sB], outs=[sC])
        P.add("dve", lambda e, sA=sA, c0n=c0n, sC=sC: e.tensor_mul(
            sC[0:1, 1:2], sA[:], c0n[:]), ins=[sA, c0n], outs=[sC])
        P.add("dve", lambda e, sB=sB, c0n=c0n, sC=sC: e.tensor_mul(
            sC[0:1, 2:3], sB[:], c0n[:]), ins=[sB, c0n], outs=[sC])
        P.add("dve", lambda e, sC=sC: e.tensor_scalar_mul(
            sC[0:1, 2:3], sC[0:1, 2:3], 0.125), ins=[sC], outs=[sC])
        P.add("dve", lambda e, sC=sC, c0n=c0n: e.reduce_sum(
            c0n[:], sC[0:1, 0:3], axis=AX.X), ins=[sC], outs=[c0n])
        P.add("dve", lambda e, c0n=c0n: e.tensor_scalar_mul(c0n[:], c0n[:], -1.0),
              ins=[c0n], outs=[c0n])
        P.add("pe", lambda e, bn=bank_n, c0n=c0n: e.matmul(
            bn[:, 0:1], C.ones_row[:], c0n[:], start=True, stop=True),
            ins=[C.ones_row, c0n], outs=[bank_n])
        P.add("dve", lambda e, bn=bank_n, c0big=c0big: e.tensor_copy(
            out=c0big[:], in_=bn[:, 0:1]), ins=[bank_n], outs=[c0big])
        for qch in range(SEQ // 512):
            q0g = qch * 512
            for i in range(4):
                qt = qch * 4 + i
                gap = AP(R_d[par].tensor, (qt * 128) * JSPAN + 127,
                         [[JSPAN - 1, 128], [1, SEQ]])
                P.dma("sp", rq_sb[:, i * SEQ:(i + 1) * SEQ], gap,
                      in_key=f"l3aR_{par}_{qt}")
            for kt in range(NKT):
                bank = pbS[kt % 2]
                for i in range(4):
                    P.add("pe", lambda e, k=kt, i=i, b=bank, rq_sb=rq_sb:
                          e.matmul(
                              b[:, i * 128:(i + 1) * 128],
                              rq_sb[:, i * SEQ + k * 128: i * SEQ + (k + 1) * 128],
                              C.ident_bf[:], start=(i == 0), stop=False),
                          ins=[rq_sb, C.ident_bf], outs=[bank])
                rk = rk_sb[kt % 4]
                gkp = AP(KR_d[par].tensor, (kt * 128) * JSPAN + 127 + q0g,
                         [[JSPAN - 1, 128], [1, 512]])
                P.dma("sp", rk[:], gkp, in_key=f"l3aK_{par}_{kt}")
                P.add("pe", lambda e, b=bank, rk=rk: e.matmul(
                    b[:], C.ident_bf[:], rk[:], start=False, stop=False),
                    ins=[C.ident_bf, rk], outs=[bank])
                P.add("pe", lambda e, k=kt, q=qch, b=bank, qsb=qsb, ksb=ksb:
                      e.matmul(b[:], ksb[:, k * 128:(k + 1) * 128],
                               qsb[:, q * 512:(q + 1) * 512],
                               start=False, stop=True),
                      ins=[ksb, qsb], outs=[bank])
                P.add("act", lambda e, b=bank, k=kt, ptil=ptil, c0big=c0big:
                      e.activation(
                          out=ptil[:, (k % 2) * 512:(k % 2 + 1) * 512],
                          in_=b[:], func=AF.Exp, bias=c0big[:]),
                      ins=[bank, c0big], outs=[ptil])
                if kt > 0:
                    P.add("pe", lambda e, k=kt - 1, h=h, b=pbA, ptil=ptil,
                          s=(kt - 1 == 0):
                          e.matmul(b[0:65, :],
                                   vtmb[:, k * 1040 + h * 65:
                                        k * 1040 + (h + 1) * 65],
                                   ptil[:, (k % 2) * 512:(k % 2 + 1) * 512],
                                   start=s, stop=False),
                          ins=[vtmb, ptil], outs=[pbA])
            P.add("pe", lambda e, k=NKT - 1, h=h, b=pbA, ptil=ptil:
                  e.matmul(b[0:65, :],
                           vtmb[:, k * 1040 + h * 65: k * 1040 + (h + 1) * 65],
                           ptil[:, (k % 2) * 512:(k % 2 + 1) * 512],
                           start=False, stop=True),
                  ins=[vtmb, ptil], outs=[pbA])
            P.add("dve", lambda e, b=pbA, rsum=rsum: e.tensor_copy(
                out=rsum[:], in_=b[64:65, :]), ins=[pbA], outs=[rsum])
            P.add("dve", lambda e, rsum=rsum: e.reciprocal(rsum[:], rsum[:]),
                  ins=[rsum], outs=[rsum])
            P.add("pe", lambda e, pb_=pbrb, rsum=rsum: e.matmul(
                pb_[0:64, :], C.ones_row[0:1, 0:64], rsum[:],
                start=True, stop=True), ins=[C.ones_row, rsum], outs=[pbrb])
            P.add("dve", lambda e, pb_=pbrb, rbig=rbig: e.tensor_copy(
                out=rbig[:], in_=pb_[0:64, :]), ins=[pbrb], outs=[rbig])
            oc = h // 2
            pb0 = (h % 2) * 64
            tg = b_ * 4 + qch
            P.add("dve", lambda e, b=pbA, o=oc, p0=pb0, q=qch, bb=b_, rbig=rbig:
                  e.tensor_mul(
                      afm[p0:p0 + 64,
                          o * T + bb * SEQ + q * 512: o * T + bb * SEQ + (q + 1) * 512],
                      b[0:64, :], rbig[:]),
                  ins=[pbA, rbig], outs=[f"l3afm{uu}_{h}_{tg}"])

    for b_ in range(BPC):
        P.dma("sp", vtmb[:], vd[:, b_ * NKT * 1040:(b_ + 1) * NKT * 1040])
        for h in range(16):
            head_body(b_, h)
    for B in BB:
        for k_ in ("qsb", "ksb", "sq", "nrm", "sA", "sB", "sC", "c0n", "c0big",
                   "ptil", "rsum", "rbig"):
            P.sfree(B[k_])
        for t_ in B["rk_sb"]:
            P.sfree(t_)
        for bb2 in B["pbS"] + [B["pbA"], B["pbrb"]]:
            P.pfree(bb2)
    for t_ in (peRv, peK, mpe2, vtmb, rstg_sh[0], rstg_sh[1], rq_sh):
        P.sfree(t_)
    akeys = [f"l3afm{uu}_{h}_{t}" for h in range(16) for t in range(8)]
    P.add("dve", lambda e: e.engine_nop(), ins=akeys, outs=[afm])
    emit_delta_residual_store(C, xin, xout, afm, tt_range=range(NT),
                              tag=tag, tstride=T, t_base=0)
    P.sfree(afm)


BLOCK_CHAIN = None


def build_program(w, n_blocks=8):
    nc = bass.Bass("TRN2", target_bir_lowering=False)
    x_in = nc.dram_tensor("x_in", [T, D], F32, kind="ExternalInput").ap()
    y = nc.dram_tensor("y", [T, D], F32, kind="ExternalOutput").ap()
    P = Prog(nc)
    P.mem_init()
    win = LazyWeights(nc, w)
    C = Ctx(nc, P, win)
    emit_dev_consts(C)
    xa = C.scratch_dram("x_chain_a", [T, D], F32)
    xb = C.scratch_dram("x_chain_b", [T, D], F32)
    chain = [emit_l0_attn, emit_l0_ffn, emit_l1_attn, emit_l1_ffn,
             emit_l2_attn, emit_l2_ffn, emit_l3_attn, emit_l3_ffn][:n_blocks]
    bufs = [x_in, xa, xb, xa, xb, xa, xb, xa, y]
    bufs[n_blocks] = y
    for i, f in enumerate(chain):
        f(C, bufs[i], bufs[i + 1], tag=("out" if i == len(chain) - 1 else ""))
    P.finalize()
    P.emit()
    return nc, win, P


def kernel(**inputs):
    w = prep_weights(inputs)
    x = np.ascontiguousarray(np.asarray(inputs["L0_attn_x"], dtype=np.float32))
    shards = [np.ascontiguousarray(x[2 * i:2 * i + 2].reshape(T, D))
              for i in range(NCORES)]
    nc, win, P = build_program(w)
    in_maps = [dict({"x_in": s}, **{k: w[k] for k in win.used()})
               for s in shards]
    res = run_bass_kernel_spmd(nc, in_maps, core_ids=list(range(NCORES)))
    out = np.concatenate(
        [np.asarray(res.results[i]["y"]).reshape(2, SEQ, D)
         for i in range(NCORES)], axis=0)
    return out
